# version 19
# baseline (speedup 1.0000x reference)
import os
import numpy as np
import ml_dtypes
from contextlib import ExitStack

import concourse.bass as bass
import concourse.mybir as mybir
from concourse.alu_op_type import AluOpType as ALU
from concourse.bass_utils import run_bass_kernel_spmd

F32 = mybir.dt.float32
BF16 = mybir.dt.bfloat16
I16 = mybir.dt.int16
AFT = mybir.ActivationFunctionType
AX = mybir.AxisListType

H, DP, DH, DT = 16, 16, 64, 273
DM, HID = 1024, 4096
NM = HID // 128
SQH = 2.0 ** (-0.25)
SQHI = 2.0 ** (0.25)
EPS = 1e-5
NCOL = 52400

ENGS = ("pe", "dve", "act", "pool", "sp")
QUEUES = ("qsp", "qact", "qpool")


def fdims(ap, dims, extra_off=0):
    return bass.AP(ap.tensor, ap.offset + extra_off,
                   [list(ap.ap[0])] + [list(d) for d in dims])


class Prog:
    RSLOT = 8

    def __init__(self, nc, block, strict=False, strict_dma=False, strict_eng=""):
        self.strict = strict
        self.strict_dma = strict_dma
        self.strict_eng = strict_eng
        self.sems = {}
        self.qsems = {}
        self._ctx = ExitStack()
        for name in ENGS:
            self.sems[name] = self._ctx.enter_context(nc.semaphore("sem_" + name))
        for q in QUEUES:
            self.qsems[q] = [self._ctx.enter_context(nc.semaphore(f"sem_{q}_{i}"))
                             for i in range(self.RSLOT)]
        self.cnt = {name: 0 for name in ENGS}
        self.qcnt = {q: 0 for q in QUEUES}
        self.last_w = {}
        self.readers = {}
        self.hw = {}
        self.secs = {"pe": block.tensor, "dve": block.vector, "act": block.scalar,
                     "pool": block.gpsimd, "sp": block.sync}
        self.nops = 0
        self.maxops = int(os.environ.get("KMAXOPS", "0"))
        self.act_cur = None
        from concourse.hw_specs import get_activation_tables
        self._act_tables = list(get_activation_tables(nc.m.arch).values())

    def act_set(self, func):
        for idx, funcs in enumerate(self._act_tables):
            if func in funcs:
                break
        else:
            raise ValueError(func)
        if self.act_cur is not None and func in self._act_tables[self.act_cur]:
            return
        self.act_cur = idx
        self.op("act", lambda e: e.preload_act_table(func))

    def _qwait(self, q, idx):
        slot = (idx - 1) % self.RSLOT
        uses = (idx - 1) // self.RSLOT + 1
        return self.qsems[q][slot], 16 * uses, (q, slot)

    def _edge_wait(self, eng, dom, val, waits):
        if dom in QUEUES:
            sem, tgt, hwkey = self._qwait(dom, val)
        else:
            sem, tgt, hwkey = self.sems[dom], val, dom
        k = (eng, hwkey)
        if self.hw.get(k, 0) < tgt:
            self.hw[k] = tgt
            waits.append((sem, tgt))

    def op(self, eng, fn, reads=(), writes=(), dom=None):
        if self.maxops and self.nops >= self.maxops:
            return 0
        dom = dom or eng
        is_dma = dom in QUEUES

        def keep(d):
            if d != dom:
                return True
            if self.strict:
                return True
            if d in QUEUES:
                return self.strict_dma
            return d in self.strict_eng

        edges = []
        for k in reads:
            lw = self.last_w.get(k)
            if lw is not None and keep(lw[0]):
                edges.append(lw)
        for k in writes:
            lw = self.last_w.get(k)
            if lw is not None and keep(lw[0]):
                edges.append(lw)
            for rdom, rval in self.readers.get(k, {}).items():
                if keep(rdom):
                    edges.append((rdom, rval))
        waits = []
        for d, v in edges:
            self._edge_wait(eng, d, v, waits)
        if is_dma:
            self.qcnt[dom] += 1
            val = self.qcnt[dom]
            sem, tgt, hwkey = self._qwait(dom, val)
            k = (eng, hwkey)
            if tgt - 16 > 0 and self.hw.get(k, 0) < tgt - 16:
                self.hw[k] = tgt - 16
                waits.append((sem, tgt - 16))
            inc_sem, inc_amt = sem, 16
        else:
            self.cnt[dom] += 1
            val = self.cnt[dom]
            inc_sem, inc_amt = self.sems[dom], 1
            if self.strict and val > 1:
                waits.append((inc_sem, val - 1))

        def body(e):
            for sm, v in waits:
                e.wait_ge(sm, v)
            fn(e).then_inc(inc_sem, inc_amt)

        self.secs[eng](body)
        self.nops += 1
        for k in reads:
            self.readers.setdefault(k, {})[dom] = val
        for k in writes:
            self.last_w[k] = (dom, val)
            self.readers[k] = {}
        return val

    def dma(self, eng, queue, out, in_, reads=(), writes=()):
        return self.op(eng, lambda e: e.dma_start(out=out, in_=in_),
                       reads=reads, writes=writes, dom=queue)

    def _all_final(self):
        fin = []
        for name in ENGS:
            if self.cnt[name] > 0:
                fin.append((name, self.sems[name], self.cnt[name], name))
        for q in QUEUES:
            n = self.qcnt[q]
            for slot in range(self.RSLOT):
                uses = (n - slot + self.RSLOT - 1) // self.RSLOT if n > slot else 0
                uses = len(range(slot, n, self.RSLOT))
                if uses > 0:
                    fin.append(((q, slot), self.qsems[q][slot], 16 * uses, (q, slot)))
        return fin

    def barrier(self):
        fin = self._all_final()
        for eng in ENGS:
            waits = []
            for hwkey, sem, v, _ in fin:
                if hwkey == eng:
                    continue
                if self.hw.get((eng, hwkey), 0) < v:
                    self.hw[(eng, hwkey)] = v
                    waits.append((sem, v))
            def body(e, waits=waits):
                for sm, v in waits:
                    e.wait_ge(sm, v)

            self.secs[eng](body)

    def finalize(self):
        fin = self._all_final()

        def body(e):
            for hwkey, sem, v, _ in fin:
                if hwkey == "sp":
                    continue
                e.wait_ge(sem, v)

        self.secs["sp"](body)


class Arena:
    def __init__(self, nc):
        self.h = nc.alloc_sbuf_tensor("arena", [128, NCOL], F32)
        self.A32 = self.h[:, :]
        self.A16 = self.h[:, :].bitcast(BF16)
        self.AI16 = self.h[:, :].bitcast(I16)
        self.cur = 0
        self.limit = NCOL

    def region(self, base, limit):
        self.cur = base
        self.limit = limit

    def f32(self, n):
        ap = self.A32[:, self.cur:self.cur + n]
        self.cur += n
        assert self.cur <= self.limit, f"arena overflow {self.cur} > {self.limit}"
        return ap

    def bf16(self, n16):
        assert n16 % 2 == 0
        ap = self.A16[:, 2 * self.cur: 2 * self.cur + n16]
        self.cur += n16 // 2
        assert self.cur <= self.limit, f"arena overflow {self.cur} > {self.limit}"
        return ap

    def i16(self, n16):
        assert n16 % 2 == 0
        ap = self.AI16[:, 2 * self.cur: 2 * self.cur + n16]
        self.cur += n16 // 2
        assert self.cur <= self.limit, f"arena overflow {self.cur} > {self.limit}"
        return ap


def build_program(nt=16, dbg=False, strict=False, detect=False, strict_dma=False, strict_eng=""):
    SEQ = nt * 128
    NW = min(512, SEQ)
    NG = SEQ // NW
    nc = bass.Bass("TRN2", target_bir_lowering=False, debug=False,
                   detect_race_conditions=detect)

    dram = {}
    for name, shape, dt in [
        ("x_in", [SEQ, DM], F32), ("wqtp", [DM, 512], F32), ("wktp", [DM, 512], F32),
        ("wvt", [DM, DM], F32), ("bqp", [128, 4], F32), ("bkp", [128, 4], F32),
        ("bv_rep", [128, DM], F32), ("w1t", [DM, HID], F32), ("w2t", [HID, DM], F32),
        ("cw_h", [128, 96], F32), ("cb_h", [128, 32], F32), ("ident", [128, 128], BF16),
        ("m01ea", [128, 384], BF16), ("m01ah", [128, 128], BF16),
        ("onescf", [1, 128], F32), ("one11f", [1, 1], F32), ("onecol", [128, 1], F32),
        ("identf", [128, 128], F32),
    ]:
        dram[name] = nc.dram_tensor(name, shape, dt, kind="ExternalInput").ap()
    dram["out"] = nc.dram_tensor("out", [SEQ, DM], F32, kind="ExternalOutput").ap()
    xmid_d = nc.dram_tensor("xmid_sc", [SEQ, DM], F32, kind="Internal").ap()
    if dbg:
        for name, shape, dt in [("dbg_attn", [SEQ, DM], F32), ("dbg_xmid", [SEQ, DM], F32),
                                ("dbg_xqf", [128, 8 * SEQ], BF16),
                                ("dbg_h2", [128, HID], F32), ("dbg_hq", [128, HID], BF16)]:
            dram[name] = nc.dram_tensor(name, shape, dt, kind="ExternalOutput").ap()

    ar = Arena(nc)
    ar.region(0, 1984)
    ident = ar.bf16(128)
    m01ea = ar.bf16(384)
    m01ah = ar.bf16(128)
    onescf = ar.f32(128)[0:1, :]
    one11f = ar.f32(2)[0:1, 0:1]
    onecol = ar.f32(1)
    bqp = ar.f32(4)
    bkp = ar.f32(4)
    cw_h = ar.f32(96)
    cb_h = ar.f32(32)
    smal = ar.f32(100)
    s1x1 = ar.f32(64)[0:1, :]
    wsum1 = ar.f32(32)
    wsum2 = ar.f32(32)
    stat6 = ar.f32(12)
    identf = ar.f32(128)
    ceps = ar.f32(1)
    czero = ar.f32(1)
    c1e8 = ar.f32(1)
    csqh = ar.f32(1)
    bv_rep = ar.f32(1024)
    sbcp = bv_rep[:, 0:128]
    pmax = bv_rep[:, 128:256]

    C_MEAN, C_VAR, C_R, C_NMR = 0, 1, 2, 3
    C_MA1, C_SC1, C_MA2, C_SC2 = 4, 5, 6, 7
    C_MAX, C_ISX, C_F, C_S4, C_B4, C_VT2, C_SX = 8, 9, 10, 11, 12, 13, 14
    C_T = 16

    def sc(c):
        return smal[:, c:c + 1]

    B1, L1 = 1984, 11050
    B2, L2 = 11050, 27434
    B3, L3 = 27434, 35754
    B4, L4 = 35754, 52138

    ar.region(B2, L2)
    qfm4 = ar.bf16(4 * SEQ)
    kfm4 = ar.bf16(4 * SEQ)
    qtm4s = ar.bf16(4 * SEQ)
    ktm4s = ar.bf16(4 * SEQ)
    ar.region(B2, L2)
    w1q = ar.bf16(8 * HID)

    ar.region(B3, L3)
    vt = ar.bf16(nt * 1040)
    ar.region(B3, L3)
    wbuf = [ar.f32(1024), ar.f32(1024)]
    sgb = [ar.bf16(1024), ar.bf16(1024)]
    dbb = [ar.bf16(1024), ar.bf16(1024)]
    ar.region(B3, L3)
    h2 = ar.f32(4096)
    xqs = [ar.bf16(8 * 132) for _ in range(4)]

    ar.region(B4, L4)
    attn = ar.f32(nt * DM)
    ar.region(B4, L4)
    w2q = ar.bf16(NM * DM)
    ar.region(B4, L4)
    wqk_sb = ar.bf16(8 * 512)
    wvt_sb = ar.bf16(8 * DM)
    wstg = [ar.f32(1024), ar.f32(1024)]
    x_t = [ar.f32(1024), ar.f32(1024)]
    xln_tm = [ar.bf16(1024), ar.bf16(1024)]

    ar.region(B1, L1)
    xlnf = ar.bf16(8 * SEQ)
    ar.region(B1, L1)
    phk = [ar.bf16(nt * 280 + 104), ar.bf16(nt * 280 + 104)]
    phqr = [ar.bf16(2 * 384), ar.bf16(2 * 384)]
    phq_tm = [ar.bf16(280), ar.bf16(280)]
    e_sb = [ar.bf16(384) for _ in range(4)]
    s_sb = [ar.f32(195), ar.f32(195)]
    s16 = [ar.bf16(196), ar.bf16(196)]
    tloc = [bv_rep[:, 256:320], bv_rep[:, 320:384]]
    w1stg = [ar.f32(1024), ar.f32(1024)]
    ar.region(B1, L1)
    x_t4 = [ar.f32(1024), ar.f32(1024)]
    ln3b = ar.f32(1024)
    ar.region(B1, L1)
    xm_in = [ar.f32(1024) for _ in range(4)]
    xl2b = ar.f32(1024)
    xqtm = [ar.bf16(1024), ar.bf16(1024)]
    hsb = [ar.f32(130), ar.f32(130)]
    accb = [ar.f32(128), ar.f32(128)]
    hqg = [ar.bf16(1024), ar.bf16(1024)]
    sbc = ar.f32(128)
    srow_full = ar.f32(128)
    outb = ar.f32(1024)

    hqg_i16 = [bass.AP(ar.AI16.tensor, q.offset, [list(q.ap[0])] + [list(d) for d in q.ap[1:]])
               for q in hqg]
    xqi = [bass.AP(ar.AI16.tensor, q.offset, [list(q.ap[0])] + [list(d) for d in q.ap[1:]])
           for q in xqtm]

    es = ExitStack()
    psT = es.enter_context(nc.psum_tensor("psT", [128, 1024], BF16))
    psA = es.enter_context(nc.psum_tensor("psA", [128, 512], F32))
    psB = es.enter_context(nc.psum_tensor("psB", [128, 512], F32))
    psC = es.enter_context(nc.psum_tensor("psC", [128, 1024], F32))
    psD = es.enter_context(nc.psum_tensor("psD", [128, 512], F32))
    psE = es.enter_context(nc.psum_tensor("psE", [128, 512], F32))
    psF = es.enter_context(nc.psum_tensor("psF", [128, 512], F32))
    block = es.enter_context(nc.Block())
    p = Prog(nc, block, strict=strict, strict_dma=strict_dma, strict_eng=strict_eng)

    p.op("dve", lambda e: e.memset(ceps, EPS), writes=("ceps",))
    p.op("dve", lambda e: e.memset(czero, 0.0), writes=("czero",))
    p.op("dve", lambda e: e.memset(c1e8, 1e-8), writes=("c1e8",))
    p.op("dve", lambda e: e.memset(csqh, 2.0 ** -0.5), writes=("csqh",))
    for name, dst in [("ident", ident), ("m01ea", m01ea), ("m01ah", m01ah),
                      ("onescf", onescf), ("one11f", one11f), ("onecol", onecol),
                      ("bqp", bqp), ("bkp", bkp), ("bv_rep", bv_rep),
                      ("cw_h", cw_h), ("cb_h", cb_h), ("identf", identf)]:
        p.dma("sp", "qsp", dst, dram[name], writes=(name,))

    def ln_stats(src, rkeys, pb=0):
        b0 = 24 + 8 * pb
        st = stat6 if pb == 0 else smal[:, 48:60]
        kst = f"stat6_{pb}"
        p.op("dve", lambda e, st=st: e.bn_stats(st[:, 0:6], src[:, 0:512]),
             reads=rkeys, writes=(kst,))
        p.op("dve", lambda e, st=st: e.bn_stats(st[:, 6:12], src[:, 512:1024]),
             reads=rkeys, writes=(kst,))
        p.op("dve", lambda e, st=st, b0=b0: e.bn_aggr(smal[:, b0:b0 + 2], st[:, 0:12]),
             reads=(kst,), writes=(f"c_mv{pb}",))
        p.act_set(AFT.Sqrt)
        p.op("act", lambda e, b0=b0: e.activation(sc(b0 + 2), sc(b0 + 1), AFT.Sqrt,
                                                  bias=ceps, scale=1.0),
             reads=(f"c_mv{pb}", "ceps"), writes=(f"c_r{pb}",))
        p.op("dve", lambda e, b0=b0: e.reciprocal(sc(b0 + 2), sc(b0 + 2)),
             reads=(f"c_r{pb}",), writes=(f"c_r{pb}",))
        p.op("dve", lambda e, b0=b0: e.tensor_scalar(sc(b0 + 3), sc(b0), sc(b0 + 2), -1.0,
                                                     ALU.mult, ALU.mult),
             reads=(f"c_mv{pb}", f"c_r{pb}"), writes=(f"c_nmr{pb}",))
        return b0

    for t in range(nt):
        xb, lb = x_t[t % 2], xln_tm[t % 2]
        kx, kl = f"x{t % 2}", f"xl{t % 2}"
        p.dma("sp", "qsp", xb, dram["x_in"][t * 128:(t + 1) * 128, :], writes=(kx,))
        b0 = ln_stats(xb, (kx,), pb=t % 2)
        p.act_set(AFT.Identity)
        p.op("act", lambda e, xb=xb, lb=lb, b0=b0, t=t: e.activation(lb, xb, AFT.Identity,
             bias=sc(b0 + 3), scale=sc(b0 + 2)),
             reads=(kx, f"c_r{t % 2}", f"c_nmr{t % 2}"), writes=(kl,))
        for dc in range(8):
            sl = psT[:, dc * 128: dc * 128 + 128]
            kp = "bT"
            p.op("pe", lambda e, lb=lb, dc=dc, sl=sl: e.transpose(
                sl, lb[:, dc * 128:(dc + 1) * 128], ident),
                reads=(kl, "ident"), writes=(kp,))
            p.op("dve", lambda e, sl=sl, dc=dc, t=t: e.tensor_copy(
                xlnf[:, dc * SEQ + t * 128: dc * SEQ + (t + 1) * 128], sl),
                reads=(), writes=("xlnf", kp))

    for wname, bt, bkey, dst, dkey in [("wqtp", bqp, "bqp", qfm4, "qfm4"),
                                       ("wktp", bkp, "bkp", kfm4, "kfm4")]:
        for dc in range(8):
            stg, ks = wstg[dc % 2], f"wstg{dc % 2}"
            p.dma("sp", "qsp", stg[:, 0:512], dram[wname][dc * 128:(dc + 1) * 128, :],
                  writes=(ks,))
            p.op("dve", lambda e, stg=stg, dc=dc: e.tensor_copy(
                wqk_sb[:, dc * 512:(dc + 1) * 512], stg[:, 0:512]),
                reads=(ks,), writes=("wqk",))
        for pg in range(4):
            for ng in range(NG):
                ps, kp = (psA, "bA") if ng % 2 == 0 else (psB, "bB")
                for dc in range(8):
                    p.op("pe", lambda e, pg=pg, ng=ng, dc=dc, ps=ps: e.matmul(
                        ps[:, 0:NW],
                        wqk_sb[:, dc * 512 + pg * 128: dc * 512 + (pg + 1) * 128],
                        xlnf[:, dc * SEQ + ng * NW: dc * SEQ + (ng + 1) * NW],
                        start=(dc == 0), stop=(dc == 7)),
                        reads=("wqk", "xlnf"), writes=(kp,))
                p.act_set(AFT.Identity)
                p.op("act", lambda e, pg=pg, ng=ng, ps=ps, bt=bt, dst=dst: e.activation(
                    dst[:, pg * SEQ + ng * NW: pg * SEQ + (ng + 1) * NW], ps[:, 0:NW],
                    AFT.Identity, bias=bt[:, pg:pg + 1], scale=1.0),
                    reads=(bkey,), writes=(dkey, kp))

    for dc in range(8):
        stg, ks = wstg[dc % 2], f"wstg{dc % 2}"
        p.dma("sp", "qsp", stg[:, 0:1024], dram["wvt"][dc * 128:(dc + 1) * 128, :],
              writes=(ks,))
        p.op("dve", lambda e, stg=stg, dc=dc: e.tensor_copy(
            wvt_sb[:, dc * DM:(dc + 1) * DM], stg[:, 0:1024]), reads=(ks,), writes=("wvt",))
    for t in range(nt):
        for dc in range(8):
            for hf in range(2):
                p.op("pe", lambda e, t=t, dc=dc, hf=hf: e.matmul(
                    psC[:, hf * 512:(hf + 1) * 512],
                    xlnf[:, dc * SEQ + t * 128: dc * SEQ + (t + 1) * 128],
                    wvt_sb[:, dc * DM + hf * 512: dc * DM + (hf + 1) * 512],
                    start=(dc == 0), stop=(dc == 7)),
                    reads=("wvt", "xlnf"), writes=("bC0", "bC1"))
        dstv = fdims(vt[:, t * 1040: t * 1040 + 1], [[65, 16], [1, 64]])
        p.op("dve", lambda e, dstv=dstv: e.tensor_tensor(dstv, psC[:], bv_rep, ALU.add),
             reads=("bv_rep",), writes=("vt", "bC0", "bC1"))
        ones_ap = fdims(vt[:, t * 1040 + 64: t * 1040 + 65], [[65, 16]])
        p.op("dve", lambda e, ones_ap=ones_ap: e.memset(ones_ap, 1.0), writes=("vt",))

    for src, skey, dst, dkey in [(qfm4, "qfm4", qtm4s, "qtm4s"), (kfm4, "kfm4", ktm4s, "ktm4s")]:
        for t in range(nt):
            for pg in range(4):
                p.op("pe", lambda e, src=src, pg=pg, t=t: e.transpose(
                    psT[:, pg * 128: (pg + 1) * 128],
                    src[:, pg * SEQ + t * 128: pg * SEQ + (t + 1) * 128], ident),
                    reads=(skey, "ident"), writes=("bT",))
            p.act_set(AFT.Copy)
            p.op("act", lambda e, t=t, dst=dst: e.activation(
                dst[:, t * 512:(t + 1) * 512], psT[:, 0:512],
                AFT.Copy, bias=0.0, scale=SQH),
                reads=(), writes=(dkey, "bT"))

    p.barrier()

    for hh in range(2):
        for t in range(nt):
            p.op("dve", lambda e, hh=hh, t=t: e.memset(
                phk[hh][:, t * 280: t * 280 + 1], 1.0), writes=(f"phk{hh}",))
        p.op("dve", lambda e, hh=hh: e.memset(phk[hh][:, nt * 280: nt * 280 + 104], 0.0),
             writes=(f"phk{hh}",))
        p.op("dve", lambda e, hh=hh: e.memset(phq_tm[hh][:, 0:1], 1.0),
             writes=(f"phq_tm{hh}",))

    def wp_dma(idx):
        if idx < 32:
            r, cb = idx // 4, idx % 4
            src = dram["w1t"][r * 128:(r + 1) * 128, cb * 1024:(cb + 1) * 1024]
        else:
            i = idx - 32
            src = dram["w2t"][i * 128:(i + 1) * 128, :]
        j = idx % 2
        p.dma("act", "qact", w1stg[j], src, writes=(f"w1stg{j}",))

    def wp_comp(idx):
        wcol = wsum1[:, idx:idx + 1] if idx < 32 else wsum2[:, idx - 32:idx - 31]
        j = idx % 2
        p.act_set(AFT.Abs)
        p.op("act", lambda e, j=j: e.activation(w1stg[j], w1stg[j], AFT.Abs, bias=0.0,
                                                scale=1.0),
             reads=(f"w1stg{j}",), writes=(f"w1stg{j}",))
        p.op("dve", lambda e, wcol=wcol, j=j: e.reduce_sum(wcol, w1stg[j], AX.X),
             reads=(f"w1stg{j}",), writes=("wsum",))

    def wpass(idx):
        if idx > 0:
            wp_comp(idx - 1)
        if idx < 64:
            wp_dma(idx)

    wp_iter = iter(range(65))

    for hp in range(H // 2):
        heads = (2 * hp, 2 * hp + 1)
        hinfo = []
        for h in heads:
            pg_, sl_ = h // 4, h % 4
            po = sl_ * 32
            hinfo.append((h, pg_, po, (96, 0) if po == 96 else None))

        for hh, (h, pg_, po, tp) in enumerate(hinfo):
            kph = f"phk{hh}"
            us0 = ktm4s[:, pg_ * 128 + po: pg_ * 128 + po + 16]
            uin = fdims(us0, [[512, nt], [1, 16]])
            uout = fdims(phk[hh][:, 1:2], [[280, nt], [1, 16]])
            p.op("dve", lambda e, uin=uin, uout=uout: e.tensor_single_scalar(
                uout, uin, SQHI, ALU.mult), reads=("ktm4s",), writes=(kph,))
            o_out = fdims(phk[hh][:, 17:18], [[280, nt], [16, 16], [1, 16]])
            in_i = fdims(us0, [[512, nt], [1, 16], [0, 16]])
            in_j = fdims(us0, [[512, nt], [0, 16], [1, 16]])
            p.op("dve", lambda e, o_out=o_out, in_i=in_i, in_j=in_j: e.tensor_tensor(
                o_out, in_i, in_j, ALU.mult), reads=("ktm4s",), writes=(kph,))
            p.op("dve", lambda e, hh=hh: e.memset(s_sb[hh][:, 0:195], 0.0),
                 writes=(f"s_sb{hh}",))

        def gmm(hh, cc, half):
            h, pg_, po, tp = hinfo[hh]
            kq = qfm4[:, pg_ * SEQ + cc * 128: pg_ * SEQ + (cc + 1) * 128][po:po + 16, :]
            kk = kfm4[:, pg_ * SEQ + (cc - 1 + half) * 128: pg_ * SEQ + (cc + half) * 128][po:po + 16, :]
            p.op("pe", lambda e, kk=kk, kq=kq, hh=hh, half=half, tp=tp: e.matmul(
                psA[:, hh * 256 + half * 128: hh * 256 + (half + 1) * 128], kk, kq,
                start=True, stop=True, tile_position=tp), reads=("kfm4", "qfm4"),
                writes=("bA",))

        def psy_of(hh, c):
            if hh == 0:
                return (psD[:, 0:130], "bD") if c % 2 == 0 else (psB[:, 0:130], "bB")
            return (psE[:, 0:130], "bE") if c % 2 == 0 else (psF[:, 0:130], "bF")

        def qbuild(cc):
            for hh, (h, pg_, po, tp) in enumerate(hinfo):
                kpt = f"phq_tm{hh}"
                us = qtm4s[:, cc * 512 + pg_ * 128 + po: cc * 512 + pg_ * 128 + po + 16]
                p.op("dve", lambda e, us=us, hh=hh: e.tensor_single_scalar(
                    phq_tm[hh][:, 1:17], us, SQHI, ALU.mult),
                    reads=("qtm4s",), writes=(kpt,))
                o_out = fdims(phq_tm[hh][:, 17:18], [[16, 16], [1, 16]])
                in_i = fdims(us, [[1, 16], [0, 16]])
                in_j = fdims(us, [[0, 16], [1, 16]])
                p.op("dve", lambda e, o_out=o_out, in_i=in_i, in_j=in_j: e.tensor_tensor(
                    o_out, in_i, in_j, ALU.mult), reads=("qtm4s",), writes=(kpt,))

        def epi_act(c):
            for hh in range(2):
                psy, ky = psy_of(hh, c)
                tl, ktl = tloc[hh], f"tl{hh}"
                ct = C_T + 3 * hh
                p.op("act", lambda e, psy=psy, ct=ct: e.activation(
                    smal[:, ct:ct + 1], psy[:, 129:130], AFT.Identity, bias=c1e8,
                    scale=1.0), reads=("c1e8",), writes=(f"den{hh}", ky))
                p.op("act", lambda e, psy=psy, ct=ct: e.activation(
                    smal[:, ct + 2:ct + 3], psy[:, 64:65], AFT.Identity, bias=czero,
                    scale=1.0), reads=("czero",), writes=(f"dloc{hh}", ky))
                if c + 1 < nt:
                    p.op("act", lambda e, hh=hh: e.activation(
                        s16[hh][:, 0:195], s_sb[hh][:, 0:195], AFT.Copy, bias=0.0,
                        scale=1.0), reads=(f"s_sb{hh}",), writes=(f"s16{hh}",))

        def epi_dve(c):
            for hh in range(2):
                psy, ky = psy_of(hh, c)
                h = heads[hh]
                tl, ktl = tloc[hh], f"tl{hh}"
                ct = C_T + 3 * hh
                p.op("dve", lambda e, ct=ct: e.reciprocal(smal[:, ct + 1:ct + 2],
                                                          smal[:, ct:ct + 1]),
                     reads=(f"den{hh}",), writes=(f"rlin{hh}",))
                p.op("dve", lambda e, ct=ct: e.reciprocal(smal[:, ct + 2:ct + 3],
                                                          smal[:, ct + 2:ct + 3]),
                     reads=(f"dloc{hh}",), writes=(f"dloc{hh}",))
                p.op("dve", lambda e, psy=psy, tl=tl, ct=ct: e.tensor_single_scalar(
                    tl, psy[:, 0:64], smal[:, ct + 2:ct + 3], ALU.mult),
                    reads=(f"dloc{hh}",), writes=(ktl, ky))
                p.op("dve", lambda e, psy=psy, tl=tl, ct=ct, c=c, h=h: e.scalar_tensor_tensor(
                    attn[:, c * DM + h * 64: c * DM + h * 64 + 64], psy[:, 65:129],
                    smal[:, ct + 1:ct + 2], tl, ALU.mult, ALU.add),
                    reads=(f"rlin{hh}", ktl), writes=("attn", ky))

        if nt > 1:
            qbuild(1)
        for c in range(nt):
            lo = 0 if c > 0 else 128
            for hh, (h, pg_, po, tp) in enumerate(hinfo):
                pss = psC[:, 0:195] if hh == 0 else psC[:, 512:707]
                ks_ = f"bC{hh}"
                vcur = vt[:, c * 1040 + h * 65: c * 1040 + h * 65 + 65]
                for k in range(3):
                    p.op("pe", lambda e, k=k, c=c, hh=hh, pss=pss, vcur=vcur: e.matmul(
                        pss[:, k * 65:(k + 1) * 65],
                        phk[hh][:, c * 280 + k * 128: c * 280 + (k + 1) * 128],
                        vcur, start=True, stop=True),
                        reads=(f"phk{hh}", "vt"), writes=(ks_,))
            if c == 0:
                for hh in range(2):
                    gmm(hh, 0, 1)

            p.act_set(AFT.Exp)
            for hh in range(2):
                esb, ke = e_sb[hh * 2 + c % 2], f"esb{hh}_{c % 2}"
                p.op("act", lambda e, hh=hh, esb=esb, lo=lo: e.activation(
                    esb[:, lo:256], psA[:, hh * 256 + lo: hh * 256 + 256], AFT.Exp,
                    bias=0.0, scale=1.0), reads=(), writes=(ke, "bA"))
                p.op("act", lambda e, hh=hh, esb=esb: e.activation(
                    esb[:, 256:384], psA[:, hh * 256 + 128: hh * 256 + 256], AFT.Square,
                    bias=csqh, scale=2.0 ** -0.5), reads=("csqh",), writes=(ke, "bA"))
            if c > 0:
                epi_act(c - 1)

            for hh in range(2):
                esb, ke = e_sb[hh * 2 + c % 2], f"esb{hh}_{c % 2}"
                p.op("dve", lambda e, esb=esb, lo=lo: e.tensor_tensor(
                    esb[:, lo:256], esb[:, lo:256], m01ea[:, lo:256], ALU.mult),
                    reads=("m01ea",), writes=(ke,))
                p.op("dve", lambda e, esb=esb: e.scalar_tensor_tensor(
                    esb[:, 256:384], esb[:, 256:384], 0.5, m01ea[:, 256:384],
                    ALU.add, ALU.mult), reads=("m01ea",), writes=(ke,))

            if c + 1 < nt:
                for hh in range(2):
                    gmm(hh, c + 1, 0)
                    gmm(hh, c + 1, 1)
                for hh in range(2):
                    kpt = f"phq_tm{hh}"
                    for k in range(3):
                        w = 128 if k < 2 else 17
                        p.op("pe", lambda e, k=k, w=w, hh=hh: e.transpose(
                            psT[0:w, hh * 512 + k * 128: hh * 512 + k * 128 + 128],
                            phq_tm[hh][:, k * 128:k * 128 + w], ident),
                            reads=(kpt, "ident"), writes=("bT",))

            for hh, (h, pg_, po, tp) in enumerate(hinfo):
                psy, ky = psy_of(hh, c)
                esb, ke = e_sb[hh * 2 + c % 2], f"esb{hh}_{c % 2}"
                vprev = vt[:, (c - 1) * 1040 + h * 65: (c - 1) * 1040 + h * 65 + 65]
                vcur = vt[:, c * 1040 + h * 65: c * 1040 + h * 65 + 65]
                if c > 0:
                    p.op("pe", lambda e, esb=esb, vprev=vprev, psy=psy: e.matmul(
                        psy[:, 0:65], esb[:, 0:128], vprev, start=True, stop=False),
                        reads=(ke, "vt"), writes=(ky,))
                    p.op("pe", lambda e, esb=esb, vcur=vcur, psy=psy: e.matmul(
                        psy[:, 0:65], esb[:, 128:256], vcur, start=False, stop=True),
                        reads=(ke, "vt"), writes=(ky,))
                else:
                    p.op("pe", lambda e, esb=esb, vcur=vcur, psy=psy: e.matmul(
                        psy[:, 0:65], esb[:, 128:256], vcur, start=True, stop=True),
                        reads=(ke, "vt"), writes=(ky,))
                p.op("pe", lambda e, esb=esb, vcur=vcur, psy=psy, c=c: e.matmul(
                    psy[:, 65:130], esb[:, 256:384], vcur, start=True, stop=(c == 0)),
                    reads=(ke, "vt"), writes=(ky,))
                if c > 0:
                    slot = c % 2
                    kr = f"phqr{hh}_{slot}"
                    for k in range(3):
                        kw = 128 if k < 2 else 17
                        p.op("pe", lambda e, k=k, kw=kw, hh=hh, slot=slot, psy=psy: e.matmul(
                            psy[:, 65:130],
                            phqr[hh][0:kw, slot * 384 + k * 128: slot * 384 + (k + 1) * 128],
                            s16[hh][0:kw, k * 65:(k + 1) * 65], start=False, stop=(k == 2)),
                            reads=(kr, f"s16{hh}"), writes=(ky,))

            for hh in range(2):
                pss = psC[:, 0:195] if hh == 0 else psC[:, 512:707]
                ks_ = f"bC{hh}"
                p.op("dve", lambda e, pss=pss, hh=hh: e.tensor_tensor(
                    s_sb[hh][:, 0:195], s_sb[hh][:, 0:195], pss[:, 0:195], ALU.add),
                    reads=(f"s_sb{hh}",), writes=(f"s_sb{hh}", ks_))
            if c > 0:
                epi_dve(c - 1)
            if c + 1 < nt:
                for hh in range(2):
                    slot = (c + 1) % 2
                    kr = f"phqr{hh}_{slot}"
                    p.op("dve", lambda e, hh=hh, slot=slot: e.tensor_copy(
                        phqr[hh][:, slot * 384: slot * 384 + 256],
                        psT[:, hh * 512: hh * 512 + 256]), reads=(), writes=(kr, "bT"))
                    p.op("dve", lambda e, hh=hh, slot=slot: e.tensor_copy(
                        phqr[hh][0:17, slot * 384 + 256: slot * 384 + 384],
                        psT[0:17, hh * 512 + 256: hh * 512 + 384]),
                        reads=(), writes=(kr, "bT"))
            if c + 2 < nt:
                qbuild(c + 2)

            if (hp * nt + c) % 2 == 0:
                nxt = next(wp_iter, None)
                if nxt is not None:
                    wpass(nxt)
        epi_act(nt - 1)
        epi_dve(nt - 1)

    for nxt in wp_iter:
        wpass(nxt)

    for i, (ws, nel, c_ma, c_sc) in enumerate([(wsum1, DM * HID, C_MA1, C_SC1),
                                               (wsum2, HID * DM, C_MA2, C_SC2)]):
        p.op("pe", lambda e, ws=ws, i=i: e.matmul(psE[0:1, 256 + i * 32:256 + (i + 1) * 32],
                                                  onecol, ws, start=True, stop=True),
             reads=("wsum", "onecol"), writes=("bE",))
        p.op("dve", lambda e, i=i: e.tensor_reduce(
            s1x1[:, 4 * i:4 * i + 1], psE[0:1, 256 + i * 32:256 + (i + 1) * 32], AX.X, ALU.add),
            reads=(), writes=(f"s1a{i}", "bE"))
        p.op("dve", lambda e, i=i, nel=nel: e.tensor_scalar(
            s1x1[:, 4 * i + 1:4 * i + 2], s1x1[:, 4 * i:4 * i + 1], 1.0 / nel, 1e-5,
            ALU.mult, ALU.max), reads=(f"s1a{i}",), writes=(f"s1b{i}",))
        p.op("pe", lambda e, i=i: e.matmul(psE[:, 320 + i:321 + i], onescf,
                                           s1x1[:, 4 * i + 1:4 * i + 2],
                                           start=True, stop=True),
             reads=(f"s1b{i}", "onescf"), writes=("bE",))
        p.op("dve", lambda e, i=i, c_ma=c_ma: e.tensor_copy(sc(c_ma), psE[:, 320 + i:321 + i]),
             reads=(), writes=(f"cma{i}", "bE"))
        p.op("dve", lambda e, c_ma=c_ma, c_sc=c_sc, i=i: e.reciprocal(sc(c_sc), sc(c_ma)),
             reads=(f"cma{i}",), writes=(f"csc{i}",))

    p.barrier()
    if dbg:
        for t in range(nt):
            p.dma("sp", "qsp", dram["dbg_attn"][t * 128:(t + 1) * 128, :],
                  attn[:, t * DM:(t + 1) * DM], reads=("attn",))

    def tern_src(i):
        if i < 32:
            r, cb = i // 4, i % 4
            return dram["w1t"][r * 128:(r + 1) * 128, cb * 1024:(cb + 1) * 1024]
        r = i - 32
        return dram["w2t"][r * 128:(r + 1) * 128, :]

    def tern_dma(i):
        p.dma("act", "qact", wbuf[i % 2], tern_src(i), writes=(f"wbuf{i % 2}",))

    def tern_chunk(i):
        if i < 32:
            dst = w1q[:, (i // 4) * HID + (i % 4) * 1024: (i // 4) * HID + (i % 4 + 1) * 1024]
            kd, scl, kscl = "w1q", sc(C_SC1), "csc0"
        else:
            dst = w2q[:, (i - 32) * DM:(i - 31) * DM]
            kd, scl, kscl = "w2q", sc(C_SC2), "csc1"
        j = i % 2
        wb, sg, db = wbuf[j], sgb[j], dbb[j]
        kw, ks, kb2 = f"wbuf{j}", f"sgb{j}", f"dbb{j}"
        p.act_set(AFT.Sign)
        p.op("act", lambda e, wb=wb, sg=sg: e.activation(sg, wb, AFT.Sign, bias=0.0,
                                                         scale=1.0),
             reads=(kw,), writes=(ks,))
        p.act_set(AFT.Abs)
        p.op("act", lambda e, scl=scl, wb=wb: e.activation(wb, wb, AFT.Abs, bias=0.0,
                                                           scale=scl),
             reads=(kw, kscl), writes=(kw,))
        p.op("dve", lambda e, wb=wb, db=db: e.tensor_single_scalar(db, wb, 0.5, ALU.is_ge),
             reads=(kw,), writes=(kb2,))
        p.op("dve", lambda e, dst=dst, db=db, sg=sg: e.tensor_tensor(dst, db, sg, ALU.mult),
             reads=(kb2, ks), writes=(kd,))
        if i + 2 < 64:
            tern_dma(i + 2)

    tern_iter = iter(range(64))
    tern_dma(0)
    tern_dma(1)
    for t in range(nt):
        pb = t % 2
        xb, kx = x_t4[pb], f"x4_{pb}"
        p.dma("sp", "qsp", xb, dram["x_in"][t * 128:(t + 1) * 128, :], writes=(kx,))
        asl = attn[:, t * DM:(t + 1) * DM]
        b0 = ln_stats(asl, ("attn",), pb=pb)
        p.act_set(AFT.Identity)
        p.op("act", lambda e, asl=asl, b0=b0: e.activation(ln3b, asl, AFT.Identity,
             bias=sc(b0 + 3), scale=sc(b0 + 2)),
             reads=("attn", f"c_r{pb}", f"c_nmr{pb}"), writes=("ln3b",))
        p.op("dve", lambda e, xb=xb: e.tensor_tensor(xb, ln3b, xb, ALU.add),
             reads=("ln3b", kx), writes=(kx,))
        p.dma("sp", "qsp", xmid_d[t * 128:(t + 1) * 128, :], xb, reads=(kx,),
              writes=(f"xmd{t}",))
        if dbg:
            p.dma("sp", "qsp", dram["dbg_xmid"][t * 128:(t + 1) * 128, :], xb, reads=(kx,))
        for _ in range(4):
            nxt = next(tern_iter, None)
            if nxt is not None:
                tern_chunk(nxt)
    for nxt in tern_iter:
        tern_chunk(nxt)

    p.barrier()

    def ph4b_pre(t):
        sl4 = t % 4
        xm, kxm = xm_in[sl4], f"xm{sl4}"
        pb = t % 2
        xqtb, kqt = xqtm[pb], f"xqtm{pb}"
        p.dma("sp", "qsp", xm, xmid_d[t * 128:(t + 1) * 128, :],
              reads=(f"xmd{t}",), writes=(kxm,))
        st = smal[:, 60 + 12 * pb: 72 + 12 * pb]
        kst = f"st4_{pb}"
        b0 = 84 + 4 * pb
        p.op("dve", lambda e, st=st, xm=xm: e.bn_stats(st[:, 0:6], xm[:, 0:512]),
             reads=(kxm,), writes=(kst,))
        p.op("dve", lambda e, st=st, xm=xm: e.bn_stats(st[:, 6:12], xm[:, 512:1024]),
             reads=(kxm,), writes=(kst,))
        p.op("dve", lambda e, st=st, b0=b0: e.bn_aggr(smal[:, b0:b0 + 2], st[:, 0:12]),
             reads=(kst,), writes=(f"mv4{pb}",))
        p.act_set(AFT.Sqrt)
        p.op("act", lambda e, b0=b0: e.activation(sc(b0 + 2), sc(b0 + 1), AFT.Sqrt,
                                                  bias=ceps, scale=1.0),
             reads=(f"mv4{pb}", "ceps"), writes=(f"r4{pb}",))
        p.op("dve", lambda e, b0=b0: e.reciprocal(sc(b0 + 2), sc(b0 + 2)),
             reads=(f"r4{pb}",), writes=(f"r4{pb}",))
        p.op("dve", lambda e, b0=b0: e.tensor_scalar(sc(b0 + 3), sc(b0), sc(b0 + 2), -1.0,
                                                     ALU.mult, ALU.mult),
             reads=(f"mv4{pb}", f"r4{pb}"), writes=(f"n4{pb}",))
        p.act_set(AFT.Identity)
        p.op("act", lambda e, xm=xm, b0=b0: e.activation(xl2b, xm, AFT.Identity,
             bias=sc(b0 + 3), scale=sc(b0 + 2)),
             reads=(kxm, f"r4{pb}", f"n4{pb}"), writes=("xl2b",))
        cq = 92 + 3 * pb
        p.op("dve", lambda e, cq=cq: e.reduce_max(sc(cq), xl2b, AX.X,
                                                  apply_absolute_value=True),
             reads=("xl2b",), writes=(f"cmax4_{pb}",))
        p.op("dve", lambda e, cq=cq: e.tensor_single_scalar(sc(cq), sc(cq), 1e-5, ALU.max),
             reads=(f"cmax4_{pb}",), writes=(f"cmax4_{pb}",))
        p.op("dve", lambda e, cq=cq: e.tensor_scalar(sc(cq + 1), sc(cq), sc(C_MA1),
                                                     1.0 / 127.0, ALU.mult, ALU.mult),
             reads=(f"cmax4_{pb}", "cma0"), writes=(f"cisx4_{pb}",))
        p.op("dve", lambda e, cq=cq: e.reciprocal(sc(cq + 2), sc(cq)),
             reads=(f"cmax4_{pb}",), writes=(f"csx4_{pb}",))
        p.op("dve", lambda e, cq=cq: e.tensor_single_scalar(sc(cq + 2), sc(cq + 2), 127.0,
                                                            ALU.mult),
             reads=(f"csx4_{pb}",), writes=(f"csx4_{pb}",))
        p.op("dve", lambda e, cq=cq, pb=pb: e.tensor_scalar(
            xqi[pb], xl2b, sc(cq + 2), 1000.5, ALU.mult, ALU.add),
            reads=("xl2b", f"csx4_{pb}"), writes=(kqt,))
        p.op("dve", lambda e, cq=cq, pb=pb, xqtb=xqtb: e.tensor_scalar(
            xqtb, xqi[pb], 1000.0, sc(cq + 1), ALU.subtract, ALU.mult),
            reads=(kqt, f"cisx4_{pb}"), writes=(kqt,))
    def ph4b_tr(t):
        sl4 = t % 4
        pb = t % 2
        xqtb, kqt = xqtm[pb], f"xqtm{pb}"
        xs, kxs = xqs[sl4], f"xqs{sl4}"
        for dc in range(8):
            p.op("pe", lambda e, dc=dc, xqtb=xqtb: e.transpose(
                psT[:, dc * 128: (dc + 1) * 128],
                xqtb[:, dc * 128:(dc + 1) * 128], ident),
                reads=(kqt, "ident"), writes=("bT",))
        d1 = fdims(xs[:, 1:2], [[132, 8], [1, 128]])
        p.op("dve", lambda e, d1=d1: e.tensor_copy(d1, psT[:, 0:1024]),
             reads=(), writes=(kxs, "bT"))
        if t == 0:
            z0 = fdims(xs[:, 0:1], [[132, 8]])
            p.op("dve", lambda e, z0=z0: e.memset(z0, 0.0), writes=(kxs,))
        else:
            slp = (t - 1) % 4
            dsth = fdims(xqs[slp][:, 129:130], [[132, 8]])
            srch = fdims(xs[:, 1:2], [[132, 8]])
            p.op("dve", lambda e, dsth=dsth, srch=srch: e.tensor_copy(dsth, srch),
                 reads=(kxs,), writes=(f"xqs{slp}",))
            dstl = fdims(xs[:, 0:1], [[132, 8]])
            srcl = fdims(xqs[slp][:, 128:129], [[132, 8]])
            p.op("dve", lambda e, dstl=dstl, srcl=srcl: e.tensor_copy(dstl, srcl),
                 reads=(f"xqs{slp}",), writes=(kxs,))
        if t == nt - 1:
            z1 = fdims(xs[:, 129:130], [[132, 8]])
            p.op("dve", lambda e, z1=z1: e.memset(z1, 0.0), writes=(kxs,))
        if dbg:
            dsrc = fdims(xs[:, 1:2], [[132, 8], [1, 128]])
            ddst = fdims(dram["dbg_xqf"][:, t * 128: t * 128 + 1], [[SEQ, 8], [1, 128]])
            p.dma("sp", "qsp", ddst, dsrc, reads=(kxs,))

    def quant_group(t, g):
        hqb, hqi, khq = hqg[g % 2], hqg_i16[g % 2], f"hqg{g % 2}"
        h2g = h2[:, g * 1024:(g + 1) * 1024]
        kh2 = f"h2g{g}"
        srep = fdims(sbc[:, 0:1], [[0, 8], [1, 128]])
        p.op("dve", lambda e, h2g=h2g, srep=srep: e.tensor_tensor(h2g, h2g, srep, ALU.mult),
             reads=("sbc", kh2), writes=(kh2,))
        p.op("dve", lambda e, h2g=h2g, hqi=hqi: e.tensor_single_scalar(hqi, h2g, 1000.5,
                                                                       ALU.add),
             reads=(kh2,), writes=(khq,))
        p.op("dve", lambda e, hqb=hqb, hqi=hqi: e.tensor_single_scalar(hqb, hqi, 1000.0,
                                                                       ALU.subtract),
             reads=(khq,), writes=(khq,))
        if dbg and t == 0:
            p.dma("sp", "qsp", dram["dbg_h2"][:, g * 1024:(g + 1) * 1024], h2g, reads=(kh2,))
            p.dma("sp", "qsp", dram["dbg_hq"][:, g * 1024:(g + 1) * 1024], hqb, reads=(khq,))

    def w2_group(g):
        hqb, khq = hqg[g % 2], f"hqg{g % 2}"
        for mj in range(8):
            m = g * 8 + mj
            for hf in range(2):
                p.op("pe", lambda e, m=m, mj=mj, hf=hf, hqb=hqb: e.matmul(
                    psC[:, hf * 512:(hf + 1) * 512], hqb[:, mj * 128:(mj + 1) * 128],
                    w2q[:, m * DM + hf * 512: m * DM + (hf + 1) * 512],
                    start=(m == 0), stop=(m == NM - 1)),
                    reads=(khq, "w2q"), writes=("bC0", "bC1"))

    def stage1(t, fill=None):
        xs, kxs = xqs[t % 4], f"xqs{t % 4}"
        w1banks = [(psA, "bA"), (psB, "bB"), (psE, "bE"), (psF, "bF")]
        for m in range(NM):
            psh, kh = w1banks[m % 4]
            hb, kb = hsb[m % 2], f"hsb{m % 2}"
            acb, kc = accb[m % 2], f"acc{m % 2}"
            for dc in range(8):
                p.op("pe", lambda e, m=m, psh=psh, xs=xs, dc=dc: e.matmul(
                    psh[:, 0:130],
                    w1q[:, dc * HID + m * 128: dc * HID + (m + 1) * 128],
                    xs[:, dc * 132: dc * 132 + 130], start=(dc == 0), stop=(dc == 7)),
                    reads=("w1q", kxs), writes=(kh,))
            p.act_set(AFT.Copy)
            p.op("act", lambda e, psh=psh, hb=hb: e.activation(
                hb, psh[:, 0:130], AFT.Copy, bias=0.0, scale=1.0),
                reads=(), writes=(kb, kh))
            p.op("dve", lambda e, hb=hb, acb=acb, m=m: e.tensor_single_scalar(
                acb, hb[:, 0:128], cw_h[:, m:m + 1], ALU.mult),
                reads=(kb, "cw_h"), writes=(kc,))
            p.op("dve", lambda e, hb=hb, acb=acb, m=m: e.scalar_tensor_tensor(
                acb, hb[:, 1:129], cw_h[:, 32 + m:33 + m], acb, ALU.mult, ALU.add),
                reads=(kb, kc, "cw_h"), writes=(kc,))
            p.op("dve", lambda e, hb=hb, acb=acb, m=m: e.scalar_tensor_tensor(
                acb, hb[:, 2:130], cw_h[:, 64 + m:65 + m], acb, ALU.mult, ALU.add),
                reads=(kb, kc, "cw_h"), writes=(kc,))
            p.act_set(AFT.Silu)
            p.op("act", lambda e, acb=acb, m=m: e.activation(
                h2[:, m * 128:(m + 1) * 128], acb, AFT.Silu, bias=cb_h[:, m:m + 1], scale=1.0),
                reads=(kc, "cb_h"), writes=(f"h2g{m // 8}",))
            if fill is not None and m in (7, 15, 23, 31):
                fill(m // 8)

    def h2_partial(t, b):
        rin = fdims(h2[:, b * 1024: b * 1024 + 1], [[1, 128], [128, 8]])
        dst = pmax if b == 0 else sbcp
        kdst = "pmax" if b == 0 else "sbcp"
        p.op("dve", lambda e, rin=rin, dst=dst: e.tensor_reduce(
            dst, rin, AX.X, ALU.max, apply_absolute_value=True),
            reads=(f"h2g{b}",), writes=(kdst,))
        if b > 0:
            p.op("dve", lambda e: e.tensor_tensor(pmax, pmax, sbcp, ALU.max),
                 reads=("pmax", "sbcp"), writes=("pmax",))

    def qdance(t):
        p.op("pe", lambda e: e.transpose(psD[:, 0:128], pmax, identf),
             reads=("pmax", "identf"), writes=("bD",))
        p.op("dve", lambda e: e.reduce_max(sc(C_MAX), psD[:, 0:128], AX.X),
             reads=(), writes=("cmax5", "bD"))
        p.op("dve", lambda e: e.tensor_single_scalar(sc(C_MAX), sc(C_MAX), 1e-5, ALU.max),
             reads=("cmax5",), writes=("cmax5",))
        p.op("dve", lambda e: e.tensor_scalar(sc(C_F), sc(C_MAX), sc(C_MA2), 1.0 / 127.0,
                                              ALU.mult, ALU.mult),
             reads=("cmax5", "cma1"), writes=("cf",))
        p.op("dve", lambda e: e.reciprocal(sc(C_SX), sc(C_MAX)),
             reads=("cmax5",), writes=("csx5",))
        p.op("dve", lambda e: e.tensor_single_scalar(sc(C_SX), sc(C_SX), 127.0, ALU.mult),
             reads=("csx5",), writes=("csx5",))
        p.op("pe", lambda e: e.transpose(psD[0:1, 128:256], sc(C_SX), identf),
             reads=("csx5", "identf"), writes=("bD",))
        p.op("dve", lambda e: e.tensor_copy(srow_full[0:1, :], psD[0:1, 128:256]),
             reads=(), writes=("srow", "bD"))
        p.op("pe", lambda e: e.matmul(psD[:, 256:384], onescf, srow_full[0:1, :],
                                      start=True, stop=True),
             reads=("srow", "onescf"), writes=("bD",))
        p.op("dve", lambda e: e.tensor_copy(sbc, psD[:, 256:384]),
             reads=(), writes=("sbc", "bD"))
        quant_group(t, 0)
        quant_group(t, 1)

    def mk_fill(t):
        def fill(b):
            h2_partial(t, b)
            if t > 0:
                if b == 1:
                    w2_group(0)
                    quant_group(t - 1, 2)
                elif b == 2:
                    w2_group(1)
                    quant_group(t - 1, 3)
                elif b == 3:
                    w2_group(2)
                    w2_group(3)
        return fill

    def stage2b(t):
        p.op("dve", lambda e: e.bn_stats(stat6[:, 0:6], psC[:, 0:512]),
             reads=(), writes=("stat6", "bC0"))
        p.op("dve", lambda e: e.bn_stats(stat6[:, 6:12], psC[:, 512:1024]),
             reads=(), writes=("stat6", "bC1"))
        p.op("dve", lambda e: e.bn_aggr(smal[:, C_MEAN:C_VAR + 1], stat6[:]),
             reads=("stat6",), writes=("c_mv",))
        p.op("dve", lambda e: e.tensor_scalar(sc(C_VT2), sc(C_VAR), sc(C_F), sc(C_F),
                                              ALU.mult, ALU.mult),
             reads=("c_mv", "cf"), writes=("cvt2",))
        p.act_set(AFT.Sqrt)
        p.op("act", lambda e: e.activation(sc(C_R), sc(C_VT2), AFT.Sqrt, bias=ceps, scale=1.0),
             reads=("cvt2", "ceps"), writes=("c_r",))
        p.op("dve", lambda e: e.reciprocal(sc(C_R), sc(C_R)), reads=("c_r",), writes=("c_r",))
        p.op("dve", lambda e: e.tensor_scalar(sc(C_S4), sc(C_R), sc(C_F), None, ALU.mult),
             reads=("c_r", "cf"), writes=("cs4",))
        p.op("dve", lambda e: e.tensor_scalar(sc(C_B4), sc(C_MEAN), sc(C_S4), -1.0,
                                              ALU.mult, ALU.mult),
             reads=("c_mv", "cs4"), writes=("cb4",))
        p.act_set(AFT.Identity)
        p.op("act", lambda e: e.activation(outb, psC[:], AFT.Identity,
                                           bias=sc(C_B4), scale=sc(C_S4)),
             reads=("cs4", "cb4"), writes=("outb", "bC0", "bC1"))
        p.op("dve", lambda e, t=t: e.tensor_tensor(outb, outb, xm_in[t % 4], ALU.add),
             reads=("outb", f"xm{t % 4}"), writes=("outb",))
        p.dma("sp", "qsp", dram["out"][t * 128:(t + 1) * 128, :], outb, reads=("outb",))

    ph4b_pre(0)
    if nt > 1:
        ph4b_pre(1)
    ph4b_tr(0)
    if nt > 1:
        if nt > 2:
            ph4b_pre(2)
        ph4b_tr(1)
    for t in range(nt + 1):
        if t < nt:
            stage1(t, fill=mk_fill(t))
        else:
            w2_group(0)
            quant_group(t - 1, 2)
            w2_group(1)
            quant_group(t - 1, 3)
            w2_group(2)
            w2_group(3)
        if 0 < t:
            stage2b(t - 1)
        if t + 2 < nt:
            ph4b_tr(t + 2)
        if t < nt:
            qdance(t)
        if t + 3 < nt:
            ph4b_pre(t + 3)

    p.finalize()
    return nc, es


_BUILD_CACHE = {}


def _get_program(nt, dbg=False):
    key = (nt, dbg)
    if key not in _BUILD_CACHE:
        _BUILD_CACHE[key] = build_program(nt, dbg, strict_eng="dve")
    return _BUILD_CACHE[key]


def host_inputs(x_core, Wq, bq, Wk, bk, Wv, bv, w1, conv_w, conv_b, w2, nt=16):
    f32 = np.float32
    bf = ml_dtypes.bfloat16
    d = {}
    d["x_in"] = np.ascontiguousarray(x_core, dtype=f32)
    wqt = np.zeros((DM, 512), f32)
    wkt = np.zeros((DM, 512), f32)
    bqp = np.zeros((128, 4), f32)
    bkp = np.zeros((128, 4), f32)
    for h in range(H):
        pg, sl = h // 4, h % 4
        wqt[:, pg * 128 + sl * 32: pg * 128 + sl * 32 + 16] = Wq[h * 16:(h + 1) * 16, :].T
        wkt[:, pg * 128 + sl * 32: pg * 128 + sl * 32 + 16] = Wk[h * 16:(h + 1) * 16, :].T
        bqp[sl * 32: sl * 32 + 16, pg] = bq[h * 16:(h + 1) * 16]
        bkp[sl * 32: sl * 32 + 16, pg] = bk[h * 16:(h + 1) * 16]
    d["wqtp"] = wqt
    d["wktp"] = wkt
    d["bqp"] = bqp
    d["bkp"] = bkp
    d["wvt"] = np.ascontiguousarray(Wv.T, dtype=f32)
    d["bv_rep"] = np.ascontiguousarray(np.broadcast_to(bv, (128, DM)), dtype=f32)
    d["w1t"] = np.ascontiguousarray(w1.T, dtype=f32)
    d["w2t"] = np.ascontiguousarray(w2.T, dtype=f32)
    cw = np.zeros((128, 96), f32)
    for k in range(3):
        cw[:, k * 32:(k + 1) * 32] = conv_w[:, 0, k].reshape(NM, 128).T
    d["cw_h"] = cw
    d["cb_h"] = np.ascontiguousarray(conv_b.reshape(NM, 128).T, dtype=f32)
    d["ident"] = np.eye(128, dtype=bf)
    pp, qq = np.arange(128)[:, None], np.arange(128)[None, :]
    m_prev = (pp - qq >= 65).astype(bf)
    dqp = qq - pp
    m_diag = ((dqp >= 0) & (dqp <= 63)).astype(bf)
    m01a = (qq >= pp).astype(np.float32)
    d["m01ea"] = np.ascontiguousarray(
        np.concatenate([m_prev.astype(np.float32), m_diag.astype(np.float32), m01a], axis=1)).astype(bf)
    d["m01ah"] = (0.5 * m01a).astype(bf)
    d["onescf"] = np.ones((1, 128), f32)
    d["identf"] = np.eye(128, dtype=f32)
    d["one11f"] = np.ones((1, 1), f32)
    d["onecol"] = np.ones((128, 1), f32)
    return d


def kernel(L0_attn_x, L0_attn_Wq, L0_attn_bq, L0_attn_Wk, L0_attn_bk, L0_attn_Wv,
           L0_attn_bv, L0_ffn_w1, L0_ffn_conv_w, L0_ffn_conv_b, L0_ffn_w2):
    x = np.asarray(L0_attn_x)
    nt = x.shape[1] // 128
    nc, _es = _get_program(nt)
    in_maps = []
    for c in range(x.shape[0]):
        in_maps.append(host_inputs(
            x[c], np.asarray(L0_attn_Wq), np.asarray(L0_attn_bq), np.asarray(L0_attn_Wk),
            np.asarray(L0_attn_bk), np.asarray(L0_attn_Wv), np.asarray(L0_attn_bv),
            np.asarray(L0_ffn_w1), np.asarray(L0_ffn_conv_w), np.asarray(L0_ffn_conv_b),
            np.asarray(L0_ffn_w2), nt=nt))
    res = run_bass_kernel_spmd(nc, in_maps, core_ids=list(range(x.shape[0])))
    return np.stack([np.asarray(r["out"]) for r in res.results], axis=0)


# version 20
# speedup vs baseline: 1.0021x; 1.0021x over previous
import os
import numpy as np
import ml_dtypes
from contextlib import ExitStack

import concourse.bass as bass
import concourse.mybir as mybir
from concourse.alu_op_type import AluOpType as ALU
from concourse.bass_utils import run_bass_kernel_spmd

F32 = mybir.dt.float32
BF16 = mybir.dt.bfloat16
I16 = mybir.dt.int16
AFT = mybir.ActivationFunctionType
AX = mybir.AxisListType

H, DP, DH, DT = 16, 16, 64, 273
DM, HID = 1024, 4096
NM = HID // 128
SQH = 2.0 ** (-0.25)
SQHI = 2.0 ** (0.25)
EPS = 1e-5
NCOL = 52400

ENGS = ("pe", "dve", "act", "pool", "sp")
QUEUES = ("qsp", "qact", "qpool")


def fdims(ap, dims, extra_off=0):
    return bass.AP(ap.tensor, ap.offset + extra_off,
                   [list(ap.ap[0])] + [list(d) for d in dims])


class Prog:
    RSLOT = 8

    def __init__(self, nc, block, strict=False, strict_dma=False, strict_eng=""):
        self.strict = strict
        self.strict_dma = strict_dma
        self.strict_eng = strict_eng
        self.sems = {}
        self.qsems = {}
        self._ctx = ExitStack()
        for name in ENGS:
            self.sems[name] = self._ctx.enter_context(nc.semaphore("sem_" + name))
        for q in QUEUES:
            self.qsems[q] = [self._ctx.enter_context(nc.semaphore(f"sem_{q}_{i}"))
                             for i in range(self.RSLOT)]
        self.cnt = {name: 0 for name in ENGS}
        self.qcnt = {q: 0 for q in QUEUES}
        self.last_w = {}
        self.readers = {}
        self.hw = {}
        self.secs = {"pe": block.tensor, "dve": block.vector, "act": block.scalar,
                     "pool": block.gpsimd, "sp": block.sync}
        self.nops = 0
        self.maxops = int(os.environ.get("KMAXOPS", "0"))
        self.act_cur = None
        from concourse.hw_specs import get_activation_tables
        self._act_tables = list(get_activation_tables(nc.m.arch).values())

    def act_set(self, func):
        for idx, funcs in enumerate(self._act_tables):
            if func in funcs:
                break
        else:
            raise ValueError(func)
        if self.act_cur is not None and func in self._act_tables[self.act_cur]:
            return
        self.act_cur = idx
        self.op("act", lambda e: e.preload_act_table(func))

    def _qwait(self, q, idx):
        slot = (idx - 1) % self.RSLOT
        uses = (idx - 1) // self.RSLOT + 1
        return self.qsems[q][slot], 16 * uses, (q, slot)

    def _edge_wait(self, eng, dom, val, waits):
        if dom in QUEUES:
            sem, tgt, hwkey = self._qwait(dom, val)
        else:
            sem, tgt, hwkey = self.sems[dom], val, dom
        k = (eng, hwkey)
        if self.hw.get(k, 0) < tgt:
            self.hw[k] = tgt
            waits.append((sem, tgt))

    def op(self, eng, fn, reads=(), writes=(), dom=None):
        if self.maxops and self.nops >= self.maxops:
            return 0
        dom = dom or eng
        is_dma = dom in QUEUES

        def keep(d):
            if d != dom:
                return True
            if self.strict:
                return True
            if d in QUEUES:
                return self.strict_dma
            return d in self.strict_eng

        edges = []
        for k in reads:
            lw = self.last_w.get(k)
            if lw is not None and keep(lw[0]):
                edges.append(lw)
        for k in writes:
            lw = self.last_w.get(k)
            if lw is not None and keep(lw[0]):
                edges.append(lw)
            for rdom, rval in self.readers.get(k, {}).items():
                if keep(rdom):
                    edges.append((rdom, rval))
        waits = []
        for d, v in edges:
            self._edge_wait(eng, d, v, waits)
        if is_dma:
            self.qcnt[dom] += 1
            val = self.qcnt[dom]
            sem, tgt, hwkey = self._qwait(dom, val)
            k = (eng, hwkey)
            if tgt - 16 > 0 and self.hw.get(k, 0) < tgt - 16:
                self.hw[k] = tgt - 16
                waits.append((sem, tgt - 16))
            inc_sem, inc_amt = sem, 16
        else:
            self.cnt[dom] += 1
            val = self.cnt[dom]
            inc_sem, inc_amt = self.sems[dom], 1
            if self.strict and val > 1:
                waits.append((inc_sem, val - 1))

        def body(e):
            for sm, v in waits:
                e.wait_ge(sm, v)
            fn(e).then_inc(inc_sem, inc_amt)

        self.secs[eng](body)
        self.nops += 1
        for k in reads:
            self.readers.setdefault(k, {})[dom] = val
        for k in writes:
            self.last_w[k] = (dom, val)
            self.readers[k] = {}
        return val

    def dma(self, eng, queue, out, in_, reads=(), writes=()):
        return self.op(eng, lambda e: e.dma_start(out=out, in_=in_),
                       reads=reads, writes=writes, dom=queue)

    def _all_final(self):
        fin = []
        for name in ENGS:
            if self.cnt[name] > 0:
                fin.append((name, self.sems[name], self.cnt[name], name))
        for q in QUEUES:
            n = self.qcnt[q]
            for slot in range(self.RSLOT):
                uses = (n - slot + self.RSLOT - 1) // self.RSLOT if n > slot else 0
                uses = len(range(slot, n, self.RSLOT))
                if uses > 0:
                    fin.append(((q, slot), self.qsems[q][slot], 16 * uses, (q, slot)))
        return fin

    def barrier(self):
        fin = self._all_final()
        for eng in ENGS:
            waits = []
            for hwkey, sem, v, _ in fin:
                if hwkey == eng:
                    continue
                if self.hw.get((eng, hwkey), 0) < v:
                    self.hw[(eng, hwkey)] = v
                    waits.append((sem, v))
            def body(e, waits=waits):
                for sm, v in waits:
                    e.wait_ge(sm, v)

            self.secs[eng](body)

    def finalize(self):
        fin = self._all_final()

        def body(e):
            for hwkey, sem, v, _ in fin:
                if hwkey == "sp":
                    continue
                e.wait_ge(sem, v)

        self.secs["sp"](body)


class Arena:
    def __init__(self, nc):
        self.h = nc.alloc_sbuf_tensor("arena", [128, NCOL], F32)
        self.A32 = self.h[:, :]
        self.A16 = self.h[:, :].bitcast(BF16)
        self.AI16 = self.h[:, :].bitcast(I16)
        self.cur = 0
        self.limit = NCOL

    def region(self, base, limit):
        self.cur = base
        self.limit = limit

    def f32(self, n):
        ap = self.A32[:, self.cur:self.cur + n]
        self.cur += n
        assert self.cur <= self.limit, f"arena overflow {self.cur} > {self.limit}"
        return ap

    def bf16(self, n16):
        assert n16 % 2 == 0
        ap = self.A16[:, 2 * self.cur: 2 * self.cur + n16]
        self.cur += n16 // 2
        assert self.cur <= self.limit, f"arena overflow {self.cur} > {self.limit}"
        return ap

    def i16(self, n16):
        assert n16 % 2 == 0
        ap = self.AI16[:, 2 * self.cur: 2 * self.cur + n16]
        self.cur += n16 // 2
        assert self.cur <= self.limit, f"arena overflow {self.cur} > {self.limit}"
        return ap


def build_program(nt=16, dbg=False, strict=False, detect=False, strict_dma=False, strict_eng=""):
    SEQ = nt * 128
    NW = min(512, SEQ)
    NG = SEQ // NW
    nc = bass.Bass("TRN2", target_bir_lowering=False, debug=False,
                   detect_race_conditions=detect)

    dram = {}
    for name, shape, dt in [
        ("x_in", [SEQ, DM], F32), ("wqtp", [DM, 512], F32), ("wktp", [DM, 512], F32),
        ("wvt", [DM, DM], F32), ("bqp", [128, 4], F32), ("bkp", [128, 4], F32),
        ("bv_rep", [128, DM], F32), ("w1t", [DM, HID], F32), ("w2t", [HID, DM], F32),
        ("cw_h", [128, 96], F32), ("cb_h", [128, 32], F32), ("ident", [128, 128], BF16),
        ("m01ea", [128, 384], BF16), ("m01ah", [128, 128], BF16),
        ("onescf", [1, 128], F32), ("one11f", [1, 1], F32), ("onecol", [128, 1], F32),
        ("identf", [128, 128], F32),
    ]:
        dram[name] = nc.dram_tensor(name, shape, dt, kind="ExternalInput").ap()
    dram["out"] = nc.dram_tensor("out", [SEQ, DM], F32, kind="ExternalOutput").ap()
    xmid_d = nc.dram_tensor("xmid_sc", [SEQ, DM], F32, kind="Internal").ap()
    if dbg:
        for name, shape, dt in [("dbg_attn", [SEQ, DM], F32), ("dbg_xmid", [SEQ, DM], F32),
                                ("dbg_xqf", [128, 8 * SEQ], BF16),
                                ("dbg_h2", [128, HID], F32), ("dbg_hq", [128, HID], BF16)]:
            dram[name] = nc.dram_tensor(name, shape, dt, kind="ExternalOutput").ap()

    ar = Arena(nc)
    ar.region(0, 1984)
    ident = ar.bf16(128)
    m01ea = ar.bf16(384)
    m01ah = ar.bf16(128)
    onescf = ar.f32(128)[0:1, :]
    one11f = ar.f32(2)[0:1, 0:1]
    onecol = ar.f32(1)
    bqp = ar.f32(4)
    bkp = ar.f32(4)
    cw_h = ar.f32(96)
    cb_h = ar.f32(32)
    smal = ar.f32(100)
    s1x1 = ar.f32(64)[0:1, :]
    wsum1 = ar.f32(32)
    wsum2 = ar.f32(32)
    stat6 = ar.f32(12)
    identf = ar.f32(128)
    ceps = ar.f32(1)
    czero = ar.f32(1)
    c1e8 = ar.f32(1)
    csqh = ar.f32(1)
    bv_rep = ar.f32(1024)
    sbcp = bv_rep[:, 0:128]
    pmax = bv_rep[:, 128:256]

    C_MEAN, C_VAR, C_R, C_NMR = 0, 1, 2, 3
    C_MA1, C_SC1, C_MA2, C_SC2 = 4, 5, 6, 7
    C_MAX, C_ISX, C_F, C_S4, C_B4, C_VT2, C_SX = 8, 9, 10, 11, 12, 13, 14
    C_T = 16

    def sc(c):
        return smal[:, c:c + 1]

    B1, L1 = 1984, 11050
    B2, L2 = 11050, 27434
    B3, L3 = 27434, 35754
    B4, L4 = 35754, 52138

    ar.region(B2, L2)
    qfm4 = ar.bf16(4 * SEQ)
    kfm4 = ar.bf16(4 * SEQ)
    qtm4s = ar.bf16(4 * SEQ)
    ktm4s = ar.bf16(4 * SEQ)
    ar.region(B2, L2)
    w1q = ar.bf16(8 * HID)

    ar.region(B3, L3)
    vt = ar.bf16(nt * 1040)
    ar.region(B3, L3)
    wbuf = [ar.f32(1024) for _ in range(3)]
    sgb = [ar.bf16(1024) for _ in range(3)]
    dbb = [ar.bf16(1024) for _ in range(3)]
    ar.region(B3, L3)
    h2 = ar.f32(4096)
    xqs = [ar.bf16(8 * 132) for _ in range(4)]

    ar.region(B4, L4)
    attn = ar.f32(nt * DM)
    ar.region(B4, L4)
    w2q = ar.bf16(NM * DM)
    ar.region(B4, L4)
    wqk_sb = ar.bf16(8 * 512)
    wvt_sb = ar.bf16(8 * DM)
    wstg = [ar.f32(1024), ar.f32(1024)]
    x_t = [ar.f32(1024), ar.f32(1024)]
    xln_tm = [ar.bf16(1024), ar.bf16(1024)]

    ar.region(B1, L1)
    xlnf = ar.bf16(8 * SEQ)
    ar.region(B1, L1)
    phk = [ar.bf16(nt * 280 + 104), ar.bf16(nt * 280 + 104)]
    phqr = [ar.bf16(2 * 384), ar.bf16(2 * 384)]
    phq_tm = [ar.bf16(280), ar.bf16(280)]
    e_sb = [ar.bf16(384) for _ in range(4)]
    s_sb = [ar.f32(195), ar.f32(195)]
    s16 = [ar.bf16(196), ar.bf16(196)]
    tloc = [bv_rep[:, 256:320], bv_rep[:, 320:384]]
    w1stg = [ar.f32(1024), ar.f32(1024)]
    ar.region(B1, L1)
    x_t4 = [ar.f32(1024), ar.f32(1024)]
    ln3b = ar.f32(1024)
    ar.region(B1, L1)
    xm_in = [ar.f32(1024) for _ in range(4)]
    xl2b = ar.f32(1024)
    xqtm = [ar.bf16(1024), ar.bf16(1024)]
    hsb = [ar.f32(130), ar.f32(130)]
    accb = [ar.f32(128), ar.f32(128)]
    hqg = [ar.bf16(1024), ar.bf16(1024)]
    sbc = ar.f32(128)
    srow_full = ar.f32(128)
    outb = ar.f32(1024)

    hqg_i16 = [bass.AP(ar.AI16.tensor, q.offset, [list(q.ap[0])] + [list(d) for d in q.ap[1:]])
               for q in hqg]
    xqi = [bass.AP(ar.AI16.tensor, q.offset, [list(q.ap[0])] + [list(d) for d in q.ap[1:]])
           for q in xqtm]

    es = ExitStack()
    psT = es.enter_context(nc.psum_tensor("psT", [128, 1024], BF16))
    psA = es.enter_context(nc.psum_tensor("psA", [128, 512], F32))
    psB = es.enter_context(nc.psum_tensor("psB", [128, 512], F32))
    psC = es.enter_context(nc.psum_tensor("psC", [128, 1024], F32))
    psD = es.enter_context(nc.psum_tensor("psD", [128, 512], F32))
    psE = es.enter_context(nc.psum_tensor("psE", [128, 512], F32))
    psF = es.enter_context(nc.psum_tensor("psF", [128, 512], F32))
    block = es.enter_context(nc.Block())
    p = Prog(nc, block, strict=strict, strict_dma=strict_dma, strict_eng=strict_eng)

    p.op("dve", lambda e: e.memset(ceps, EPS), writes=("ceps",))
    p.op("dve", lambda e: e.memset(czero, 0.0), writes=("czero",))
    p.op("dve", lambda e: e.memset(c1e8, 1e-8), writes=("c1e8",))
    p.op("dve", lambda e: e.memset(csqh, 2.0 ** -0.5), writes=("csqh",))
    for name, dst in [("ident", ident), ("m01ea", m01ea), ("m01ah", m01ah),
                      ("onescf", onescf), ("one11f", one11f), ("onecol", onecol),
                      ("bqp", bqp), ("bkp", bkp), ("bv_rep", bv_rep),
                      ("cw_h", cw_h), ("cb_h", cb_h), ("identf", identf)]:
        p.dma("sp", "qsp", dst, dram[name], writes=(name,))

    def ln_stats(src, rkeys, pb=0):
        b0 = 24 + 8 * pb
        st = stat6 if pb == 0 else smal[:, 48:60]
        kst = f"stat6_{pb}"
        p.op("dve", lambda e, st=st: e.bn_stats(st[:, 0:6], src[:, 0:512]),
             reads=rkeys, writes=(kst,))
        p.op("dve", lambda e, st=st: e.bn_stats(st[:, 6:12], src[:, 512:1024]),
             reads=rkeys, writes=(kst,))
        p.op("dve", lambda e, st=st, b0=b0: e.bn_aggr(smal[:, b0:b0 + 2], st[:, 0:12]),
             reads=(kst,), writes=(f"c_mv{pb}",))
        p.act_set(AFT.Sqrt)
        p.op("act", lambda e, b0=b0: e.activation(sc(b0 + 2), sc(b0 + 1), AFT.Sqrt,
                                                  bias=ceps, scale=1.0),
             reads=(f"c_mv{pb}", "ceps"), writes=(f"c_r{pb}",))
        p.op("dve", lambda e, b0=b0: e.reciprocal(sc(b0 + 2), sc(b0 + 2)),
             reads=(f"c_r{pb}",), writes=(f"c_r{pb}",))
        p.op("dve", lambda e, b0=b0: e.tensor_scalar(sc(b0 + 3), sc(b0), sc(b0 + 2), -1.0,
                                                     ALU.mult, ALU.mult),
             reads=(f"c_mv{pb}", f"c_r{pb}"), writes=(f"c_nmr{pb}",))
        return b0

    for t in range(nt):
        xb, lb = x_t[t % 2], xln_tm[t % 2]
        kx, kl = f"x{t % 2}", f"xl{t % 2}"
        p.dma("sp", "qsp", xb, dram["x_in"][t * 128:(t + 1) * 128, :], writes=(kx,))
        b0 = ln_stats(xb, (kx,), pb=t % 2)
        p.act_set(AFT.Identity)
        p.op("act", lambda e, xb=xb, lb=lb, b0=b0, t=t: e.activation(lb, xb, AFT.Identity,
             bias=sc(b0 + 3), scale=sc(b0 + 2)),
             reads=(kx, f"c_r{t % 2}", f"c_nmr{t % 2}"), writes=(kl,))
        for dc in range(8):
            sl = psT[:, dc * 128: dc * 128 + 128]
            kp = "bT"
            p.op("pe", lambda e, lb=lb, dc=dc, sl=sl: e.transpose(
                sl, lb[:, dc * 128:(dc + 1) * 128], ident),
                reads=(kl, "ident"), writes=(kp,))
            p.op("dve", lambda e, sl=sl, dc=dc, t=t: e.tensor_copy(
                xlnf[:, dc * SEQ + t * 128: dc * SEQ + (t + 1) * 128], sl),
                reads=(), writes=("xlnf", kp))

    for wname, bt, bkey, dst, dkey in [("wqtp", bqp, "bqp", qfm4, "qfm4"),
                                       ("wktp", bkp, "bkp", kfm4, "kfm4")]:
        for dc in range(8):
            stg, ks = wstg[dc % 2], f"wstg{dc % 2}"
            p.dma("sp", "qsp", stg[:, 0:512], dram[wname][dc * 128:(dc + 1) * 128, :],
                  writes=(ks,))
            p.op("dve", lambda e, stg=stg, dc=dc: e.tensor_copy(
                wqk_sb[:, dc * 512:(dc + 1) * 512], stg[:, 0:512]),
                reads=(ks,), writes=("wqk",))
        for pg in range(4):
            for ng in range(NG):
                ps, kp = (psA, "bA") if ng % 2 == 0 else (psB, "bB")
                for dc in range(8):
                    p.op("pe", lambda e, pg=pg, ng=ng, dc=dc, ps=ps: e.matmul(
                        ps[:, 0:NW],
                        wqk_sb[:, dc * 512 + pg * 128: dc * 512 + (pg + 1) * 128],
                        xlnf[:, dc * SEQ + ng * NW: dc * SEQ + (ng + 1) * NW],
                        start=(dc == 0), stop=(dc == 7)),
                        reads=("wqk", "xlnf"), writes=(kp,))
                p.act_set(AFT.Identity)
                p.op("act", lambda e, pg=pg, ng=ng, ps=ps, bt=bt, dst=dst: e.activation(
                    dst[:, pg * SEQ + ng * NW: pg * SEQ + (ng + 1) * NW], ps[:, 0:NW],
                    AFT.Identity, bias=bt[:, pg:pg + 1], scale=1.0),
                    reads=(bkey,), writes=(dkey, kp))

    for dc in range(8):
        stg, ks = wstg[dc % 2], f"wstg{dc % 2}"
        p.dma("sp", "qsp", stg[:, 0:1024], dram["wvt"][dc * 128:(dc + 1) * 128, :],
              writes=(ks,))
        p.op("dve", lambda e, stg=stg, dc=dc: e.tensor_copy(
            wvt_sb[:, dc * DM:(dc + 1) * DM], stg[:, 0:1024]), reads=(ks,), writes=("wvt",))
    for t in range(nt):
        for dc in range(8):
            for hf in range(2):
                p.op("pe", lambda e, t=t, dc=dc, hf=hf: e.matmul(
                    psC[:, hf * 512:(hf + 1) * 512],
                    xlnf[:, dc * SEQ + t * 128: dc * SEQ + (t + 1) * 128],
                    wvt_sb[:, dc * DM + hf * 512: dc * DM + (hf + 1) * 512],
                    start=(dc == 0), stop=(dc == 7)),
                    reads=("wvt", "xlnf"), writes=("bC0", "bC1"))
        dstv = fdims(vt[:, t * 1040: t * 1040 + 1], [[65, 16], [1, 64]])
        p.op("dve", lambda e, dstv=dstv: e.tensor_tensor(dstv, psC[:], bv_rep, ALU.add),
             reads=("bv_rep",), writes=("vt", "bC0", "bC1"))
        ones_ap = fdims(vt[:, t * 1040 + 64: t * 1040 + 65], [[65, 16]])
        p.op("dve", lambda e, ones_ap=ones_ap: e.memset(ones_ap, 1.0), writes=("vt",))

    for src, skey, dst, dkey in [(qfm4, "qfm4", qtm4s, "qtm4s"), (kfm4, "kfm4", ktm4s, "ktm4s")]:
        for t in range(nt):
            for pg in range(4):
                p.op("pe", lambda e, src=src, pg=pg, t=t: e.transpose(
                    psT[:, pg * 128: (pg + 1) * 128],
                    src[:, pg * SEQ + t * 128: pg * SEQ + (t + 1) * 128], ident),
                    reads=(skey, "ident"), writes=("bT",))
            p.act_set(AFT.Copy)
            p.op("act", lambda e, t=t, dst=dst: e.activation(
                dst[:, t * 512:(t + 1) * 512], psT[:, 0:512],
                AFT.Copy, bias=0.0, scale=SQH),
                reads=(), writes=(dkey, "bT"))

    p.barrier()

    for hh in range(2):
        for t in range(nt):
            p.op("dve", lambda e, hh=hh, t=t: e.memset(
                phk[hh][:, t * 280: t * 280 + 1], 1.0), writes=(f"phk{hh}",))
        p.op("dve", lambda e, hh=hh: e.memset(phk[hh][:, nt * 280: nt * 280 + 104], 0.0),
             writes=(f"phk{hh}",))
        p.op("dve", lambda e, hh=hh: e.memset(phq_tm[hh][:, 0:1], 1.0),
             writes=(f"phq_tm{hh}",))

    def wp_dma(idx):
        if idx < 32:
            r, cb = idx // 4, idx % 4
            src = dram["w1t"][r * 128:(r + 1) * 128, cb * 1024:(cb + 1) * 1024]
        else:
            i = idx - 32
            src = dram["w2t"][i * 128:(i + 1) * 128, :]
        j = idx % 2
        p.dma("act", "qact", w1stg[j], src, writes=(f"w1stg{j}",))

    def wp_comp(idx):
        wcol = wsum1[:, idx:idx + 1] if idx < 32 else wsum2[:, idx - 32:idx - 31]
        j = idx % 2
        p.act_set(AFT.Abs)
        p.op("act", lambda e, j=j: e.activation(w1stg[j], w1stg[j], AFT.Abs, bias=0.0,
                                                scale=1.0),
             reads=(f"w1stg{j}",), writes=(f"w1stg{j}",))
        p.op("dve", lambda e, wcol=wcol, j=j: e.reduce_sum(wcol, w1stg[j], AX.X),
             reads=(f"w1stg{j}",), writes=("wsum",))

    def wpass(idx):
        if idx > 0:
            wp_comp(idx - 1)
        if idx < 64:
            wp_dma(idx)

    wp_iter = iter(range(65))

    for hp in range(H // 2):
        heads = (2 * hp, 2 * hp + 1)
        hinfo = []
        for h in heads:
            pg_, sl_ = h // 4, h % 4
            po = sl_ * 32
            hinfo.append((h, pg_, po, (96, 0) if po == 96 else None))

        for hh, (h, pg_, po, tp) in enumerate(hinfo):
            kph = f"phk{hh}"
            us0 = ktm4s[:, pg_ * 128 + po: pg_ * 128 + po + 16]
            uin = fdims(us0, [[512, nt], [1, 16]])
            uout = fdims(phk[hh][:, 1:2], [[280, nt], [1, 16]])
            p.op("dve", lambda e, uin=uin, uout=uout: e.tensor_single_scalar(
                uout, uin, SQHI, ALU.mult), reads=("ktm4s",), writes=(kph,))
            o_out = fdims(phk[hh][:, 17:18], [[280, nt], [16, 16], [1, 16]])
            in_i = fdims(us0, [[512, nt], [1, 16], [0, 16]])
            in_j = fdims(us0, [[512, nt], [0, 16], [1, 16]])
            p.op("dve", lambda e, o_out=o_out, in_i=in_i, in_j=in_j: e.tensor_tensor(
                o_out, in_i, in_j, ALU.mult), reads=("ktm4s",), writes=(kph,))
            p.op("dve", lambda e, hh=hh: e.memset(s_sb[hh][:, 0:195], 0.0),
                 writes=(f"s_sb{hh}",))

        def gmm(hh, cc, half):
            h, pg_, po, tp = hinfo[hh]
            kq = qfm4[:, pg_ * SEQ + cc * 128: pg_ * SEQ + (cc + 1) * 128][po:po + 16, :]
            kk = kfm4[:, pg_ * SEQ + (cc - 1 + half) * 128: pg_ * SEQ + (cc + half) * 128][po:po + 16, :]
            p.op("pe", lambda e, kk=kk, kq=kq, hh=hh, half=half, tp=tp: e.matmul(
                psA[:, hh * 256 + half * 128: hh * 256 + (half + 1) * 128], kk, kq,
                start=True, stop=True, tile_position=tp), reads=("kfm4", "qfm4"),
                writes=("bA",))

        def psy_of(hh, c):
            if hh == 0:
                return (psD[:, 0:130], "bD") if c % 2 == 0 else (psB[:, 0:130], "bB")
            return (psE[:, 0:130], "bE") if c % 2 == 0 else (psF[:, 0:130], "bF")

        def qbuild(cc):
            for hh, (h, pg_, po, tp) in enumerate(hinfo):
                kpt = f"phq_tm{hh}"
                us = qtm4s[:, cc * 512 + pg_ * 128 + po: cc * 512 + pg_ * 128 + po + 16]
                p.op("dve", lambda e, us=us, hh=hh: e.tensor_single_scalar(
                    phq_tm[hh][:, 1:17], us, SQHI, ALU.mult),
                    reads=("qtm4s",), writes=(kpt,))
                o_out = fdims(phq_tm[hh][:, 17:18], [[16, 16], [1, 16]])
                in_i = fdims(us, [[1, 16], [0, 16]])
                in_j = fdims(us, [[0, 16], [1, 16]])
                p.op("dve", lambda e, o_out=o_out, in_i=in_i, in_j=in_j: e.tensor_tensor(
                    o_out, in_i, in_j, ALU.mult), reads=("qtm4s",), writes=(kpt,))

        def epi_act(c):
            for hh in range(2):
                psy, ky = psy_of(hh, c)
                tl, ktl = tloc[hh], f"tl{hh}"
                ct = C_T + 3 * hh
                p.op("act", lambda e, psy=psy, ct=ct: e.activation(
                    smal[:, ct:ct + 1], psy[:, 129:130], AFT.Identity, bias=c1e8,
                    scale=1.0), reads=("c1e8",), writes=(f"den{hh}", ky))
                p.op("act", lambda e, psy=psy, ct=ct: e.activation(
                    smal[:, ct + 2:ct + 3], psy[:, 64:65], AFT.Identity, bias=czero,
                    scale=1.0), reads=("czero",), writes=(f"dloc{hh}", ky))
                if c + 1 < nt:
                    p.op("act", lambda e, hh=hh: e.activation(
                        s16[hh][:, 0:195], s_sb[hh][:, 0:195], AFT.Copy, bias=0.0,
                        scale=1.0), reads=(f"s_sb{hh}",), writes=(f"s16{hh}",))

        def epi_dve(c):
            for hh in range(2):
                psy, ky = psy_of(hh, c)
                h = heads[hh]
                tl, ktl = tloc[hh], f"tl{hh}"
                ct = C_T + 3 * hh
                p.op("dve", lambda e, ct=ct: e.reciprocal(smal[:, ct + 1:ct + 2],
                                                          smal[:, ct:ct + 1]),
                     reads=(f"den{hh}",), writes=(f"rlin{hh}",))
                p.op("dve", lambda e, ct=ct: e.reciprocal(smal[:, ct + 2:ct + 3],
                                                          smal[:, ct + 2:ct + 3]),
                     reads=(f"dloc{hh}",), writes=(f"dloc{hh}",))
                p.op("dve", lambda e, psy=psy, tl=tl, ct=ct: e.tensor_single_scalar(
                    tl, psy[:, 0:64], smal[:, ct + 2:ct + 3], ALU.mult),
                    reads=(f"dloc{hh}",), writes=(ktl, ky))
                p.op("dve", lambda e, psy=psy, tl=tl, ct=ct, c=c, h=h: e.scalar_tensor_tensor(
                    attn[:, c * DM + h * 64: c * DM + h * 64 + 64], psy[:, 65:129],
                    smal[:, ct + 1:ct + 2], tl, ALU.mult, ALU.add),
                    reads=(f"rlin{hh}", ktl), writes=("attn", ky))

        if nt > 1:
            qbuild(1)
        for c in range(nt):
            lo = 0 if c > 0 else 128
            for hh, (h, pg_, po, tp) in enumerate(hinfo):
                pss = psC[:, 0:195] if hh == 0 else psC[:, 512:707]
                ks_ = f"bC{hh}"
                vcur = vt[:, c * 1040 + h * 65: c * 1040 + h * 65 + 65]
                for k in range(3):
                    p.op("pe", lambda e, k=k, c=c, hh=hh, pss=pss, vcur=vcur: e.matmul(
                        pss[:, k * 65:(k + 1) * 65],
                        phk[hh][:, c * 280 + k * 128: c * 280 + (k + 1) * 128],
                        vcur, start=True, stop=True),
                        reads=(f"phk{hh}", "vt"), writes=(ks_,))
            if c == 0:
                for hh in range(2):
                    gmm(hh, 0, 1)

            p.act_set(AFT.Exp)
            for hh in range(2):
                esb, ke = e_sb[hh * 2 + c % 2], f"esb{hh}_{c % 2}"
                p.op("act", lambda e, hh=hh, esb=esb, lo=lo: e.activation(
                    esb[:, lo:256], psA[:, hh * 256 + lo: hh * 256 + 256], AFT.Exp,
                    bias=0.0, scale=1.0), reads=(), writes=(ke, "bA"))
                p.op("act", lambda e, hh=hh, esb=esb: e.activation(
                    esb[:, 256:384], psA[:, hh * 256 + 128: hh * 256 + 256], AFT.Square,
                    bias=csqh, scale=2.0 ** -0.5), reads=("csqh",), writes=(ke, "bA"))
            if c > 0:
                epi_act(c - 1)

            for hh in range(2):
                esb, ke = e_sb[hh * 2 + c % 2], f"esb{hh}_{c % 2}"
                p.op("dve", lambda e, esb=esb, lo=lo: e.tensor_tensor(
                    esb[:, lo:256], esb[:, lo:256], m01ea[:, lo:256], ALU.mult),
                    reads=("m01ea",), writes=(ke,))
                p.op("dve", lambda e, esb=esb: e.scalar_tensor_tensor(
                    esb[:, 256:384], esb[:, 256:384], 0.5, m01ea[:, 256:384],
                    ALU.add, ALU.mult), reads=("m01ea",), writes=(ke,))

            if c + 1 < nt:
                for hh in range(2):
                    gmm(hh, c + 1, 0)
                    gmm(hh, c + 1, 1)
                for hh in range(2):
                    kpt = f"phq_tm{hh}"
                    for k in range(3):
                        w = 128 if k < 2 else 17
                        p.op("pe", lambda e, k=k, w=w, hh=hh: e.transpose(
                            psT[0:w, hh * 512 + k * 128: hh * 512 + k * 128 + 128],
                            phq_tm[hh][:, k * 128:k * 128 + w], ident),
                            reads=(kpt, "ident"), writes=("bT",))

            for hh, (h, pg_, po, tp) in enumerate(hinfo):
                psy, ky = psy_of(hh, c)
                esb, ke = e_sb[hh * 2 + c % 2], f"esb{hh}_{c % 2}"
                vprev = vt[:, (c - 1) * 1040 + h * 65: (c - 1) * 1040 + h * 65 + 65]
                vcur = vt[:, c * 1040 + h * 65: c * 1040 + h * 65 + 65]
                if c > 0:
                    p.op("pe", lambda e, esb=esb, vprev=vprev, psy=psy: e.matmul(
                        psy[:, 0:65], esb[:, 0:128], vprev, start=True, stop=False),
                        reads=(ke, "vt"), writes=(ky,))
                    p.op("pe", lambda e, esb=esb, vcur=vcur, psy=psy: e.matmul(
                        psy[:, 0:65], esb[:, 128:256], vcur, start=False, stop=True),
                        reads=(ke, "vt"), writes=(ky,))
                else:
                    p.op("pe", lambda e, esb=esb, vcur=vcur, psy=psy: e.matmul(
                        psy[:, 0:65], esb[:, 128:256], vcur, start=True, stop=True),
                        reads=(ke, "vt"), writes=(ky,))
                p.op("pe", lambda e, esb=esb, vcur=vcur, psy=psy, c=c: e.matmul(
                    psy[:, 65:130], esb[:, 256:384], vcur, start=True, stop=(c == 0)),
                    reads=(ke, "vt"), writes=(ky,))
                if c > 0:
                    slot = c % 2
                    kr = f"phqr{hh}_{slot}"
                    for k in range(3):
                        kw = 128 if k < 2 else 17
                        p.op("pe", lambda e, k=k, kw=kw, hh=hh, slot=slot, psy=psy: e.matmul(
                            psy[:, 65:130],
                            phqr[hh][0:kw, slot * 384 + k * 128: slot * 384 + (k + 1) * 128],
                            s16[hh][0:kw, k * 65:(k + 1) * 65], start=False, stop=(k == 2)),
                            reads=(kr, f"s16{hh}"), writes=(ky,))

            for hh in range(2):
                pss = psC[:, 0:195] if hh == 0 else psC[:, 512:707]
                ks_ = f"bC{hh}"
                p.op("dve", lambda e, pss=pss, hh=hh: e.tensor_tensor(
                    s_sb[hh][:, 0:195], s_sb[hh][:, 0:195], pss[:, 0:195], ALU.add),
                    reads=(f"s_sb{hh}",), writes=(f"s_sb{hh}", ks_))
            if c > 0:
                epi_dve(c - 1)
            if c + 1 < nt:
                for hh in range(2):
                    slot = (c + 1) % 2
                    kr = f"phqr{hh}_{slot}"
                    p.op("dve", lambda e, hh=hh, slot=slot: e.tensor_copy(
                        phqr[hh][:, slot * 384: slot * 384 + 256],
                        psT[:, hh * 512: hh * 512 + 256]), reads=(), writes=(kr, "bT"))
                    p.op("dve", lambda e, hh=hh, slot=slot: e.tensor_copy(
                        phqr[hh][0:17, slot * 384 + 256: slot * 384 + 384],
                        psT[0:17, hh * 512 + 256: hh * 512 + 384]),
                        reads=(), writes=(kr, "bT"))
            if c + 2 < nt:
                qbuild(c + 2)

            if (hp * nt + c) % 2 == 0:
                nxt = next(wp_iter, None)
                if nxt is not None:
                    wpass(nxt)
        epi_act(nt - 1)
        epi_dve(nt - 1)

    for nxt in wp_iter:
        wpass(nxt)

    for i, (ws, nel, c_ma, c_sc) in enumerate([(wsum1, DM * HID, C_MA1, C_SC1),
                                               (wsum2, HID * DM, C_MA2, C_SC2)]):
        p.op("pe", lambda e, ws=ws, i=i: e.matmul(psE[0:1, 256 + i * 32:256 + (i + 1) * 32],
                                                  onecol, ws, start=True, stop=True),
             reads=("wsum", "onecol"), writes=("bE",))
        p.op("dve", lambda e, i=i: e.tensor_reduce(
            s1x1[:, 4 * i:4 * i + 1], psE[0:1, 256 + i * 32:256 + (i + 1) * 32], AX.X, ALU.add),
            reads=(), writes=(f"s1a{i}", "bE"))
        p.op("dve", lambda e, i=i, nel=nel: e.tensor_scalar(
            s1x1[:, 4 * i + 1:4 * i + 2], s1x1[:, 4 * i:4 * i + 1], 1.0 / nel, 1e-5,
            ALU.mult, ALU.max), reads=(f"s1a{i}",), writes=(f"s1b{i}",))
        p.op("pe", lambda e, i=i: e.matmul(psE[:, 320 + i:321 + i], onescf,
                                           s1x1[:, 4 * i + 1:4 * i + 2],
                                           start=True, stop=True),
             reads=(f"s1b{i}", "onescf"), writes=("bE",))
        p.op("dve", lambda e, i=i, c_ma=c_ma: e.tensor_copy(sc(c_ma), psE[:, 320 + i:321 + i]),
             reads=(), writes=(f"cma{i}", "bE"))
        p.op("dve", lambda e, c_ma=c_ma, c_sc=c_sc, i=i: e.reciprocal(sc(c_sc), sc(c_ma)),
             reads=(f"cma{i}",), writes=(f"csc{i}",))

    p.barrier()
    if dbg:
        for t in range(nt):
            p.dma("sp", "qsp", dram["dbg_attn"][t * 128:(t + 1) * 128, :],
                  attn[:, t * DM:(t + 1) * DM], reads=("attn",))

    def tern_src(i):
        if i < 32:
            r, cb = i // 4, i % 4
            return dram["w1t"][r * 128:(r + 1) * 128, cb * 1024:(cb + 1) * 1024]
        r = i - 32
        return dram["w2t"][r * 128:(r + 1) * 128, :]

    def tern_dma(i):
        p.dma("act", "qact", wbuf[i % 3], tern_src(i), writes=(f"wbuf{i % 3}",))

    def tern_chunk(i):
        if i < 32:
            dst = w1q[:, (i // 4) * HID + (i % 4) * 1024: (i // 4) * HID + (i % 4 + 1) * 1024]
            kd, scl, kscl = "w1q", sc(C_SC1), "csc0"
        else:
            dst = w2q[:, (i - 32) * DM:(i - 31) * DM]
            kd, scl, kscl = "w2q", sc(C_SC2), "csc1"
        j = i % 3
        wb, sg, db = wbuf[j], sgb[j], dbb[j]
        kw, ks, kb2 = f"wbuf{j}", f"sgb{j}", f"dbb{j}"
        p.act_set(AFT.Sign)
        p.op("act", lambda e, wb=wb, sg=sg: e.activation(sg, wb, AFT.Sign, bias=0.0,
                                                         scale=1.0),
             reads=(kw,), writes=(ks,))
        p.act_set(AFT.Abs)
        p.op("act", lambda e, scl=scl, wb=wb: e.activation(wb, wb, AFT.Abs, bias=0.0,
                                                           scale=scl),
             reads=(kw, kscl), writes=(kw,))
        p.op("dve", lambda e, wb=wb, db=db: e.tensor_single_scalar(db, wb, 0.5, ALU.is_ge),
             reads=(kw,), writes=(kb2,))
        p.op("dve", lambda e, dst=dst, db=db, sg=sg: e.tensor_tensor(dst, db, sg, ALU.mult),
             reads=(kb2, ks), writes=(kd,))
        if i + 3 < 64:
            tern_dma(i + 3)

    tern_iter = iter(range(64))
    tern_dma(0)
    tern_dma(1)
    tern_dma(2)
    for t in range(nt):
        pb = t % 2
        xb, kx = x_t4[pb], f"x4_{pb}"
        p.dma("sp", "qsp", xb, dram["x_in"][t * 128:(t + 1) * 128, :], writes=(kx,))
        asl = attn[:, t * DM:(t + 1) * DM]
        b0 = ln_stats(asl, ("attn",), pb=pb)
        p.act_set(AFT.Identity)
        p.op("act", lambda e, asl=asl, b0=b0: e.activation(ln3b, asl, AFT.Identity,
             bias=sc(b0 + 3), scale=sc(b0 + 2)),
             reads=("attn", f"c_r{pb}", f"c_nmr{pb}"), writes=("ln3b",))
        p.op("dve", lambda e, xb=xb: e.tensor_tensor(xb, ln3b, xb, ALU.add),
             reads=("ln3b", kx), writes=(kx,))
        p.dma("sp", "qsp", xmid_d[t * 128:(t + 1) * 128, :], xb, reads=(kx,),
              writes=(f"xmd{t}",))
        if dbg:
            p.dma("sp", "qsp", dram["dbg_xmid"][t * 128:(t + 1) * 128, :], xb, reads=(kx,))
        for _ in range(4):
            nxt = next(tern_iter, None)
            if nxt is not None:
                tern_chunk(nxt)
    for nxt in tern_iter:
        tern_chunk(nxt)

    p.barrier()

    def ph4b_pre(t):
        sl4 = t % 4
        xm, kxm = xm_in[sl4], f"xm{sl4}"
        pb = t % 2
        xqtb, kqt = xqtm[pb], f"xqtm{pb}"
        p.dma("sp", "qsp", xm, xmid_d[t * 128:(t + 1) * 128, :],
              reads=(f"xmd{t}",), writes=(kxm,))
        st = smal[:, 60 + 12 * pb: 72 + 12 * pb]
        kst = f"st4_{pb}"
        b0 = 84 + 4 * pb
        p.op("dve", lambda e, st=st, xm=xm: e.bn_stats(st[:, 0:6], xm[:, 0:512]),
             reads=(kxm,), writes=(kst,))
        p.op("dve", lambda e, st=st, xm=xm: e.bn_stats(st[:, 6:12], xm[:, 512:1024]),
             reads=(kxm,), writes=(kst,))
        p.op("dve", lambda e, st=st, b0=b0: e.bn_aggr(smal[:, b0:b0 + 2], st[:, 0:12]),
             reads=(kst,), writes=(f"mv4{pb}",))
        p.act_set(AFT.Sqrt)
        p.op("act", lambda e, b0=b0: e.activation(sc(b0 + 2), sc(b0 + 1), AFT.Sqrt,
                                                  bias=ceps, scale=1.0),
             reads=(f"mv4{pb}", "ceps"), writes=(f"r4{pb}",))
        p.op("dve", lambda e, b0=b0: e.reciprocal(sc(b0 + 2), sc(b0 + 2)),
             reads=(f"r4{pb}",), writes=(f"r4{pb}",))
        p.op("dve", lambda e, b0=b0: e.tensor_scalar(sc(b0 + 3), sc(b0), sc(b0 + 2), -1.0,
                                                     ALU.mult, ALU.mult),
             reads=(f"mv4{pb}", f"r4{pb}"), writes=(f"n4{pb}",))
        p.act_set(AFT.Identity)
        p.op("act", lambda e, xm=xm, b0=b0: e.activation(xl2b, xm, AFT.Identity,
             bias=sc(b0 + 3), scale=sc(b0 + 2)),
             reads=(kxm, f"r4{pb}", f"n4{pb}"), writes=("xl2b",))
        cq = 92 + 3 * pb
        p.op("dve", lambda e, cq=cq: e.reduce_max(sc(cq), xl2b, AX.X,
                                                  apply_absolute_value=True),
             reads=("xl2b",), writes=(f"cmax4_{pb}",))
        p.op("dve", lambda e, cq=cq: e.tensor_single_scalar(sc(cq), sc(cq), 1e-5, ALU.max),
             reads=(f"cmax4_{pb}",), writes=(f"cmax4_{pb}",))
        p.op("dve", lambda e, cq=cq: e.tensor_scalar(sc(cq + 1), sc(cq), sc(C_MA1),
                                                     1.0 / 127.0, ALU.mult, ALU.mult),
             reads=(f"cmax4_{pb}", "cma0"), writes=(f"cisx4_{pb}",))
        p.op("dve", lambda e, cq=cq: e.reciprocal(sc(cq + 2), sc(cq)),
             reads=(f"cmax4_{pb}",), writes=(f"csx4_{pb}",))
        p.op("dve", lambda e, cq=cq: e.tensor_single_scalar(sc(cq + 2), sc(cq + 2), 127.0,
                                                            ALU.mult),
             reads=(f"csx4_{pb}",), writes=(f"csx4_{pb}",))
        p.op("dve", lambda e, cq=cq, pb=pb: e.tensor_scalar(
            xqi[pb], xl2b, sc(cq + 2), 1000.5, ALU.mult, ALU.add),
            reads=("xl2b", f"csx4_{pb}"), writes=(kqt,))
        p.op("dve", lambda e, cq=cq, pb=pb, xqtb=xqtb: e.tensor_scalar(
            xqtb, xqi[pb], 1000.0, sc(cq + 1), ALU.subtract, ALU.mult),
            reads=(kqt, f"cisx4_{pb}"), writes=(kqt,))
    def ph4b_tr(t):
        sl4 = t % 4
        pb = t % 2
        xqtb, kqt = xqtm[pb], f"xqtm{pb}"
        xs, kxs = xqs[sl4], f"xqs{sl4}"
        for dc in range(8):
            p.op("pe", lambda e, dc=dc, xqtb=xqtb: e.transpose(
                psT[:, dc * 128: (dc + 1) * 128],
                xqtb[:, dc * 128:(dc + 1) * 128], ident),
                reads=(kqt, "ident"), writes=("bT",))
        d1 = fdims(xs[:, 1:2], [[132, 8], [1, 128]])
        p.op("dve", lambda e, d1=d1: e.tensor_copy(d1, psT[:, 0:1024]),
             reads=(), writes=(kxs, "bT"))
        if t == 0:
            z0 = fdims(xs[:, 0:1], [[132, 8]])
            p.op("dve", lambda e, z0=z0: e.memset(z0, 0.0), writes=(kxs,))
        else:
            slp = (t - 1) % 4
            dsth = fdims(xqs[slp][:, 129:130], [[132, 8]])
            srch = fdims(xs[:, 1:2], [[132, 8]])
            p.op("dve", lambda e, dsth=dsth, srch=srch: e.tensor_copy(dsth, srch),
                 reads=(kxs,), writes=(f"xqs{slp}",))
            dstl = fdims(xs[:, 0:1], [[132, 8]])
            srcl = fdims(xqs[slp][:, 128:129], [[132, 8]])
            p.op("dve", lambda e, dstl=dstl, srcl=srcl: e.tensor_copy(dstl, srcl),
                 reads=(f"xqs{slp}",), writes=(kxs,))
        if t == nt - 1:
            z1 = fdims(xs[:, 129:130], [[132, 8]])
            p.op("dve", lambda e, z1=z1: e.memset(z1, 0.0), writes=(kxs,))
        if dbg:
            dsrc = fdims(xs[:, 1:2], [[132, 8], [1, 128]])
            ddst = fdims(dram["dbg_xqf"][:, t * 128: t * 128 + 1], [[SEQ, 8], [1, 128]])
            p.dma("sp", "qsp", ddst, dsrc, reads=(kxs,))

    def quant_group(t, g):
        hqb, hqi, khq = hqg[g % 2], hqg_i16[g % 2], f"hqg{g % 2}"
        h2g = h2[:, g * 1024:(g + 1) * 1024]
        kh2 = f"h2g{g}"
        srep = fdims(sbc[:, 0:1], [[0, 8], [1, 128]])
        p.op("dve", lambda e, h2g=h2g, srep=srep: e.tensor_tensor(h2g, h2g, srep, ALU.mult),
             reads=("sbc", kh2), writes=(kh2,))
        p.op("dve", lambda e, h2g=h2g, hqi=hqi: e.tensor_single_scalar(hqi, h2g, 1000.5,
                                                                       ALU.add),
             reads=(kh2,), writes=(khq,))
        p.op("dve", lambda e, hqb=hqb, hqi=hqi: e.tensor_single_scalar(hqb, hqi, 1000.0,
                                                                       ALU.subtract),
             reads=(khq,), writes=(khq,))
        if dbg and t == 0:
            p.dma("sp", "qsp", dram["dbg_h2"][:, g * 1024:(g + 1) * 1024], h2g, reads=(kh2,))
            p.dma("sp", "qsp", dram["dbg_hq"][:, g * 1024:(g + 1) * 1024], hqb, reads=(khq,))

    def w2_group(g):
        hqb, khq = hqg[g % 2], f"hqg{g % 2}"
        for mj in range(8):
            m = g * 8 + mj
            for hf in range(2):
                p.op("pe", lambda e, m=m, mj=mj, hf=hf, hqb=hqb: e.matmul(
                    psC[:, hf * 512:(hf + 1) * 512], hqb[:, mj * 128:(mj + 1) * 128],
                    w2q[:, m * DM + hf * 512: m * DM + (hf + 1) * 512],
                    start=(m == 0), stop=(m == NM - 1)),
                    reads=(khq, "w2q"), writes=("bC0", "bC1"))

    def stage1(t, fill=None):
        xs, kxs = xqs[t % 4], f"xqs{t % 4}"
        w1banks = [(psA, "bA"), (psB, "bB"), (psE, "bE"), (psF, "bF")]
        for m in range(NM):
            psh, kh = w1banks[m % 4]
            hb, kb = hsb[m % 2], f"hsb{m % 2}"
            acb, kc = accb[m % 2], f"acc{m % 2}"
            for dc in range(8):
                p.op("pe", lambda e, m=m, psh=psh, xs=xs, dc=dc: e.matmul(
                    psh[:, 0:130],
                    w1q[:, dc * HID + m * 128: dc * HID + (m + 1) * 128],
                    xs[:, dc * 132: dc * 132 + 130], start=(dc == 0), stop=(dc == 7)),
                    reads=("w1q", kxs), writes=(kh,))
            p.act_set(AFT.Copy)
            p.op("act", lambda e, psh=psh, hb=hb: e.activation(
                hb, psh[:, 0:130], AFT.Copy, bias=0.0, scale=1.0),
                reads=(), writes=(kb, kh))
            p.op("dve", lambda e, hb=hb, acb=acb, m=m: e.tensor_single_scalar(
                acb, hb[:, 0:128], cw_h[:, m:m + 1], ALU.mult),
                reads=(kb, "cw_h"), writes=(kc,))
            p.op("dve", lambda e, hb=hb, acb=acb, m=m: e.scalar_tensor_tensor(
                acb, hb[:, 1:129], cw_h[:, 32 + m:33 + m], acb, ALU.mult, ALU.add),
                reads=(kb, kc, "cw_h"), writes=(kc,))
            p.op("dve", lambda e, hb=hb, acb=acb, m=m: e.scalar_tensor_tensor(
                acb, hb[:, 2:130], cw_h[:, 64 + m:65 + m], acb, ALU.mult, ALU.add),
                reads=(kb, kc, "cw_h"), writes=(kc,))
            p.act_set(AFT.Silu)
            p.op("act", lambda e, acb=acb, m=m: e.activation(
                h2[:, m * 128:(m + 1) * 128], acb, AFT.Silu, bias=cb_h[:, m:m + 1], scale=1.0),
                reads=(kc, "cb_h"), writes=(f"h2g{m // 8}",))
            if fill is not None and m in (7, 15, 23, 31):
                fill(m // 8)

    def h2_partial(t, b):
        rin = fdims(h2[:, b * 1024: b * 1024 + 1], [[1, 128], [128, 8]])
        dst = pmax if b == 0 else sbcp
        kdst = "pmax" if b == 0 else "sbcp"
        p.op("dve", lambda e, rin=rin, dst=dst: e.tensor_reduce(
            dst, rin, AX.X, ALU.max, apply_absolute_value=True),
            reads=(f"h2g{b}",), writes=(kdst,))
        if b > 0:
            p.op("dve", lambda e: e.tensor_tensor(pmax, pmax, sbcp, ALU.max),
                 reads=("pmax", "sbcp"), writes=("pmax",))

    def qdance(t):
        p.op("pe", lambda e: e.transpose(psD[:, 0:128], pmax, identf),
             reads=("pmax", "identf"), writes=("bD",))
        p.op("dve", lambda e: e.reduce_max(sc(C_MAX), psD[:, 0:128], AX.X),
             reads=(), writes=("cmax5", "bD"))
        p.op("dve", lambda e: e.tensor_single_scalar(sc(C_MAX), sc(C_MAX), 1e-5, ALU.max),
             reads=("cmax5",), writes=("cmax5",))
        p.op("dve", lambda e: e.tensor_scalar(sc(C_F), sc(C_MAX), sc(C_MA2), 1.0 / 127.0,
                                              ALU.mult, ALU.mult),
             reads=("cmax5", "cma1"), writes=("cf",))
        p.op("dve", lambda e: e.reciprocal(sc(C_SX), sc(C_MAX)),
             reads=("cmax5",), writes=("csx5",))
        p.op("dve", lambda e: e.tensor_single_scalar(sc(C_SX), sc(C_SX), 127.0, ALU.mult),
             reads=("csx5",), writes=("csx5",))
        p.op("pe", lambda e: e.transpose(psD[0:1, 128:256], sc(C_SX), identf),
             reads=("csx5", "identf"), writes=("bD",))
        p.op("dve", lambda e: e.tensor_copy(srow_full[0:1, :], psD[0:1, 128:256]),
             reads=(), writes=("srow", "bD"))
        p.op("pe", lambda e: e.matmul(psD[:, 256:384], onescf, srow_full[0:1, :],
                                      start=True, stop=True),
             reads=("srow", "onescf"), writes=("bD",))
        p.op("dve", lambda e: e.tensor_copy(sbc, psD[:, 256:384]),
             reads=(), writes=("sbc", "bD"))
        quant_group(t, 0)
        quant_group(t, 1)

    def mk_fill(t):
        def fill(b):
            h2_partial(t, b)
            if t > 0:
                if b == 1:
                    w2_group(0)
                    quant_group(t - 1, 2)
                elif b == 2:
                    w2_group(1)
                    quant_group(t - 1, 3)
                elif b == 3:
                    w2_group(2)
                    w2_group(3)
        return fill

    def stage2b(t):
        p.op("dve", lambda e: e.bn_stats(stat6[:, 0:6], psC[:, 0:512]),
             reads=(), writes=("stat6", "bC0"))
        p.op("dve", lambda e: e.bn_stats(stat6[:, 6:12], psC[:, 512:1024]),
             reads=(), writes=("stat6", "bC1"))
        p.op("dve", lambda e: e.bn_aggr(smal[:, C_MEAN:C_VAR + 1], stat6[:]),
             reads=("stat6",), writes=("c_mv",))
        p.op("dve", lambda e: e.tensor_scalar(sc(C_VT2), sc(C_VAR), sc(C_F), sc(C_F),
                                              ALU.mult, ALU.mult),
             reads=("c_mv", "cf"), writes=("cvt2",))
        p.act_set(AFT.Sqrt)
        p.op("act", lambda e: e.activation(sc(C_R), sc(C_VT2), AFT.Sqrt, bias=ceps, scale=1.0),
             reads=("cvt2", "ceps"), writes=("c_r",))
        p.op("dve", lambda e: e.reciprocal(sc(C_R), sc(C_R)), reads=("c_r",), writes=("c_r",))
        p.op("dve", lambda e: e.tensor_scalar(sc(C_S4), sc(C_R), sc(C_F), None, ALU.mult),
             reads=("c_r", "cf"), writes=("cs4",))
        p.op("dve", lambda e: e.tensor_scalar(sc(C_B4), sc(C_MEAN), sc(C_S4), -1.0,
                                              ALU.mult, ALU.mult),
             reads=("c_mv", "cs4"), writes=("cb4",))
        p.act_set(AFT.Identity)
        p.op("act", lambda e: e.activation(outb, psC[:], AFT.Identity,
                                           bias=sc(C_B4), scale=sc(C_S4)),
             reads=("cs4", "cb4"), writes=("outb", "bC0", "bC1"))
        p.op("dve", lambda e, t=t: e.tensor_tensor(outb, outb, xm_in[t % 4], ALU.add),
             reads=("outb", f"xm{t % 4}"), writes=("outb",))
        p.dma("sp", "qsp", dram["out"][t * 128:(t + 1) * 128, :], outb, reads=("outb",))

    ph4b_pre(0)
    if nt > 1:
        ph4b_pre(1)
    ph4b_tr(0)
    if nt > 1:
        if nt > 2:
            ph4b_pre(2)
        ph4b_tr(1)
    for t in range(nt + 1):
        if t < nt:
            stage1(t, fill=mk_fill(t))
        else:
            w2_group(0)
            quant_group(t - 1, 2)
            w2_group(1)
            quant_group(t - 1, 3)
            w2_group(2)
            w2_group(3)
        if 0 < t:
            stage2b(t - 1)
        if t + 2 < nt:
            ph4b_tr(t + 2)
        if t < nt:
            qdance(t)
        if t + 3 < nt:
            ph4b_pre(t + 3)

    p.finalize()
    return nc, es


_BUILD_CACHE = {}


def _get_program(nt, dbg=False):
    key = (nt, dbg)
    if key not in _BUILD_CACHE:
        _BUILD_CACHE[key] = build_program(nt, dbg, strict_eng="dve")
    return _BUILD_CACHE[key]


def host_inputs(x_core, Wq, bq, Wk, bk, Wv, bv, w1, conv_w, conv_b, w2, nt=16):
    f32 = np.float32
    bf = ml_dtypes.bfloat16
    d = {}
    d["x_in"] = np.ascontiguousarray(x_core, dtype=f32)
    wqt = np.zeros((DM, 512), f32)
    wkt = np.zeros((DM, 512), f32)
    bqp = np.zeros((128, 4), f32)
    bkp = np.zeros((128, 4), f32)
    for h in range(H):
        pg, sl = h // 4, h % 4
        wqt[:, pg * 128 + sl * 32: pg * 128 + sl * 32 + 16] = Wq[h * 16:(h + 1) * 16, :].T
        wkt[:, pg * 128 + sl * 32: pg * 128 + sl * 32 + 16] = Wk[h * 16:(h + 1) * 16, :].T
        bqp[sl * 32: sl * 32 + 16, pg] = bq[h * 16:(h + 1) * 16]
        bkp[sl * 32: sl * 32 + 16, pg] = bk[h * 16:(h + 1) * 16]
    d["wqtp"] = wqt
    d["wktp"] = wkt
    d["bqp"] = bqp
    d["bkp"] = bkp
    d["wvt"] = np.ascontiguousarray(Wv.T, dtype=f32)
    d["bv_rep"] = np.ascontiguousarray(np.broadcast_to(bv, (128, DM)), dtype=f32)
    d["w1t"] = np.ascontiguousarray(w1.T, dtype=f32)
    d["w2t"] = np.ascontiguousarray(w2.T, dtype=f32)
    cw = np.zeros((128, 96), f32)
    for k in range(3):
        cw[:, k * 32:(k + 1) * 32] = conv_w[:, 0, k].reshape(NM, 128).T
    d["cw_h"] = cw
    d["cb_h"] = np.ascontiguousarray(conv_b.reshape(NM, 128).T, dtype=f32)
    d["ident"] = np.eye(128, dtype=bf)
    pp, qq = np.arange(128)[:, None], np.arange(128)[None, :]
    m_prev = (pp - qq >= 65).astype(bf)
    dqp = qq - pp
    m_diag = ((dqp >= 0) & (dqp <= 63)).astype(bf)
    m01a = (qq >= pp).astype(np.float32)
    d["m01ea"] = np.ascontiguousarray(
        np.concatenate([m_prev.astype(np.float32), m_diag.astype(np.float32), m01a], axis=1)).astype(bf)
    d["m01ah"] = (0.5 * m01a).astype(bf)
    d["onescf"] = np.ones((1, 128), f32)
    d["identf"] = np.eye(128, dtype=f32)
    d["one11f"] = np.ones((1, 1), f32)
    d["onecol"] = np.ones((128, 1), f32)
    return d


def kernel(L0_attn_x, L0_attn_Wq, L0_attn_bq, L0_attn_Wk, L0_attn_bk, L0_attn_Wv,
           L0_attn_bv, L0_ffn_w1, L0_ffn_conv_w, L0_ffn_conv_b, L0_ffn_w2):
    x = np.asarray(L0_attn_x)
    nt = x.shape[1] // 128
    nc, _es = _get_program(nt)
    in_maps = []
    for c in range(x.shape[0]):
        in_maps.append(host_inputs(
            x[c], np.asarray(L0_attn_Wq), np.asarray(L0_attn_bq), np.asarray(L0_attn_Wk),
            np.asarray(L0_attn_bk), np.asarray(L0_attn_Wv), np.asarray(L0_attn_bv),
            np.asarray(L0_ffn_w1), np.asarray(L0_ffn_conv_w), np.asarray(L0_ffn_conv_b),
            np.asarray(L0_ffn_w2), nt=nt))
    res = run_bass_kernel_spmd(nc, in_maps, core_ids=list(range(x.shape[0])))
    return np.stack([np.asarray(r["out"]) for r in res.results], axis=0)


# version 21
# speedup vs baseline: 1.0250x; 1.0228x over previous
import os
import numpy as np
import ml_dtypes
from contextlib import ExitStack

import concourse.bass as bass
import concourse.mybir as mybir
from concourse.alu_op_type import AluOpType as ALU
from concourse.bass_utils import run_bass_kernel_spmd

F32 = mybir.dt.float32
BF16 = mybir.dt.bfloat16
I16 = mybir.dt.int16
AFT = mybir.ActivationFunctionType
AX = mybir.AxisListType

H, DP, DH, DT = 16, 16, 64, 273
DM, HID = 1024, 4096
NM = HID // 128
SQH = 2.0 ** (-0.25)
SQHI = 2.0 ** (0.25)
EPS = 1e-5
NCOL = 52400

ENGS = ("pe", "dve", "act", "pool", "sp")
QUEUES = ("qsp", "qact", "qpool")


def fdims(ap, dims, extra_off=0):
    return bass.AP(ap.tensor, ap.offset + extra_off,
                   [list(ap.ap[0])] + [list(d) for d in dims])


class Prog:
    RSLOT = 8

    def __init__(self, nc, block, strict=False, strict_dma=False, strict_eng=""):
        self.strict = strict
        self.strict_dma = strict_dma
        self.strict_eng = strict_eng
        self.sems = {}
        self.qsems = {}
        self._ctx = ExitStack()
        for name in ENGS:
            self.sems[name] = self._ctx.enter_context(nc.semaphore("sem_" + name))
        for q in QUEUES:
            self.qsems[q] = [self._ctx.enter_context(nc.semaphore(f"sem_{q}_{i}"))
                             for i in range(self.RSLOT)]
        self.cnt = {name: 0 for name in ENGS}
        self.qcnt = {q: 0 for q in QUEUES}
        self.last_w = {}
        self.readers = {}
        self.hw = {}
        self.secs = {"pe": block.tensor, "dve": block.vector, "act": block.scalar,
                     "pool": block.gpsimd, "sp": block.sync}
        self.nops = 0
        self.maxops = int(os.environ.get("KMAXOPS", "0"))
        self.act_cur = None
        from concourse.hw_specs import get_activation_tables
        self._act_tables = list(get_activation_tables(nc.m.arch).values())

    def act_set(self, func):
        for idx, funcs in enumerate(self._act_tables):
            if func in funcs:
                break
        else:
            raise ValueError(func)
        if self.act_cur is not None and func in self._act_tables[self.act_cur]:
            return
        self.act_cur = idx
        self.op("act", lambda e: e.preload_act_table(func))

    def _qwait(self, q, idx):
        slot = (idx - 1) % self.RSLOT
        uses = (idx - 1) // self.RSLOT + 1
        return self.qsems[q][slot], 16 * uses, (q, slot)

    def _edge_wait(self, eng, dom, val, waits):
        if dom in QUEUES:
            sem, tgt, hwkey = self._qwait(dom, val)
        else:
            sem, tgt, hwkey = self.sems[dom], val, dom
        k = (eng, hwkey)
        if self.hw.get(k, 0) < tgt:
            self.hw[k] = tgt
            waits.append((sem, tgt))

    def op(self, eng, fn, reads=(), writes=(), dom=None):
        if self.maxops and self.nops >= self.maxops:
            return 0
        dom = dom or eng
        is_dma = dom in QUEUES

        def keep(d):
            if d != dom:
                return True
            if self.strict:
                return True
            if d in QUEUES:
                return self.strict_dma
            return d in self.strict_eng

        edges = []
        for k in reads:
            lw = self.last_w.get(k)
            if lw is not None and keep(lw[0]):
                edges.append(lw)
        for k in writes:
            lw = self.last_w.get(k)
            if lw is not None and keep(lw[0]):
                edges.append(lw)
            for rdom, rval in self.readers.get(k, {}).items():
                if keep(rdom):
                    edges.append((rdom, rval))
        waits = []
        for d, v in edges:
            self._edge_wait(eng, d, v, waits)
        if is_dma:
            self.qcnt[dom] += 1
            val = self.qcnt[dom]
            sem, tgt, hwkey = self._qwait(dom, val)
            k = (eng, hwkey)
            if tgt - 16 > 0 and self.hw.get(k, 0) < tgt - 16:
                self.hw[k] = tgt - 16
                waits.append((sem, tgt - 16))
            inc_sem, inc_amt = sem, 16
        else:
            self.cnt[dom] += 1
            val = self.cnt[dom]
            inc_sem, inc_amt = self.sems[dom], 1
            if self.strict and val > 1:
                waits.append((inc_sem, val - 1))

        def body(e):
            for sm, v in waits:
                e.wait_ge(sm, v)
            fn(e).then_inc(inc_sem, inc_amt)

        self.secs[eng](body)
        self.nops += 1
        for k in reads:
            self.readers.setdefault(k, {})[dom] = val
        for k in writes:
            self.last_w[k] = (dom, val)
            self.readers[k] = {}
        return val

    def dma(self, eng, queue, out, in_, reads=(), writes=()):
        return self.op(eng, lambda e: e.dma_start(out=out, in_=in_),
                       reads=reads, writes=writes, dom=queue)

    def _all_final(self):
        fin = []
        for name in ENGS:
            if self.cnt[name] > 0:
                fin.append((name, self.sems[name], self.cnt[name], name))
        for q in QUEUES:
            n = self.qcnt[q]
            for slot in range(self.RSLOT):
                uses = (n - slot + self.RSLOT - 1) // self.RSLOT if n > slot else 0
                uses = len(range(slot, n, self.RSLOT))
                if uses > 0:
                    fin.append(((q, slot), self.qsems[q][slot], 16 * uses, (q, slot)))
        return fin

    def barrier(self):
        fin = self._all_final()
        for eng in ENGS:
            waits = []
            for hwkey, sem, v, _ in fin:
                if hwkey == eng:
                    continue
                if self.hw.get((eng, hwkey), 0) < v:
                    self.hw[(eng, hwkey)] = v
                    waits.append((sem, v))
            def body(e, waits=waits):
                for sm, v in waits:
                    e.wait_ge(sm, v)

            self.secs[eng](body)

    def finalize(self):
        fin = self._all_final()

        def body(e):
            for hwkey, sem, v, _ in fin:
                if hwkey == "sp":
                    continue
                e.wait_ge(sem, v)

        self.secs["sp"](body)


class Arena:
    def __init__(self, nc):
        self.h = nc.alloc_sbuf_tensor("arena", [128, NCOL], F32)
        self.A32 = self.h[:, :]
        self.A16 = self.h[:, :].bitcast(BF16)
        self.AI16 = self.h[:, :].bitcast(I16)
        self.cur = 0
        self.limit = NCOL

    def region(self, base, limit):
        self.cur = base
        self.limit = limit

    def f32(self, n):
        ap = self.A32[:, self.cur:self.cur + n]
        self.cur += n
        assert self.cur <= self.limit, f"arena overflow {self.cur} > {self.limit}"
        return ap

    def bf16(self, n16):
        assert n16 % 2 == 0
        ap = self.A16[:, 2 * self.cur: 2 * self.cur + n16]
        self.cur += n16 // 2
        assert self.cur <= self.limit, f"arena overflow {self.cur} > {self.limit}"
        return ap

    def i16(self, n16):
        assert n16 % 2 == 0
        ap = self.AI16[:, 2 * self.cur: 2 * self.cur + n16]
        self.cur += n16 // 2
        assert self.cur <= self.limit, f"arena overflow {self.cur} > {self.limit}"
        return ap


def build_program(nt=16, dbg=False, strict=False, detect=False, strict_dma=False, strict_eng=""):
    SEQ = nt * 128
    NW = min(512, SEQ)
    NG = SEQ // NW
    nc = bass.Bass("TRN2", target_bir_lowering=False, debug=False,
                   detect_race_conditions=detect)

    dram = {}
    for name, shape, dt in [
        ("x_in", [SEQ, DM], F32), ("wqtp", [DM, 512], F32), ("wktp", [DM, 512], F32),
        ("wvt", [DM, DM], F32), ("bqp", [128, 4], F32), ("bkp", [128, 4], F32),
        ("bv_rep", [128, DM], F32), ("w1t", [DM, HID], F32), ("w2t", [HID, DM], F32),
        ("cw_h", [128, 96], F32), ("cb_h", [128, 32], F32), ("ident", [128, 128], BF16),
        ("m01ea", [128, 384], BF16), ("m01ah", [128, 128], BF16),
        ("onescf", [1, 128], F32), ("one11f", [1, 1], F32), ("onecol", [128, 1], F32),
        ("identf", [128, 128], F32),
    ]:
        dram[name] = nc.dram_tensor(name, shape, dt, kind="ExternalInput").ap()
    dram["out"] = nc.dram_tensor("out", [SEQ, DM], F32, kind="ExternalOutput").ap()
    xmid_d = nc.dram_tensor("xmid_sc", [SEQ, DM], F32, kind="Internal").ap()
    if dbg:
        for name, shape, dt in [("dbg_attn", [SEQ, DM], F32), ("dbg_xmid", [SEQ, DM], F32),
                                ("dbg_xqf", [128, 8 * SEQ], BF16),
                                ("dbg_h2", [128, HID], F32), ("dbg_hq", [128, HID], BF16)]:
            dram[name] = nc.dram_tensor(name, shape, dt, kind="ExternalOutput").ap()

    ar = Arena(nc)
    ar.region(0, 1984)
    ident = ar.bf16(128)
    m01ea = ar.bf16(384)
    m01ah = ar.bf16(128)
    onescf = ar.f32(128)[0:1, :]
    one11f = ar.f32(2)[0:1, 0:1]
    onecol = ar.f32(1)
    bqp = ar.f32(4)
    bkp = ar.f32(4)
    cw_h = ar.f32(96)
    cb_h = ar.f32(32)
    smal = ar.f32(100)
    s1x1 = ar.f32(64)[0:1, :]
    wsum1 = ar.f32(32)
    wsum2 = ar.f32(32)
    stat6 = ar.f32(12)
    identf = ar.f32(128)
    ceps = ar.f32(1)
    czero = ar.f32(1)
    c1e8 = ar.f32(1)
    csqh = ar.f32(1)
    bv_rep = ar.f32(1024)
    sbcp = bv_rep[:, 0:128]
    pmax = bv_rep[:, 128:256]

    C_MEAN, C_VAR, C_R, C_NMR = 0, 1, 2, 3
    C_MA1, C_SC1, C_MA2, C_SC2 = 4, 5, 6, 7
    C_MAX, C_ISX, C_F, C_S4, C_B4, C_VT2, C_SX = 8, 9, 10, 11, 12, 13, 14
    C_T = 16

    def sc(c):
        return smal[:, c:c + 1]

    B1, L1 = 1984, 11050
    B2, L2 = 11050, 27434
    B3, L3 = 27434, 35754
    B4, L4 = 35754, 52138

    ar.region(B2, L2)
    qfm4 = ar.bf16(4 * SEQ)
    kfm4 = ar.bf16(4 * SEQ)
    qtm4s = ar.bf16(4 * SEQ)
    ktm4s = ar.bf16(4 * SEQ)
    ar.region(B2, L2)
    w1q = ar.bf16(8 * HID)

    ar.region(B3, L3)
    vt = ar.bf16(nt * 1040)
    ar.region(B3, L3)
    wbuf = [ar.f32(1024) for _ in range(3)]
    sgb = [ar.bf16(1024) for _ in range(3)]
    dbb = [ar.bf16(1024) for _ in range(3)]
    ar.region(B3, L3)
    h2 = ar.f32(4096)
    xqs = [ar.bf16(8 * 132) for _ in range(4)]

    ar.region(B4, L4)
    attn = ar.f32(nt * DM)
    ar.region(B4, L4)
    w2q = ar.bf16(NM * DM)
    ar.region(B4, L4)
    wqk_sb = ar.bf16(8 * 512)
    wvt_sb = ar.bf16(8 * DM)
    wstg = [ar.f32(1024), ar.f32(1024)]
    x_t = [ar.f32(1024), ar.f32(1024)]
    xln_tm = [ar.bf16(1024), ar.bf16(1024)]

    ar.region(B1, L1)
    xlnf = ar.bf16(8 * SEQ)
    ar.region(B1, L1)
    phk = [ar.bf16(nt * 280 + 104), ar.bf16(nt * 280 + 104)]
    phqr = [ar.bf16(2 * 384), ar.bf16(2 * 384)]
    phq_tm = [ar.bf16(280), ar.bf16(280)]
    e_sb = [ar.bf16(384) for _ in range(4)]
    s_sb = [ar.f32(195), ar.f32(195)]
    s16 = [ar.bf16(196), ar.bf16(196)]
    tloc = [bv_rep[:, 256:320], bv_rep[:, 320:384]]
    w1stg = [ar.f32(1024), ar.f32(1024)]
    ar.region(B1, L1)
    x_t4 = [ar.f32(1024), ar.f32(1024)]
    ln3b = ar.f32(1024)
    ar.region(B1, L1)
    xm_in = [ar.f32(1024) for _ in range(4)]
    xl2b = ar.f32(1024)
    xqtm = [ar.bf16(1024), ar.bf16(1024)]
    hsb = [ar.f32(130), ar.f32(130)]
    accb = [ar.f32(128), ar.f32(128)]
    hqg = [ar.bf16(1024), ar.bf16(1024)]
    sbc = ar.f32(128)
    srow_full = ar.f32(128)
    outb = ar.f32(1024)

    hqg_i16 = [bass.AP(ar.AI16.tensor, q.offset, [list(q.ap[0])] + [list(d) for d in q.ap[1:]])
               for q in hqg]
    xqi = [bass.AP(ar.AI16.tensor, q.offset, [list(q.ap[0])] + [list(d) for d in q.ap[1:]])
           for q in xqtm]

    es = ExitStack()
    psT = es.enter_context(nc.psum_tensor("psT", [128, 1024], BF16))
    psA = es.enter_context(nc.psum_tensor("psA", [128, 512], F32))
    psB = es.enter_context(nc.psum_tensor("psB", [128, 512], F32))
    psC = es.enter_context(nc.psum_tensor("psC", [128, 1024], F32))
    psD = es.enter_context(nc.psum_tensor("psD", [128, 512], F32))
    psE = es.enter_context(nc.psum_tensor("psE", [128, 512], F32))
    psF = es.enter_context(nc.psum_tensor("psF", [128, 512], F32))
    block = es.enter_context(nc.Block())
    p = Prog(nc, block, strict=strict, strict_dma=strict_dma, strict_eng=strict_eng)

    p.op("dve", lambda e: e.memset(ceps, EPS), writes=("ceps",))
    p.op("dve", lambda e: e.memset(czero, 0.0), writes=("czero",))
    p.op("dve", lambda e: e.memset(c1e8, 1e-8), writes=("c1e8",))
    p.op("dve", lambda e: e.memset(csqh, 2.0 ** -0.5), writes=("csqh",))
    for name, dst in [("ident", ident), ("m01ea", m01ea), ("m01ah", m01ah),
                      ("onescf", onescf), ("one11f", one11f), ("onecol", onecol),
                      ("bqp", bqp), ("bkp", bkp), ("bv_rep", bv_rep),
                      ("cw_h", cw_h), ("cb_h", cb_h), ("identf", identf)]:
        p.dma("sp", "qsp", dst, dram[name], writes=(name,))

    def ln_stats(src, rkeys, pb=0):
        b0 = 24 + 8 * pb
        st = stat6 if pb == 0 else smal[:, 48:60]
        kst = f"stat6_{pb}"
        p.op("dve", lambda e, st=st: e.bn_stats(st[:, 0:6], src[:, 0:512]),
             reads=rkeys, writes=(kst,))
        p.op("dve", lambda e, st=st: e.bn_stats(st[:, 6:12], src[:, 512:1024]),
             reads=rkeys, writes=(kst,))
        p.op("dve", lambda e, st=st, b0=b0: e.bn_aggr(smal[:, b0:b0 + 2], st[:, 0:12]),
             reads=(kst,), writes=(f"c_mv{pb}",))
        p.act_set(AFT.Sqrt)
        p.op("act", lambda e, b0=b0: e.activation(sc(b0 + 2), sc(b0 + 1), AFT.Sqrt,
                                                  bias=ceps, scale=1.0),
             reads=(f"c_mv{pb}", "ceps"), writes=(f"c_r{pb}",))
        p.op("dve", lambda e, b0=b0: e.reciprocal(sc(b0 + 2), sc(b0 + 2)),
             reads=(f"c_r{pb}",), writes=(f"c_r{pb}",))
        p.op("dve", lambda e, b0=b0: e.tensor_scalar(sc(b0 + 3), sc(b0), sc(b0 + 2), -1.0,
                                                     ALU.mult, ALU.mult),
             reads=(f"c_mv{pb}", f"c_r{pb}"), writes=(f"c_nmr{pb}",))
        return b0

    for t in range(nt):
        xb, lb = x_t[t % 2], xln_tm[t % 2]
        kx, kl = f"x{t % 2}", f"xl{t % 2}"
        p.dma("sp", "qsp", xb, dram["x_in"][t * 128:(t + 1) * 128, :], writes=(kx,))
        b0 = ln_stats(xb, (kx,), pb=t % 2)
        p.act_set(AFT.Identity)
        p.op("act", lambda e, xb=xb, lb=lb, b0=b0, t=t: e.activation(lb, xb, AFT.Identity,
             bias=sc(b0 + 3), scale=sc(b0 + 2)),
             reads=(kx, f"c_r{t % 2}", f"c_nmr{t % 2}"), writes=(kl,))
        for dc in range(8):
            p.op("pe", lambda e, lb=lb, dc=dc: e.transpose(
                psT[:, dc * 128: dc * 128 + 128], lb[:, dc * 128:(dc + 1) * 128], ident),
                reads=(kl, "ident"), writes=("bT",))
        dx = fdims(xlnf[:, t * 128: t * 128 + 1], [[SEQ, 8], [1, 128]])
        p.op("dve", lambda e, dx=dx: e.tensor_copy(dx, psT[:, 0:1024]),
             reads=(), writes=("xlnf", "bT"))

    for wname, bt, bkey, dst, dkey in [("wqtp", bqp, "bqp", qfm4, "qfm4"),
                                       ("wktp", bkp, "bkp", kfm4, "kfm4")]:
        for dc in range(8):
            stg, ks = wstg[dc % 2], f"wstg{dc % 2}"
            p.dma("sp", "qsp", stg[:, 0:512], dram[wname][dc * 128:(dc + 1) * 128, :],
                  writes=(ks,))
            p.op("dve", lambda e, stg=stg, dc=dc: e.tensor_copy(
                wqk_sb[:, dc * 512:(dc + 1) * 512], stg[:, 0:512]),
                reads=(ks,), writes=("wqk",))
        for pg in range(4):
            for ng in range(NG):
                ps, kp = (psA, "bA") if ng % 2 == 0 else (psB, "bB")
                for dc in range(8):
                    p.op("pe", lambda e, pg=pg, ng=ng, dc=dc, ps=ps: e.matmul(
                        ps[:, 0:NW],
                        wqk_sb[:, dc * 512 + pg * 128: dc * 512 + (pg + 1) * 128],
                        xlnf[:, dc * SEQ + ng * NW: dc * SEQ + (ng + 1) * NW],
                        start=(dc == 0), stop=(dc == 7)),
                        reads=("wqk", "xlnf"), writes=(kp,))
                p.act_set(AFT.Identity)
                p.op("act", lambda e, pg=pg, ng=ng, ps=ps, bt=bt, dst=dst: e.activation(
                    dst[:, pg * SEQ + ng * NW: pg * SEQ + (ng + 1) * NW], ps[:, 0:NW],
                    AFT.Identity, bias=bt[:, pg:pg + 1], scale=1.0),
                    reads=(bkey,), writes=(dkey, kp))

    for dc in range(8):
        stg, ks = wstg[dc % 2], f"wstg{dc % 2}"
        p.dma("sp", "qsp", stg[:, 0:1024], dram["wvt"][dc * 128:(dc + 1) * 128, :],
              writes=(ks,))
        p.op("dve", lambda e, stg=stg, dc=dc: e.tensor_copy(
            wvt_sb[:, dc * DM:(dc + 1) * DM], stg[:, 0:1024]), reads=(ks,), writes=("wvt",))
    for t in range(nt):
        for dc in range(8):
            for hf in range(2):
                p.op("pe", lambda e, t=t, dc=dc, hf=hf: e.matmul(
                    psC[:, hf * 512:(hf + 1) * 512],
                    xlnf[:, dc * SEQ + t * 128: dc * SEQ + (t + 1) * 128],
                    wvt_sb[:, dc * DM + hf * 512: dc * DM + (hf + 1) * 512],
                    start=(dc == 0), stop=(dc == 7)),
                    reads=("wvt", "xlnf"), writes=("bC0", "bC1"))
        dstv = fdims(vt[:, t * 1040: t * 1040 + 1], [[65, 16], [1, 64]])
        p.op("dve", lambda e, dstv=dstv: e.tensor_tensor(dstv, psC[:], bv_rep, ALU.add),
             reads=("bv_rep",), writes=("vt", "bC0", "bC1"))
        ones_ap = fdims(vt[:, t * 1040 + 64: t * 1040 + 65], [[65, 16]])
        p.op("dve", lambda e, ones_ap=ones_ap: e.memset(ones_ap, 1.0), writes=("vt",))

    for src, skey, dst, dkey in [(qfm4, "qfm4", qtm4s, "qtm4s"), (kfm4, "kfm4", ktm4s, "ktm4s")]:
        for t in range(nt):
            for pg in range(4):
                p.op("pe", lambda e, src=src, pg=pg, t=t: e.transpose(
                    psT[:, pg * 128: (pg + 1) * 128],
                    src[:, pg * SEQ + t * 128: pg * SEQ + (t + 1) * 128], ident),
                    reads=(skey, "ident"), writes=("bT",))
            p.act_set(AFT.Copy)
            p.op("act", lambda e, t=t, dst=dst: e.activation(
                dst[:, t * 512:(t + 1) * 512], psT[:, 0:512],
                AFT.Copy, bias=0.0, scale=SQH),
                reads=(), writes=(dkey, "bT"))

    p.barrier()

    for hh in range(2):
        for t in range(nt):
            p.op("dve", lambda e, hh=hh, t=t: e.memset(
                phk[hh][:, t * 280: t * 280 + 1], 1.0), writes=(f"phk{hh}",))
        p.op("dve", lambda e, hh=hh: e.memset(phk[hh][:, nt * 280: nt * 280 + 104], 0.0),
             writes=(f"phk{hh}",))
        p.op("dve", lambda e, hh=hh: e.memset(phq_tm[hh][:, 0:1], 1.0),
             writes=(f"phq_tm{hh}",))

    def wp_dma(idx):
        if idx < 32:
            r, cb = idx // 4, idx % 4
            src = dram["w1t"][r * 128:(r + 1) * 128, cb * 1024:(cb + 1) * 1024]
        else:
            i = idx - 32
            src = dram["w2t"][i * 128:(i + 1) * 128, :]
        j = idx % 2
        p.dma("act", "qact", w1stg[j], src, writes=(f"w1stg{j}",))

    def wp_comp(idx):
        wcol = wsum1[:, idx:idx + 1] if idx < 32 else wsum2[:, idx - 32:idx - 31]
        j = idx % 2
        p.act_set(AFT.Abs)
        p.op("act", lambda e, j=j: e.activation(w1stg[j], w1stg[j], AFT.Abs, bias=0.0,
                                                scale=1.0),
             reads=(f"w1stg{j}",), writes=(f"w1stg{j}",))
        p.op("dve", lambda e, wcol=wcol, j=j: e.reduce_sum(wcol, w1stg[j], AX.X),
             reads=(f"w1stg{j}",), writes=("wsum",))

    def wpass(idx):
        if idx > 0:
            wp_comp(idx - 1)
        if idx < 64:
            wp_dma(idx)

    wp_iter = iter(range(65))

    for hp in range(H // 2):
        heads = (2 * hp, 2 * hp + 1)
        hinfo = []
        for h in heads:
            pg_, sl_ = h // 4, h % 4
            po = sl_ * 32
            hinfo.append((h, pg_, po, (96, 0) if po == 96 else None))

        for hh, (h, pg_, po, tp) in enumerate(hinfo):
            kph = f"phk{hh}"
            us0 = ktm4s[:, pg_ * 128 + po: pg_ * 128 + po + 16]
            uin = fdims(us0, [[512, nt], [1, 16]])
            uout = fdims(phk[hh][:, 1:2], [[280, nt], [1, 16]])
            p.op("dve", lambda e, uin=uin, uout=uout: e.tensor_single_scalar(
                uout, uin, SQHI, ALU.mult), reads=("ktm4s",), writes=(kph,))
            o_out = fdims(phk[hh][:, 17:18], [[280, nt], [16, 16], [1, 16]])
            in_i = fdims(us0, [[512, nt], [1, 16], [0, 16]])
            in_j = fdims(us0, [[512, nt], [0, 16], [1, 16]])
            p.op("dve", lambda e, o_out=o_out, in_i=in_i, in_j=in_j: e.tensor_tensor(
                o_out, in_i, in_j, ALU.mult), reads=("ktm4s",), writes=(kph,))
            p.op("dve", lambda e, hh=hh: e.memset(s_sb[hh][:, 0:195], 0.0),
                 writes=(f"s_sb{hh}",))

        def gmm(hh, cc, half):
            h, pg_, po, tp = hinfo[hh]
            kq = qfm4[:, pg_ * SEQ + cc * 128: pg_ * SEQ + (cc + 1) * 128][po:po + 16, :]
            kk = kfm4[:, pg_ * SEQ + (cc - 1 + half) * 128: pg_ * SEQ + (cc + half) * 128][po:po + 16, :]
            p.op("pe", lambda e, kk=kk, kq=kq, hh=hh, half=half, tp=tp: e.matmul(
                psA[:, hh * 256 + half * 128: hh * 256 + (half + 1) * 128], kk, kq,
                start=True, stop=True, tile_position=tp), reads=("kfm4", "qfm4"),
                writes=("bA",))

        def psy_of(hh, c):
            if hh == 0:
                return (psD[:, 0:130], "bD") if c % 2 == 0 else (psB[:, 0:130], "bB")
            return (psE[:, 0:130], "bE") if c % 2 == 0 else (psF[:, 0:130], "bF")

        def qbuild(cc):
            for hh, (h, pg_, po, tp) in enumerate(hinfo):
                kpt = f"phq_tm{hh}"
                us = qtm4s[:, cc * 512 + pg_ * 128 + po: cc * 512 + pg_ * 128 + po + 16]
                p.op("dve", lambda e, us=us, hh=hh: e.tensor_single_scalar(
                    phq_tm[hh][:, 1:17], us, SQHI, ALU.mult),
                    reads=("qtm4s",), writes=(kpt,))
                o_out = fdims(phq_tm[hh][:, 17:18], [[16, 16], [1, 16]])
                in_i = fdims(us, [[1, 16], [0, 16]])
                in_j = fdims(us, [[0, 16], [1, 16]])
                p.op("dve", lambda e, o_out=o_out, in_i=in_i, in_j=in_j: e.tensor_tensor(
                    o_out, in_i, in_j, ALU.mult), reads=("qtm4s",), writes=(kpt,))

        def epi_act(c):
            for hh in range(2):
                psy, ky = psy_of(hh, c)
                tl, ktl = tloc[hh], f"tl{hh}"
                ct = C_T + 3 * hh
                p.op("act", lambda e, psy=psy, ct=ct: e.activation(
                    smal[:, ct:ct + 1], psy[:, 129:130], AFT.Identity, bias=c1e8,
                    scale=1.0), reads=("c1e8",), writes=(f"den{hh}", ky))
                p.op("act", lambda e, psy=psy, ct=ct: e.activation(
                    smal[:, ct + 2:ct + 3], psy[:, 64:65], AFT.Identity, bias=czero,
                    scale=1.0), reads=("czero",), writes=(f"dloc{hh}", ky))
                if c + 1 < nt:
                    p.op("act", lambda e, hh=hh: e.activation(
                        s16[hh][:, 0:195], s_sb[hh][:, 0:195], AFT.Copy, bias=0.0,
                        scale=1.0), reads=(f"s_sb{hh}",), writes=(f"s16{hh}",))

        def epi_dve(c):
            for hh in range(2):
                psy, ky = psy_of(hh, c)
                h = heads[hh]
                tl, ktl = tloc[hh], f"tl{hh}"
                ct = C_T + 3 * hh
                p.op("dve", lambda e, ct=ct: e.reciprocal(smal[:, ct + 1:ct + 2],
                                                          smal[:, ct:ct + 1]),
                     reads=(f"den{hh}",), writes=(f"rlin{hh}",))
                p.op("dve", lambda e, ct=ct: e.reciprocal(smal[:, ct + 2:ct + 3],
                                                          smal[:, ct + 2:ct + 3]),
                     reads=(f"dloc{hh}",), writes=(f"dloc{hh}",))
                p.op("dve", lambda e, psy=psy, tl=tl, ct=ct: e.tensor_single_scalar(
                    tl, psy[:, 0:64], smal[:, ct + 2:ct + 3], ALU.mult),
                    reads=(f"dloc{hh}",), writes=(ktl, ky))
                p.op("dve", lambda e, psy=psy, tl=tl, ct=ct, c=c, h=h: e.scalar_tensor_tensor(
                    attn[:, c * DM + h * 64: c * DM + h * 64 + 64], psy[:, 65:129],
                    smal[:, ct + 1:ct + 2], tl, ALU.mult, ALU.add),
                    reads=(f"rlin{hh}", ktl), writes=("attn", ky))

        if nt > 1:
            qbuild(1)
        for c in range(nt):
            lo = 0 if c > 0 else 128
            for hh, (h, pg_, po, tp) in enumerate(hinfo):
                pss = psC[:, 0:195] if hh == 0 else psC[:, 512:707]
                ks_ = f"bC{hh}"
                vcur = vt[:, c * 1040 + h * 65: c * 1040 + h * 65 + 65]
                for k in range(3):
                    p.op("pe", lambda e, k=k, c=c, hh=hh, pss=pss, vcur=vcur: e.matmul(
                        pss[:, k * 65:(k + 1) * 65],
                        phk[hh][:, c * 280 + k * 128: c * 280 + (k + 1) * 128],
                        vcur, start=True, stop=True),
                        reads=(f"phk{hh}", "vt"), writes=(ks_,))
            if c == 0:
                for hh in range(2):
                    gmm(hh, 0, 1)

            p.act_set(AFT.Exp)
            for hh in range(2):
                esb, ke = e_sb[hh * 2 + c % 2], f"esb{hh}_{c % 2}"
                p.op("act", lambda e, hh=hh, esb=esb, lo=lo: e.activation(
                    esb[:, lo:256], psA[:, hh * 256 + lo: hh * 256 + 256], AFT.Exp,
                    bias=0.0, scale=1.0), reads=(), writes=(ke, "bA"))
                p.op("act", lambda e, hh=hh, esb=esb: e.activation(
                    esb[:, 256:384], psA[:, hh * 256 + 128: hh * 256 + 256], AFT.Square,
                    bias=csqh, scale=2.0 ** -0.5), reads=("csqh",), writes=(ke, "bA"))
            if c > 0:
                epi_act(c - 1)

            for hh in range(2):
                esb, ke = e_sb[hh * 2 + c % 2], f"esb{hh}_{c % 2}"
                p.op("dve", lambda e, esb=esb, lo=lo: e.tensor_tensor(
                    esb[:, lo:256], esb[:, lo:256], m01ea[:, lo:256], ALU.mult),
                    reads=("m01ea",), writes=(ke,))
                p.op("dve", lambda e, esb=esb: e.scalar_tensor_tensor(
                    esb[:, 256:384], esb[:, 256:384], 0.5, m01ea[:, 256:384],
                    ALU.add, ALU.mult), reads=("m01ea",), writes=(ke,))

            if c + 1 < nt:
                for hh in range(2):
                    gmm(hh, c + 1, 0)
                    gmm(hh, c + 1, 1)
                for hh in range(2):
                    kpt = f"phq_tm{hh}"
                    for k in range(3):
                        w = 128 if k < 2 else 17
                        p.op("pe", lambda e, k=k, w=w, hh=hh: e.transpose(
                            psT[0:w, hh * 512 + k * 128: hh * 512 + k * 128 + 128],
                            phq_tm[hh][:, k * 128:k * 128 + w], ident),
                            reads=(kpt, "ident"), writes=("bT",))

            for hh, (h, pg_, po, tp) in enumerate(hinfo):
                psy, ky = psy_of(hh, c)
                esb, ke = e_sb[hh * 2 + c % 2], f"esb{hh}_{c % 2}"
                vprev = vt[:, (c - 1) * 1040 + h * 65: (c - 1) * 1040 + h * 65 + 65]
                vcur = vt[:, c * 1040 + h * 65: c * 1040 + h * 65 + 65]
                if c > 0:
                    p.op("pe", lambda e, esb=esb, vprev=vprev, psy=psy: e.matmul(
                        psy[:, 0:65], esb[:, 0:128], vprev, start=True, stop=False),
                        reads=(ke, "vt"), writes=(ky,))
                    p.op("pe", lambda e, esb=esb, vcur=vcur, psy=psy: e.matmul(
                        psy[:, 0:65], esb[:, 128:256], vcur, start=False, stop=True),
                        reads=(ke, "vt"), writes=(ky,))
                else:
                    p.op("pe", lambda e, esb=esb, vcur=vcur, psy=psy: e.matmul(
                        psy[:, 0:65], esb[:, 128:256], vcur, start=True, stop=True),
                        reads=(ke, "vt"), writes=(ky,))
                p.op("pe", lambda e, esb=esb, vcur=vcur, psy=psy, c=c: e.matmul(
                    psy[:, 65:130], esb[:, 256:384], vcur, start=True, stop=(c == 0)),
                    reads=(ke, "vt"), writes=(ky,))
                if c > 0:
                    slot = c % 2
                    kr = f"phqr{hh}_{slot}"
                    for k in range(3):
                        kw = 128 if k < 2 else 17
                        p.op("pe", lambda e, k=k, kw=kw, hh=hh, slot=slot, psy=psy: e.matmul(
                            psy[:, 65:130],
                            phqr[hh][0:kw, slot * 384 + k * 128: slot * 384 + (k + 1) * 128],
                            s16[hh][0:kw, k * 65:(k + 1) * 65], start=False, stop=(k == 2)),
                            reads=(kr, f"s16{hh}"), writes=(ky,))

            for hh in range(2):
                pss = psC[:, 0:195] if hh == 0 else psC[:, 512:707]
                ks_ = f"bC{hh}"
                p.op("dve", lambda e, pss=pss, hh=hh: e.tensor_tensor(
                    s_sb[hh][:, 0:195], s_sb[hh][:, 0:195], pss[:, 0:195], ALU.add),
                    reads=(f"s_sb{hh}",), writes=(f"s_sb{hh}", ks_))
            if c > 0:
                epi_dve(c - 1)
            if c + 1 < nt:
                for hh in range(2):
                    slot = (c + 1) % 2
                    kr = f"phqr{hh}_{slot}"
                    p.op("dve", lambda e, hh=hh, slot=slot: e.tensor_copy(
                        phqr[hh][:, slot * 384: slot * 384 + 256],
                        psT[:, hh * 512: hh * 512 + 256]), reads=(), writes=(kr, "bT"))
                    p.op("dve", lambda e, hh=hh, slot=slot: e.tensor_copy(
                        phqr[hh][0:17, slot * 384 + 256: slot * 384 + 384],
                        psT[0:17, hh * 512 + 256: hh * 512 + 384]),
                        reads=(), writes=(kr, "bT"))
            if c + 2 < nt:
                qbuild(c + 2)

            if (hp * nt + c) % 2 == 0:
                nxt = next(wp_iter, None)
                if nxt is not None:
                    wpass(nxt)
        epi_act(nt - 1)
        epi_dve(nt - 1)

    for nxt in wp_iter:
        wpass(nxt)

    for i, (ws, nel, c_ma, c_sc) in enumerate([(wsum1, DM * HID, C_MA1, C_SC1),
                                               (wsum2, HID * DM, C_MA2, C_SC2)]):
        p.op("pe", lambda e, ws=ws, i=i: e.matmul(psE[0:1, 256 + i * 32:256 + (i + 1) * 32],
                                                  onecol, ws, start=True, stop=True),
             reads=("wsum", "onecol"), writes=("bE",))
        p.op("dve", lambda e, i=i: e.tensor_reduce(
            s1x1[:, 4 * i:4 * i + 1], psE[0:1, 256 + i * 32:256 + (i + 1) * 32], AX.X, ALU.add),
            reads=(), writes=(f"s1a{i}", "bE"))
        p.op("dve", lambda e, i=i, nel=nel: e.tensor_scalar(
            s1x1[:, 4 * i + 1:4 * i + 2], s1x1[:, 4 * i:4 * i + 1], 1.0 / nel, 1e-5,
            ALU.mult, ALU.max), reads=(f"s1a{i}",), writes=(f"s1b{i}",))
        p.op("pe", lambda e, i=i: e.matmul(psE[:, 320 + i:321 + i], onescf,
                                           s1x1[:, 4 * i + 1:4 * i + 2],
                                           start=True, stop=True),
             reads=(f"s1b{i}", "onescf"), writes=("bE",))
        p.op("dve", lambda e, i=i, c_ma=c_ma: e.tensor_copy(sc(c_ma), psE[:, 320 + i:321 + i]),
             reads=(), writes=(f"cma{i}", "bE"))
        p.op("dve", lambda e, c_ma=c_ma, c_sc=c_sc, i=i: e.reciprocal(sc(c_sc), sc(c_ma)),
             reads=(f"cma{i}",), writes=(f"csc{i}",))

    p.barrier()
    if dbg:
        for t in range(nt):
            p.dma("sp", "qsp", dram["dbg_attn"][t * 128:(t + 1) * 128, :],
                  attn[:, t * DM:(t + 1) * DM], reads=("attn",))

    def tern_src(i):
        if i < 32:
            r, cb = i // 4, i % 4
            return dram["w1t"][r * 128:(r + 1) * 128, cb * 1024:(cb + 1) * 1024]
        r = i - 32
        return dram["w2t"][r * 128:(r + 1) * 128, :]

    def tern_dma(i):
        p.dma("act", "qact", wbuf[i % 3], tern_src(i), writes=(f"wbuf{i % 3}",))

    def tern_chunk(i):
        if i < 32:
            dst = w1q[:, (i // 4) * HID + (i % 4) * 1024: (i // 4) * HID + (i % 4 + 1) * 1024]
            kd, scl, kscl = "w1q", sc(C_SC1), "csc0"
        else:
            dst = w2q[:, (i - 32) * DM:(i - 31) * DM]
            kd, scl, kscl = "w2q", sc(C_SC2), "csc1"
        j = i % 3
        wb, sg, db = wbuf[j], sgb[j], dbb[j]
        kw, ks, kb2 = f"wbuf{j}", f"sgb{j}", f"dbb{j}"
        p.act_set(AFT.Sign)
        p.op("act", lambda e, wb=wb, sg=sg: e.activation(sg, wb, AFT.Sign, bias=0.0,
                                                         scale=1.0),
             reads=(kw,), writes=(ks,))
        p.act_set(AFT.Abs)
        p.op("act", lambda e, scl=scl, wb=wb: e.activation(wb, wb, AFT.Abs, bias=0.0,
                                                           scale=scl),
             reads=(kw, kscl), writes=(kw,))
        p.op("dve", lambda e, wb=wb, db=db: e.tensor_single_scalar(db, wb, 0.5, ALU.is_ge),
             reads=(kw,), writes=(kb2,))
        p.op("dve", lambda e, dst=dst, db=db, sg=sg: e.tensor_tensor(dst, db, sg, ALU.mult),
             reads=(kb2, ks), writes=(kd,))
        if i + 3 < 64:
            tern_dma(i + 3)

    tern_iter = iter(range(64))
    tern_dma(0)
    tern_dma(1)
    tern_dma(2)
    for t in range(nt):
        pb = t % 2
        xb, kx = x_t4[pb], f"x4_{pb}"
        p.dma("sp", "qsp", xb, dram["x_in"][t * 128:(t + 1) * 128, :], writes=(kx,))
        asl = attn[:, t * DM:(t + 1) * DM]
        b0 = ln_stats(asl, ("attn",), pb=pb)
        p.act_set(AFT.Identity)
        p.op("act", lambda e, asl=asl, b0=b0: e.activation(ln3b, asl, AFT.Identity,
             bias=sc(b0 + 3), scale=sc(b0 + 2)),
             reads=("attn", f"c_r{pb}", f"c_nmr{pb}"), writes=("ln3b",))
        p.op("dve", lambda e, xb=xb: e.tensor_tensor(xb, ln3b, xb, ALU.add),
             reads=("ln3b", kx), writes=(kx,))
        p.dma("sp", "qsp", xmid_d[t * 128:(t + 1) * 128, :], xb, reads=(kx,),
              writes=(f"xmd{t}",))
        if dbg:
            p.dma("sp", "qsp", dram["dbg_xmid"][t * 128:(t + 1) * 128, :], xb, reads=(kx,))
        for _ in range(4):
            nxt = next(tern_iter, None)
            if nxt is not None:
                tern_chunk(nxt)
    for nxt in tern_iter:
        tern_chunk(nxt)

    p.barrier()

    def ph4b_pre(t):
        sl4 = t % 4
        xm, kxm = xm_in[sl4], f"xm{sl4}"
        pb = t % 2
        xqtb, kqt = xqtm[pb], f"xqtm{pb}"
        p.dma("sp", "qsp", xm, xmid_d[t * 128:(t + 1) * 128, :],
              reads=(f"xmd{t}",), writes=(kxm,))
        st = smal[:, 60 + 12 * pb: 72 + 12 * pb]
        kst = f"st4_{pb}"
        b0 = 84 + 4 * pb
        p.op("dve", lambda e, st=st, xm=xm: e.bn_stats(st[:, 0:6], xm[:, 0:512]),
             reads=(kxm,), writes=(kst,))
        p.op("dve", lambda e, st=st, xm=xm: e.bn_stats(st[:, 6:12], xm[:, 512:1024]),
             reads=(kxm,), writes=(kst,))
        p.op("dve", lambda e, st=st, b0=b0: e.bn_aggr(smal[:, b0:b0 + 2], st[:, 0:12]),
             reads=(kst,), writes=(f"mv4{pb}",))
        p.act_set(AFT.Sqrt)
        p.op("act", lambda e, b0=b0: e.activation(sc(b0 + 2), sc(b0 + 1), AFT.Sqrt,
                                                  bias=ceps, scale=1.0),
             reads=(f"mv4{pb}", "ceps"), writes=(f"r4{pb}",))
        p.op("dve", lambda e, b0=b0: e.reciprocal(sc(b0 + 2), sc(b0 + 2)),
             reads=(f"r4{pb}",), writes=(f"r4{pb}",))
        p.op("dve", lambda e, b0=b0: e.tensor_scalar(sc(b0 + 3), sc(b0), sc(b0 + 2), -1.0,
                                                     ALU.mult, ALU.mult),
             reads=(f"mv4{pb}", f"r4{pb}"), writes=(f"n4{pb}",))
        p.act_set(AFT.Identity)
        p.op("act", lambda e, xm=xm, b0=b0: e.activation(xl2b, xm, AFT.Identity,
             bias=sc(b0 + 3), scale=sc(b0 + 2)),
             reads=(kxm, f"r4{pb}", f"n4{pb}"), writes=("xl2b",))
        cq = 92 + 3 * pb
        p.op("dve", lambda e, cq=cq: e.reduce_max(sc(cq), xl2b, AX.X,
                                                  apply_absolute_value=True),
             reads=("xl2b",), writes=(f"cmax4_{pb}",))
        p.op("dve", lambda e, cq=cq: e.tensor_single_scalar(sc(cq), sc(cq), 1e-5, ALU.max),
             reads=(f"cmax4_{pb}",), writes=(f"cmax4_{pb}",))
        p.op("dve", lambda e, cq=cq: e.tensor_scalar(sc(cq + 1), sc(cq), sc(C_MA1),
                                                     1.0 / 127.0, ALU.mult, ALU.mult),
             reads=(f"cmax4_{pb}", "cma0"), writes=(f"cisx4_{pb}",))
        p.op("dve", lambda e, cq=cq: e.reciprocal(sc(cq + 2), sc(cq)),
             reads=(f"cmax4_{pb}",), writes=(f"csx4_{pb}",))
        p.op("dve", lambda e, cq=cq: e.tensor_single_scalar(sc(cq + 2), sc(cq + 2), 127.0,
                                                            ALU.mult),
             reads=(f"csx4_{pb}",), writes=(f"csx4_{pb}",))
        p.op("dve", lambda e, cq=cq, pb=pb: e.tensor_scalar(
            xqi[pb], xl2b, sc(cq + 2), 1000.5, ALU.mult, ALU.add),
            reads=("xl2b", f"csx4_{pb}"), writes=(kqt,))
        p.op("dve", lambda e, cq=cq, pb=pb, xqtb=xqtb: e.tensor_scalar(
            xqtb, xqi[pb], 1000.0, sc(cq + 1), ALU.subtract, ALU.mult),
            reads=(kqt, f"cisx4_{pb}"), writes=(kqt,))
    def ph4b_tr_a(t):
        pb = t % 2
        xqtb, kqt = xqtm[pb], f"xqtm{pb}"
        for dc in range(4):
            p.op("pe", lambda e, dc=dc, xqtb=xqtb: e.transpose(
                psT[:, dc * 128: (dc + 1) * 128],
                xqtb[:, dc * 128:(dc + 1) * 128], ident),
                reads=(kqt, "ident"), writes=("bT",))

    def ph4b_tr(t):
        sl4 = t % 4
        pb = t % 2
        xqtb, kqt = xqtm[pb], f"xqtm{pb}"
        xs, kxs = xqs[sl4], f"xqs{sl4}"
        for dc in range(4, 8):
            p.op("pe", lambda e, dc=dc, xqtb=xqtb: e.transpose(
                psT[:, dc * 128: (dc + 1) * 128],
                xqtb[:, dc * 128:(dc + 1) * 128], ident),
                reads=(kqt, "ident"), writes=("bT",))
        d1 = fdims(xs[:, 1:2], [[132, 8], [1, 128]])
        p.op("dve", lambda e, d1=d1: e.tensor_copy(d1, psT[:, 0:1024]),
             reads=(), writes=(kxs, "bT"))
        if t == 0:
            z0 = fdims(xs[:, 0:1], [[132, 8]])
            p.op("dve", lambda e, z0=z0: e.memset(z0, 0.0), writes=(kxs,))
        else:
            slp = (t - 1) % 4
            dsth = fdims(xqs[slp][:, 129:130], [[132, 8]])
            srch = fdims(xs[:, 1:2], [[132, 8]])
            p.op("dve", lambda e, dsth=dsth, srch=srch: e.tensor_copy(dsth, srch),
                 reads=(kxs,), writes=(f"xqs{slp}",))
            dstl = fdims(xs[:, 0:1], [[132, 8]])
            srcl = fdims(xqs[slp][:, 128:129], [[132, 8]])
            p.op("dve", lambda e, dstl=dstl, srcl=srcl: e.tensor_copy(dstl, srcl),
                 reads=(f"xqs{slp}",), writes=(kxs,))
        if t == nt - 1:
            z1 = fdims(xs[:, 129:130], [[132, 8]])
            p.op("dve", lambda e, z1=z1: e.memset(z1, 0.0), writes=(kxs,))
        if dbg:
            dsrc = fdims(xs[:, 1:2], [[132, 8], [1, 128]])
            ddst = fdims(dram["dbg_xqf"][:, t * 128: t * 128 + 1], [[SEQ, 8], [1, 128]])
            p.dma("sp", "qsp", ddst, dsrc, reads=(kxs,))

    def quant_group(t, g):
        hqb, hqi, khq = hqg[g % 2], hqg_i16[g % 2], f"hqg{g % 2}"
        h2g = h2[:, g * 1024:(g + 1) * 1024]
        kh2 = f"h2g{g}"
        srep = fdims(sbc[:, 0:1], [[0, 8], [1, 128]])
        p.op("dve", lambda e, h2g=h2g, srep=srep: e.tensor_tensor(h2g, h2g, srep, ALU.mult),
             reads=("sbc", kh2), writes=(kh2,))
        p.op("dve", lambda e, h2g=h2g, hqi=hqi: e.tensor_single_scalar(hqi, h2g, 1000.5,
                                                                       ALU.add),
             reads=(kh2,), writes=(khq,))
        p.op("dve", lambda e, hqb=hqb, hqi=hqi: e.tensor_single_scalar(hqb, hqi, 1000.0,
                                                                       ALU.subtract),
             reads=(khq,), writes=(khq,))
        if dbg and t == 0:
            p.dma("sp", "qsp", dram["dbg_h2"][:, g * 1024:(g + 1) * 1024], h2g, reads=(kh2,))
            p.dma("sp", "qsp", dram["dbg_hq"][:, g * 1024:(g + 1) * 1024], hqb, reads=(khq,))

    def w2_group(g):
        hqb, khq = hqg[g % 2], f"hqg{g % 2}"
        for mj in range(8):
            m = g * 8 + mj
            for hf in range(2):
                p.op("pe", lambda e, m=m, mj=mj, hf=hf, hqb=hqb: e.matmul(
                    psC[:, hf * 512:(hf + 1) * 512], hqb[:, mj * 128:(mj + 1) * 128],
                    w2q[:, m * DM + hf * 512: m * DM + (hf + 1) * 512],
                    start=(m == 0), stop=(m == NM - 1)),
                    reads=(khq, "w2q"), writes=("bC0", "bC1"))

    def stage1(t, fill=None):
        xs, kxs = xqs[t % 4], f"xqs{t % 4}"
        w1banks = [(psA, "bA"), (psB, "bB"), (psE, "bE"), (psF, "bF")]
        for m in range(NM):
            psh, kh = w1banks[m % 4]
            hb, kb = hsb[m % 2], f"hsb{m % 2}"
            acb, kc = accb[m % 2], f"acc{m % 2}"
            for dc in range(8):
                p.op("pe", lambda e, m=m, psh=psh, xs=xs, dc=dc: e.matmul(
                    psh[:, 0:130],
                    w1q[:, dc * HID + m * 128: dc * HID + (m + 1) * 128],
                    xs[:, dc * 132: dc * 132 + 130], start=(dc == 0), stop=(dc == 7)),
                    reads=("w1q", kxs), writes=(kh,))
            if m % 2 == 0:
                p.act_set(AFT.Copy)
                p.op("act", lambda e, psh=psh, hb=hb: e.activation(
                    hb, psh[:, 0:130], AFT.Copy, bias=0.0, scale=1.0),
                    reads=(), writes=(kb, kh))
            else:
                p.op("dve", lambda e, psh=psh, hb=hb: e.tensor_copy(hb, psh[:, 0:130]),
                     reads=(), writes=(kb, kh))
            p.op("dve", lambda e, hb=hb, acb=acb, m=m: e.tensor_single_scalar(
                acb, hb[:, 0:128], cw_h[:, m:m + 1], ALU.mult),
                reads=(kb, "cw_h"), writes=(kc,))
            p.op("dve", lambda e, hb=hb, acb=acb, m=m: e.scalar_tensor_tensor(
                acb, hb[:, 1:129], cw_h[:, 32 + m:33 + m], acb, ALU.mult, ALU.add),
                reads=(kb, kc, "cw_h"), writes=(kc,))
            p.op("dve", lambda e, hb=hb, acb=acb, m=m: e.scalar_tensor_tensor(
                acb, hb[:, 2:130], cw_h[:, 64 + m:65 + m], acb, ALU.mult, ALU.add),
                reads=(kb, kc, "cw_h"), writes=(kc,))
            p.act_set(AFT.Silu)
            p.op("act", lambda e, acb=acb, m=m: e.activation(
                h2[:, m * 128:(m + 1) * 128], acb, AFT.Silu, bias=cb_h[:, m:m + 1], scale=1.0),
                reads=(kc, "cb_h"), writes=(f"h2g{m // 8}",))
            if fill is not None and m in (7, 15, 23, 31):
                fill(m // 8)

    def h2_partial(t, b):
        rin = fdims(h2[:, b * 1024: b * 1024 + 1], [[1, 128], [128, 8]])
        dst = pmax if b == 0 else sbcp
        kdst = "pmax" if b == 0 else "sbcp"
        p.op("dve", lambda e, rin=rin, dst=dst: e.tensor_reduce(
            dst, rin, AX.X, ALU.max, apply_absolute_value=True),
            reads=(f"h2g{b}",), writes=(kdst,))
        if b > 0:
            p.op("dve", lambda e: e.tensor_tensor(pmax, pmax, sbcp, ALU.max),
                 reads=("pmax", "sbcp"), writes=("pmax",))

    def qdance_p1(t):
        p.op("pe", lambda e: e.transpose(psD[:, 0:128], pmax, identf),
             reads=("pmax", "identf"), writes=("bD",))
        p.op("dve", lambda e: e.reduce_max(sc(C_MAX), psD[:, 0:128], AX.X),
             reads=(), writes=("cmax5", "bD"))
        p.op("dve", lambda e: e.tensor_single_scalar(sc(C_MAX), sc(C_MAX), 1e-5, ALU.max),
             reads=("cmax5",), writes=("cmax5",))
        cfc = C_F if t % 2 == 0 else C_ISX
        p.op("dve", lambda e, cfc=cfc: e.tensor_scalar(sc(cfc), sc(C_MAX), sc(C_MA2),
                                                       1.0 / 127.0, ALU.mult, ALU.mult),
             reads=("cmax5", "cma1"), writes=(f"cf{t % 2}",))
        p.op("dve", lambda e: e.reciprocal(sc(C_SX), sc(C_MAX)),
             reads=("cmax5",), writes=("csx5",))
        p.op("dve", lambda e: e.tensor_single_scalar(sc(C_SX), sc(C_SX), 127.0, ALU.mult),
             reads=("csx5",), writes=("csx5",))

    def qdance_p2(t):
        p.op("pe", lambda e: e.transpose(psD[0:1, 128:256], sc(C_SX), identf),
             reads=("csx5", "identf"), writes=("bD",))
        p.op("dve", lambda e: e.tensor_copy(srow_full[0:1, :], psD[0:1, 128:256]),
             reads=(), writes=("srow", "bD"))

    def qdance_p3(t):
        p.op("pe", lambda e: e.matmul(psD[:, 256:384], onescf, srow_full[0:1, :],
                                      start=True, stop=True),
             reads=("srow", "onescf"), writes=("bD",))
        p.op("dve", lambda e: e.tensor_copy(sbc, psD[:, 256:384]),
             reads=(), writes=("sbc", "bD"))
        quant_group(t, 0)
        quant_group(t, 1)

    def mk_fill(t):
        def fill(b):
            h2_partial(t, b)
            if t > 0:
                if b == 1:
                    w2_group(0)
                    quant_group(t - 1, 2)
                elif b == 2:
                    w2_group(1)
                    quant_group(t - 1, 3)
                elif b == 3:
                    w2_group(2)
                    w2_group(3)
        return fill

    def stage2b(t):
        p.op("dve", lambda e: e.bn_stats(stat6[:, 0:6], psC[:, 0:512]),
             reads=(), writes=("stat6", "bC0"))
        p.op("dve", lambda e: e.bn_stats(stat6[:, 6:12], psC[:, 512:1024]),
             reads=(), writes=("stat6", "bC1"))
        p.op("dve", lambda e: e.bn_aggr(smal[:, C_MEAN:C_VAR + 1], stat6[:]),
             reads=("stat6",), writes=("c_mv",))
        cfc = C_F if t % 2 == 0 else C_ISX
        p.op("dve", lambda e, cfc=cfc: e.tensor_scalar(sc(C_VT2), sc(C_VAR), sc(cfc),
                                                       sc(cfc), ALU.mult, ALU.mult),
             reads=("c_mv", f"cf{t % 2}"), writes=("cvt2",))
        p.act_set(AFT.Sqrt)
        p.op("act", lambda e: e.activation(sc(C_R), sc(C_VT2), AFT.Sqrt, bias=ceps, scale=1.0),
             reads=("cvt2", "ceps"), writes=("c_r",))
        p.op("dve", lambda e: e.reciprocal(sc(C_R), sc(C_R)), reads=("c_r",), writes=("c_r",))
        p.op("dve", lambda e, cfc=cfc: e.tensor_scalar(sc(C_S4), sc(C_R), sc(cfc), None,
                                                       ALU.mult),
             reads=("c_r", f"cf{t % 2}"), writes=("cs4",))
        p.op("dve", lambda e: e.tensor_scalar(sc(C_B4), sc(C_MEAN), sc(C_S4), -1.0,
                                              ALU.mult, ALU.mult),
             reads=("c_mv", "cs4"), writes=("cb4",))
        p.act_set(AFT.Identity)
        p.op("act", lambda e: e.activation(outb, psC[:], AFT.Identity,
                                           bias=sc(C_B4), scale=sc(C_S4)),
             reads=("cs4", "cb4"), writes=("outb", "bC0", "bC1"))
        p.op("dve", lambda e, t=t: e.tensor_tensor(outb, outb, xm_in[t % 4], ALU.add),
             reads=("outb", f"xm{t % 4}"), writes=("outb",))
        p.dma("sp", "qsp", dram["out"][t * 128:(t + 1) * 128, :], outb, reads=("outb",))

    ph4b_pre(0)
    if nt > 1:
        ph4b_pre(1)
    ph4b_tr_a(0)
    ph4b_tr(0)
    if nt > 1:
        if nt > 2:
            ph4b_pre(2)
        ph4b_tr_a(1)
        ph4b_tr(1)
    for t in range(nt + 1):
        if t < nt:
            stage1(t, fill=mk_fill(t))
        else:
            w2_group(0)
            quant_group(t - 1, 2)
            w2_group(1)
            quant_group(t - 1, 3)
            w2_group(2)
            w2_group(3)
        if t < nt:
            qdance_p1(t)
        if t + 2 < nt:
            ph4b_tr_a(t + 2)
        if t < nt:
            qdance_p2(t)
        if t + 2 < nt:
            ph4b_tr(t + 2)
        if t < nt:
            qdance_p3(t)
        if 0 < t:
            stage2b(t - 1)
        if t + 3 < nt:
            ph4b_pre(t + 3)

    p.finalize()
    return nc, es


_BUILD_CACHE = {}


def _get_program(nt, dbg=False):
    key = (nt, dbg)
    if key not in _BUILD_CACHE:
        _BUILD_CACHE[key] = build_program(nt, dbg, strict_eng="dve")
    return _BUILD_CACHE[key]


def host_inputs(x_core, Wq, bq, Wk, bk, Wv, bv, w1, conv_w, conv_b, w2, nt=16):
    f32 = np.float32
    bf = ml_dtypes.bfloat16
    d = {}
    d["x_in"] = np.ascontiguousarray(x_core, dtype=f32)
    wqt = np.zeros((DM, 512), f32)
    wkt = np.zeros((DM, 512), f32)
    bqp = np.zeros((128, 4), f32)
    bkp = np.zeros((128, 4), f32)
    for h in range(H):
        pg, sl = h // 4, h % 4
        wqt[:, pg * 128 + sl * 32: pg * 128 + sl * 32 + 16] = Wq[h * 16:(h + 1) * 16, :].T
        wkt[:, pg * 128 + sl * 32: pg * 128 + sl * 32 + 16] = Wk[h * 16:(h + 1) * 16, :].T
        bqp[sl * 32: sl * 32 + 16, pg] = bq[h * 16:(h + 1) * 16]
        bkp[sl * 32: sl * 32 + 16, pg] = bk[h * 16:(h + 1) * 16]
    d["wqtp"] = wqt
    d["wktp"] = wkt
    d["bqp"] = bqp
    d["bkp"] = bkp
    d["wvt"] = np.ascontiguousarray(Wv.T, dtype=f32)
    d["bv_rep"] = np.ascontiguousarray(np.broadcast_to(bv, (128, DM)), dtype=f32)
    d["w1t"] = np.ascontiguousarray(w1.T, dtype=f32)
    d["w2t"] = np.ascontiguousarray(w2.T, dtype=f32)
    cw = np.zeros((128, 96), f32)
    for k in range(3):
        cw[:, k * 32:(k + 1) * 32] = conv_w[:, 0, k].reshape(NM, 128).T
    d["cw_h"] = cw
    d["cb_h"] = np.ascontiguousarray(conv_b.reshape(NM, 128).T, dtype=f32)
    d["ident"] = np.eye(128, dtype=bf)
    pp, qq = np.arange(128)[:, None], np.arange(128)[None, :]
    m_prev = (pp - qq >= 65).astype(bf)
    dqp = qq - pp
    m_diag = ((dqp >= 0) & (dqp <= 63)).astype(bf)
    m01a = (qq >= pp).astype(np.float32)
    d["m01ea"] = np.ascontiguousarray(
        np.concatenate([m_prev.astype(np.float32), m_diag.astype(np.float32), m01a], axis=1)).astype(bf)
    d["m01ah"] = (0.5 * m01a).astype(bf)
    d["onescf"] = np.ones((1, 128), f32)
    d["identf"] = np.eye(128, dtype=f32)
    d["one11f"] = np.ones((1, 1), f32)
    d["onecol"] = np.ones((128, 1), f32)
    return d


def kernel(L0_attn_x, L0_attn_Wq, L0_attn_bq, L0_attn_Wk, L0_attn_bk, L0_attn_Wv,
           L0_attn_bv, L0_ffn_w1, L0_ffn_conv_w, L0_ffn_conv_b, L0_ffn_w2):
    x = np.asarray(L0_attn_x)
    nt = x.shape[1] // 128
    nc, _es = _get_program(nt)
    in_maps = []
    for c in range(x.shape[0]):
        in_maps.append(host_inputs(
            x[c], np.asarray(L0_attn_Wq), np.asarray(L0_attn_bq), np.asarray(L0_attn_Wk),
            np.asarray(L0_attn_bk), np.asarray(L0_attn_Wv), np.asarray(L0_attn_bv),
            np.asarray(L0_ffn_w1), np.asarray(L0_ffn_conv_w), np.asarray(L0_ffn_conv_b),
            np.asarray(L0_ffn_w2), nt=nt))
    res = run_bass_kernel_spmd(nc, in_maps, core_ids=list(range(x.shape[0])))
    return np.stack([np.asarray(r["out"]) for r in res.results], axis=0)


# version 22
# speedup vs baseline: 1.0426x; 1.0172x over previous
import os
import numpy as np
import ml_dtypes
from contextlib import ExitStack

import concourse.bass as bass
import concourse.mybir as mybir
from concourse.alu_op_type import AluOpType as ALU
from concourse.bass_utils import run_bass_kernel_spmd

F32 = mybir.dt.float32
BF16 = mybir.dt.bfloat16
I16 = mybir.dt.int16
AFT = mybir.ActivationFunctionType
AX = mybir.AxisListType

H, DP, DH, DT = 16, 16, 64, 273
DM, HID = 1024, 4096
NM = HID // 128
SQH = 2.0 ** (-0.25)
SQHI = 2.0 ** (0.25)
EPS = 1e-5
NCOL = 52400

ENGS = ("pe", "dve", "act", "pool", "sp")
QUEUES = ("qsp", "qact", "qpool")


def fdims(ap, dims, extra_off=0):
    return bass.AP(ap.tensor, ap.offset + extra_off,
                   [list(ap.ap[0])] + [list(d) for d in dims])


class Prog:
    RSLOT = 8

    def __init__(self, nc, block, strict=False, strict_dma=False, strict_eng=""):
        self.strict = strict
        self.strict_dma = strict_dma
        self.strict_eng = strict_eng
        self.sems = {}
        self.qsems = {}
        self._ctx = ExitStack()
        for name in ENGS:
            self.sems[name] = self._ctx.enter_context(nc.semaphore("sem_" + name))
        for q in QUEUES:
            self.qsems[q] = [self._ctx.enter_context(nc.semaphore(f"sem_{q}_{i}"))
                             for i in range(self.RSLOT)]
        self.cnt = {name: 0 for name in ENGS}
        self.qcnt = {q: 0 for q in QUEUES}
        self.last_w = {}
        self.readers = {}
        self.hw = {}
        self.secs = {"pe": block.tensor, "dve": block.vector, "act": block.scalar,
                     "pool": block.gpsimd, "sp": block.sync}
        self.nops = 0
        self.maxops = int(os.environ.get("KMAXOPS", "0"))
        self.act_cur = None
        from concourse.hw_specs import get_activation_tables
        self._act_tables = list(get_activation_tables(nc.m.arch).values())

    def act_set(self, func):
        for idx, funcs in enumerate(self._act_tables):
            if func in funcs:
                break
        else:
            raise ValueError(func)
        if self.act_cur is not None and func in self._act_tables[self.act_cur]:
            return
        self.act_cur = idx
        self.op("act", lambda e: e.preload_act_table(func))

    def _qwait(self, q, idx):
        slot = (idx - 1) % self.RSLOT
        uses = (idx - 1) // self.RSLOT + 1
        return self.qsems[q][slot], 16 * uses, (q, slot)

    def _edge_wait(self, eng, dom, val, waits):
        if dom in QUEUES:
            sem, tgt, hwkey = self._qwait(dom, val)
        else:
            sem, tgt, hwkey = self.sems[dom], val, dom
        k = (eng, hwkey)
        if self.hw.get(k, 0) < tgt:
            self.hw[k] = tgt
            waits.append((sem, tgt))

    def op(self, eng, fn, reads=(), writes=(), dom=None):
        if self.maxops and self.nops >= self.maxops:
            return 0
        dom = dom or eng
        is_dma = dom in QUEUES

        def keep(d):
            if d != dom:
                return True
            if self.strict:
                return True
            if d in QUEUES:
                return self.strict_dma
            return d in self.strict_eng

        edges = []
        for k in reads:
            lw = self.last_w.get(k)
            if lw is not None and keep(lw[0]):
                edges.append(lw)
        for k in writes:
            lw = self.last_w.get(k)
            if lw is not None and keep(lw[0]):
                edges.append(lw)
            for rdom, rval in self.readers.get(k, {}).items():
                if keep(rdom):
                    edges.append((rdom, rval))
        waits = []
        for d, v in edges:
            self._edge_wait(eng, d, v, waits)
        if is_dma:
            self.qcnt[dom] += 1
            val = self.qcnt[dom]
            sem, tgt, hwkey = self._qwait(dom, val)
            k = (eng, hwkey)
            if tgt - 16 > 0 and self.hw.get(k, 0) < tgt - 16:
                self.hw[k] = tgt - 16
                waits.append((sem, tgt - 16))
            inc_sem, inc_amt = sem, 16
        else:
            self.cnt[dom] += 1
            val = self.cnt[dom]
            inc_sem, inc_amt = self.sems[dom], 1
            if self.strict and val > 1:
                waits.append((inc_sem, val - 1))

        def body(e):
            for sm, v in waits:
                e.wait_ge(sm, v)
            fn(e).then_inc(inc_sem, inc_amt)

        self.secs[eng](body)
        self.nops += 1
        for k in reads:
            self.readers.setdefault(k, {})[dom] = val
        for k in writes:
            self.last_w[k] = (dom, val)
            self.readers[k] = {}
        return val

    def dma(self, eng, queue, out, in_, reads=(), writes=()):
        return self.op(eng, lambda e: e.dma_start(out=out, in_=in_),
                       reads=reads, writes=writes, dom=queue)

    def _all_final(self):
        fin = []
        for name in ENGS:
            if self.cnt[name] > 0:
                fin.append((name, self.sems[name], self.cnt[name], name))
        for q in QUEUES:
            n = self.qcnt[q]
            for slot in range(self.RSLOT):
                uses = (n - slot + self.RSLOT - 1) // self.RSLOT if n > slot else 0
                uses = len(range(slot, n, self.RSLOT))
                if uses > 0:
                    fin.append(((q, slot), self.qsems[q][slot], 16 * uses, (q, slot)))
        return fin

    def barrier(self):
        fin = self._all_final()
        for eng in ENGS:
            waits = []
            for hwkey, sem, v, _ in fin:
                if hwkey == eng:
                    continue
                if self.hw.get((eng, hwkey), 0) < v:
                    self.hw[(eng, hwkey)] = v
                    waits.append((sem, v))
            def body(e, waits=waits):
                for sm, v in waits:
                    e.wait_ge(sm, v)

            self.secs[eng](body)

    def finalize(self):
        fin = self._all_final()

        def body(e):
            for hwkey, sem, v, _ in fin:
                if hwkey == "sp":
                    continue
                e.wait_ge(sem, v)

        self.secs["sp"](body)


class Arena:
    def __init__(self, nc):
        self.h = nc.alloc_sbuf_tensor("arena", [128, NCOL], F32)
        self.A32 = self.h[:, :]
        self.A16 = self.h[:, :].bitcast(BF16)
        self.AI16 = self.h[:, :].bitcast(I16)
        self.cur = 0
        self.limit = NCOL

    def region(self, base, limit):
        self.cur = base
        self.limit = limit

    def f32(self, n):
        ap = self.A32[:, self.cur:self.cur + n]
        self.cur += n
        assert self.cur <= self.limit, f"arena overflow {self.cur} > {self.limit}"
        return ap

    def bf16(self, n16):
        assert n16 % 2 == 0
        ap = self.A16[:, 2 * self.cur: 2 * self.cur + n16]
        self.cur += n16 // 2
        assert self.cur <= self.limit, f"arena overflow {self.cur} > {self.limit}"
        return ap

    def i16(self, n16):
        assert n16 % 2 == 0
        ap = self.AI16[:, 2 * self.cur: 2 * self.cur + n16]
        self.cur += n16 // 2
        assert self.cur <= self.limit, f"arena overflow {self.cur} > {self.limit}"
        return ap


def build_program(nt=16, dbg=False, strict=False, detect=False, strict_dma=False, strict_eng=""):
    SEQ = nt * 128
    NW = min(512, SEQ)
    NG = SEQ // NW
    nc = bass.Bass("TRN2", target_bir_lowering=False, debug=False,
                   detect_race_conditions=detect)

    dram = {}
    for name, shape, dt in [
        ("x_in", [SEQ, DM], F32), ("wqtp", [DM, 512], F32), ("wktp", [DM, 512], F32),
        ("wvt", [DM, DM], F32), ("bqp", [128, 4], F32), ("bkp", [128, 4], F32),
        ("bv_rep", [128, DM], F32), ("w1t", [DM, HID], F32), ("w2t", [HID, DM], F32),
        ("cw_h", [128, 96], F32), ("cb_h", [128, 32], F32), ("ident", [128, 128], BF16),
        ("m01ea", [128, 384], BF16), ("m01ah", [128, 128], BF16),
        ("onescf", [1, 128], F32), ("one11f", [1, 1], F32), ("onecol", [128, 1], F32),
        ("identf", [128, 128], F32),
    ]:
        dram[name] = nc.dram_tensor(name, shape, dt, kind="ExternalInput").ap()
    dram["out"] = nc.dram_tensor("out", [SEQ, DM], F32, kind="ExternalOutput").ap()
    xmid_d = nc.dram_tensor("xmid_sc", [SEQ, DM], F32, kind="Internal").ap()
    if dbg:
        for name, shape, dt in [("dbg_attn", [SEQ, DM], F32), ("dbg_xmid", [SEQ, DM], F32),
                                ("dbg_xqf", [128, 8 * SEQ], BF16),
                                ("dbg_h2", [128, HID], F32), ("dbg_hq", [128, HID], BF16)]:
            dram[name] = nc.dram_tensor(name, shape, dt, kind="ExternalOutput").ap()

    ar = Arena(nc)
    ar.region(0, 1984)
    ident = ar.bf16(128)
    m01ea = ar.bf16(384)
    m01ah = ar.bf16(128)
    onescf = ar.f32(128)[0:1, :]
    one11f = ar.f32(2)[0:1, 0:1]
    onecol = ar.f32(1)
    bqp = ar.f32(4)
    bkp = ar.f32(4)
    cw_h = ar.f32(96)
    cb_h = ar.f32(32)
    smal = ar.f32(100)
    s1x1 = ar.f32(64)[0:1, :]
    wsum1 = ar.f32(32)
    wsum2 = ar.f32(32)
    stat6 = ar.f32(12)
    identf = ar.f32(128)
    ceps = ar.f32(1)
    czero = ar.f32(1)
    c1e8 = ar.f32(1)
    csqh = ar.f32(1)
    bv_rep = ar.f32(1024)
    sbcp = bv_rep[:, 0:128]
    pmax = bv_rep[:, 128:256]

    C_MEAN, C_VAR, C_R, C_NMR = 0, 1, 2, 3
    C_MA1, C_SC1, C_MA2, C_SC2 = 4, 5, 6, 7
    C_MAX, C_ISX, C_F, C_S4, C_B4, C_VT2, C_SX = 8, 9, 10, 11, 12, 13, 14
    C_T = 16

    def sc(c):
        return smal[:, c:c + 1]

    B1, L1 = 1984, 11050
    B2, L2 = 11050, 27434
    B3, L3 = 27434, 35754
    B4, L4 = 35754, 52138

    ar.region(B2, L2)
    qfm4 = ar.bf16(4 * SEQ)
    kfm4 = ar.bf16(4 * SEQ)
    qtm4s = ar.bf16(4 * SEQ)
    ktm4s = ar.bf16(4 * SEQ)
    ar.region(B2, L2)
    w1q = ar.bf16(8 * HID)

    ar.region(B3, L3)
    vt = ar.bf16(nt * 1040)
    ar.region(B3, L3)
    wbuf = [ar.f32(1024) for _ in range(3)]
    sgb = [ar.bf16(1024) for _ in range(3)]
    dbb = [ar.bf16(1024) for _ in range(3)]
    ar.region(B3, L3)
    h2 = ar.f32(4096)
    xqs = [ar.bf16(8 * 132) for _ in range(4)]

    ar.region(B4, L4)
    attn = ar.f32(nt * DM)
    ar.region(B4, L4)
    w2q = ar.bf16(NM * DM)
    ar.region(B4, L4)
    wqk_sb = ar.bf16(8 * 512)
    wvt_sb = ar.bf16(8 * DM)
    wstg = [ar.f32(1024), ar.f32(1024)]
    x_t = [ar.f32(1024), ar.f32(1024)]
    xln_tm = [ar.bf16(1024), ar.bf16(1024)]

    ar.region(B1, L1)
    xlnf = ar.bf16(8 * SEQ)
    ar.region(B1, L1)
    phk = [ar.bf16(nt * 280 + 104), ar.bf16(nt * 280 + 104)]
    phqr = [ar.bf16(2 * 384), ar.bf16(2 * 384)]
    phq_tm = [ar.bf16(280), ar.bf16(280)]
    e_sb = [ar.bf16(384) for _ in range(4)]
    s_sb = [ar.f32(195), ar.f32(195)]
    s16 = [ar.bf16(196), ar.bf16(196)]
    tloc = [bv_rep[:, 256:320], bv_rep[:, 320:384]]
    w1stg = [ar.f32(1024), ar.f32(1024)]
    ar.region(B1, L1)
    x_t4 = [ar.f32(1024), ar.f32(1024)]
    ln3b = ar.f32(1024)
    ar.region(B1, L1)
    xm_in = [ar.f32(1024) for _ in range(4)]
    xl2b = ar.f32(1024)
    xqtm = [ar.bf16(1024), ar.bf16(1024)]
    hsb = [ar.f32(130), ar.f32(130)]
    accb = [ar.f32(128), ar.f32(128)]
    hqg = [ar.bf16(1024), ar.bf16(1024)]
    sbc = ar.f32(128)
    srow_full = ar.f32(128)
    outb = ar.f32(1024)

    hqg_i16 = [bass.AP(ar.AI16.tensor, q.offset, [list(q.ap[0])] + [list(d) for d in q.ap[1:]])
               for q in hqg]
    xqi = [bass.AP(ar.AI16.tensor, q.offset, [list(q.ap[0])] + [list(d) for d in q.ap[1:]])
           for q in xqtm]

    es = ExitStack()
    psT = es.enter_context(nc.psum_tensor("psT", [128, 1024], BF16))
    psA = es.enter_context(nc.psum_tensor("psA", [128, 512], F32))
    psB = es.enter_context(nc.psum_tensor("psB", [128, 512], F32))
    psC = es.enter_context(nc.psum_tensor("psC", [128, 1024], F32))
    psD = es.enter_context(nc.psum_tensor("psD", [128, 512], F32))
    psE = es.enter_context(nc.psum_tensor("psE", [128, 512], F32))
    psF = es.enter_context(nc.psum_tensor("psF", [128, 512], F32))
    block = es.enter_context(nc.Block())
    p = Prog(nc, block, strict=strict, strict_dma=strict_dma, strict_eng=strict_eng)

    p.op("dve", lambda e: e.memset(ceps, EPS), writes=("ceps",))
    p.op("dve", lambda e: e.memset(czero, 0.0), writes=("czero",))
    p.op("dve", lambda e: e.memset(c1e8, 1e-8), writes=("c1e8",))
    p.op("dve", lambda e: e.memset(csqh, 2.0 ** -0.5), writes=("csqh",))
    for name, dst in [("ident", ident), ("m01ea", m01ea), ("m01ah", m01ah),
                      ("onescf", onescf), ("one11f", one11f), ("onecol", onecol),
                      ("bqp", bqp), ("bkp", bkp), ("bv_rep", bv_rep),
                      ("cw_h", cw_h), ("cb_h", cb_h), ("identf", identf)]:
        p.dma("sp", "qsp", dst, dram[name], writes=(name,))

    def ln_stats(src, rkeys, pb=0):
        b0 = 24 + 8 * pb
        st = stat6 if pb == 0 else smal[:, 48:60]
        kst = f"stat6_{pb}"
        p.op("dve", lambda e, st=st: e.bn_stats(st[:, 0:6], src[:, 0:512]),
             reads=rkeys, writes=(kst,))
        p.op("dve", lambda e, st=st: e.bn_stats(st[:, 6:12], src[:, 512:1024]),
             reads=rkeys, writes=(kst,))
        p.op("dve", lambda e, st=st, b0=b0: e.bn_aggr(smal[:, b0:b0 + 2], st[:, 0:12]),
             reads=(kst,), writes=(f"c_mv{pb}",))
        p.act_set(AFT.Sqrt)
        p.op("act", lambda e, b0=b0: e.activation(sc(b0 + 2), sc(b0 + 1), AFT.Sqrt,
                                                  bias=ceps, scale=1.0),
             reads=(f"c_mv{pb}", "ceps"), writes=(f"c_r{pb}",))
        p.op("dve", lambda e, b0=b0: e.reciprocal(sc(b0 + 2), sc(b0 + 2)),
             reads=(f"c_r{pb}",), writes=(f"c_r{pb}",))
        p.op("dve", lambda e, b0=b0: e.tensor_scalar(sc(b0 + 3), sc(b0), sc(b0 + 2), -1.0,
                                                     ALU.mult, ALU.mult),
             reads=(f"c_mv{pb}", f"c_r{pb}"), writes=(f"c_nmr{pb}",))
        return b0

    for t in range(nt):
        xb, lb = x_t[t % 2], xln_tm[t % 2]
        kx, kl = f"x{t % 2}", f"xl{t % 2}"
        p.dma("sp", "qsp", xb, dram["x_in"][t * 128:(t + 1) * 128, :], writes=(kx,))
        b0 = ln_stats(xb, (kx,), pb=t % 2)
        p.act_set(AFT.Identity)
        p.op("act", lambda e, xb=xb, lb=lb, b0=b0, t=t: e.activation(lb, xb, AFT.Identity,
             bias=sc(b0 + 3), scale=sc(b0 + 2)),
             reads=(kx, f"c_r{t % 2}", f"c_nmr{t % 2}"), writes=(kl,))
        for dc in range(8):
            p.op("pe", lambda e, lb=lb, dc=dc: e.transpose(
                psT[:, dc * 128: dc * 128 + 128], lb[:, dc * 128:(dc + 1) * 128], ident),
                reads=(kl, "ident"), writes=("bT",))
        dx = fdims(xlnf[:, t * 128: t * 128 + 1], [[SEQ, 8], [1, 128]])
        p.op("dve", lambda e, dx=dx: e.tensor_copy(dx, psT[:, 0:1024]),
             reads=(), writes=("xlnf", "bT"))

    for wname, bt, bkey, dst, dkey in [("wqtp", bqp, "bqp", qfm4, "qfm4"),
                                       ("wktp", bkp, "bkp", kfm4, "kfm4")]:
        for dc in range(8):
            stg, ks = wstg[dc % 2], f"wstg{dc % 2}"
            p.dma("sp", "qsp", stg[:, 0:512], dram[wname][dc * 128:(dc + 1) * 128, :],
                  writes=(ks,))
            p.op("dve", lambda e, stg=stg, dc=dc: e.tensor_copy(
                wqk_sb[:, dc * 512:(dc + 1) * 512], stg[:, 0:512]),
                reads=(ks,), writes=("wqk",))
        for pg in range(4):
            for ng in range(NG):
                ps, kp = (psA, "bA") if ng % 2 == 0 else (psB, "bB")
                for dc in range(8):
                    p.op("pe", lambda e, pg=pg, ng=ng, dc=dc, ps=ps: e.matmul(
                        ps[:, 0:NW],
                        wqk_sb[:, dc * 512 + pg * 128: dc * 512 + (pg + 1) * 128],
                        xlnf[:, dc * SEQ + ng * NW: dc * SEQ + (ng + 1) * NW],
                        start=(dc == 0), stop=(dc == 7)),
                        reads=("wqk", "xlnf"), writes=(kp,))
                p.act_set(AFT.Identity)
                p.op("act", lambda e, pg=pg, ng=ng, ps=ps, bt=bt, dst=dst: e.activation(
                    dst[:, pg * SEQ + ng * NW: pg * SEQ + (ng + 1) * NW], ps[:, 0:NW],
                    AFT.Identity, bias=bt[:, pg:pg + 1], scale=1.0),
                    reads=(bkey,), writes=(dkey, kp))

    for dc in range(8):
        stg, ks = wstg[dc % 2], f"wstg{dc % 2}"
        p.dma("sp", "qsp", stg[:, 0:1024], dram["wvt"][dc * 128:(dc + 1) * 128, :],
              writes=(ks,))
        p.op("dve", lambda e, stg=stg, dc=dc: e.tensor_copy(
            wvt_sb[:, dc * DM:(dc + 1) * DM], stg[:, 0:1024]), reads=(ks,), writes=("wvt",))
    for t in range(nt):
        for dc in range(8):
            for hf in range(2):
                p.op("pe", lambda e, t=t, dc=dc, hf=hf: e.matmul(
                    psC[:, hf * 512:(hf + 1) * 512],
                    xlnf[:, dc * SEQ + t * 128: dc * SEQ + (t + 1) * 128],
                    wvt_sb[:, dc * DM + hf * 512: dc * DM + (hf + 1) * 512],
                    start=(dc == 0), stop=(dc == 7)),
                    reads=("wvt", "xlnf"), writes=("bC0", "bC1"))
        dstv = fdims(vt[:, t * 1040: t * 1040 + 1], [[65, 16], [1, 64]])
        p.op("dve", lambda e, dstv=dstv: e.tensor_tensor(dstv, psC[:], bv_rep, ALU.add),
             reads=("bv_rep",), writes=("vt", "bC0", "bC1"))
        ones_ap = fdims(vt[:, t * 1040 + 64: t * 1040 + 65], [[65, 16]])
        p.op("dve", lambda e, ones_ap=ones_ap: e.memset(ones_ap, 1.0), writes=("vt",))

    for src, skey, dst, dkey in [(qfm4, "qfm4", qtm4s, "qtm4s"), (kfm4, "kfm4", ktm4s, "ktm4s")]:
        for t in range(nt):
            for pg in range(4):
                p.op("pe", lambda e, src=src, pg=pg, t=t: e.transpose(
                    psT[:, pg * 128: (pg + 1) * 128],
                    src[:, pg * SEQ + t * 128: pg * SEQ + (t + 1) * 128], ident),
                    reads=(skey, "ident"), writes=("bT",))
            p.act_set(AFT.Copy)
            p.op("act", lambda e, t=t, dst=dst: e.activation(
                dst[:, t * 512:(t + 1) * 512], psT[:, 0:512],
                AFT.Copy, bias=0.0, scale=SQH),
                reads=(), writes=(dkey, "bT"))

    p.barrier()

    for hh in range(2):
        for t in range(nt):
            p.op("dve", lambda e, hh=hh, t=t: e.memset(
                phk[hh][:, t * 280: t * 280 + 1], 1.0), writes=(f"phk{hh}",))
        p.op("dve", lambda e, hh=hh: e.memset(phk[hh][:, nt * 280: nt * 280 + 104], 0.0),
             writes=(f"phk{hh}",))
        p.op("dve", lambda e, hh=hh: e.memset(phq_tm[hh][:, 0:1], 1.0),
             writes=(f"phq_tm{hh}",))

    def wp_dma(idx):
        if idx < 32:
            r, cb = idx // 4, idx % 4
            src = dram["w1t"][r * 128:(r + 1) * 128, cb * 1024:(cb + 1) * 1024]
        else:
            i = idx - 32
            src = dram["w2t"][i * 128:(i + 1) * 128, :]
        j = idx % 2
        p.dma("act", "qact", w1stg[j], src, writes=(f"w1stg{j}",))

    def wp_comp(idx):
        wcol = wsum1[:, idx:idx + 1] if idx < 32 else wsum2[:, idx - 32:idx - 31]
        j = idx % 2
        p.act_set(AFT.Abs)
        p.op("act", lambda e, j=j: e.activation(w1stg[j], w1stg[j], AFT.Abs, bias=0.0,
                                                scale=1.0),
             reads=(f"w1stg{j}",), writes=(f"w1stg{j}",))
        p.op("dve", lambda e, wcol=wcol, j=j: e.reduce_sum(wcol, w1stg[j], AX.X),
             reads=(f"w1stg{j}",), writes=("wsum",))

    def wpass(idx):
        if idx > 0:
            wp_comp(idx - 1)
        if idx < 64:
            wp_dma(idx)

    wp_iter = iter(range(65))

    for hp in range(H // 2):
        heads = (2 * hp, 2 * hp + 1)
        hinfo = []
        for h in heads:
            pg_, sl_ = h // 4, h % 4
            po = sl_ * 32
            hinfo.append((h, pg_, po, (96, 0) if po == 96 else None))

        for hh, (h, pg_, po, tp) in enumerate(hinfo):
            kph = f"phk{hh}"
            us0 = ktm4s[:, pg_ * 128 + po: pg_ * 128 + po + 16]
            uin = fdims(us0, [[512, nt], [1, 16]])
            uout = fdims(phk[hh][:, 1:2], [[280, nt], [1, 16]])
            p.op("dve", lambda e, uin=uin, uout=uout: e.tensor_single_scalar(
                uout, uin, SQHI, ALU.mult), reads=("ktm4s",), writes=(kph,))
            o_out = fdims(phk[hh][:, 17:18], [[280, nt], [16, 16], [1, 16]])
            in_i = fdims(us0, [[512, nt], [1, 16], [0, 16]])
            in_j = fdims(us0, [[512, nt], [0, 16], [1, 16]])
            p.op("dve", lambda e, o_out=o_out, in_i=in_i, in_j=in_j: e.tensor_tensor(
                o_out, in_i, in_j, ALU.mult), reads=("ktm4s",), writes=(kph,))
            p.op("dve", lambda e, hh=hh: e.memset(s_sb[hh][:, 0:195], 0.0),
                 writes=(f"s_sb{hh}",))

        def gmm(hh, cc, half):
            h, pg_, po, tp = hinfo[hh]
            kq = qfm4[:, pg_ * SEQ + cc * 128: pg_ * SEQ + (cc + 1) * 128][po:po + 16, :]
            kk = kfm4[:, pg_ * SEQ + (cc - 1 + half) * 128: pg_ * SEQ + (cc + half) * 128][po:po + 16, :]
            p.op("pe", lambda e, kk=kk, kq=kq, hh=hh, half=half, tp=tp: e.matmul(
                psA[:, hh * 256 + half * 128: hh * 256 + (half + 1) * 128], kk, kq,
                start=True, stop=True, tile_position=tp), reads=("kfm4", "qfm4"),
                writes=("bA",))

        def psy_of(hh, c):
            if hh == 0:
                return (psD[:, 0:130], "bD") if c % 2 == 0 else (psB[:, 0:130], "bB")
            return (psE[:, 0:130], "bE") if c % 2 == 0 else (psF[:, 0:130], "bF")

        def qbuild(cc):
            for hh, (h, pg_, po, tp) in enumerate(hinfo):
                kpt = f"phq_tm{hh}"
                us = qtm4s[:, cc * 512 + pg_ * 128 + po: cc * 512 + pg_ * 128 + po + 16]
                p.op("dve", lambda e, us=us, hh=hh: e.tensor_single_scalar(
                    phq_tm[hh][:, 1:17], us, SQHI, ALU.mult),
                    reads=("qtm4s",), writes=(kpt,))
                o_out = fdims(phq_tm[hh][:, 17:18], [[16, 16], [1, 16]])
                in_i = fdims(us, [[1, 16], [0, 16]])
                in_j = fdims(us, [[0, 16], [1, 16]])
                p.op("dve", lambda e, o_out=o_out, in_i=in_i, in_j=in_j: e.tensor_tensor(
                    o_out, in_i, in_j, ALU.mult), reads=("qtm4s",), writes=(kpt,))

        def epi_act(c):
            for hh in range(2):
                psy, ky = psy_of(hh, c)
                tl, ktl = tloc[hh], f"tl{hh}"
                ct = C_T + 3 * hh
                p.op("act", lambda e, psy=psy, ct=ct: e.activation(
                    smal[:, ct:ct + 1], psy[:, 129:130], AFT.Identity, bias=c1e8,
                    scale=1.0), reads=("c1e8",), writes=(f"den{hh}", ky))
                p.op("act", lambda e, psy=psy, ct=ct: e.activation(
                    smal[:, ct + 2:ct + 3], psy[:, 64:65], AFT.Identity, bias=czero,
                    scale=1.0), reads=("czero",), writes=(f"dloc{hh}", ky))
                if c + 1 < nt:
                    p.op("act", lambda e, hh=hh: e.activation(
                        s16[hh][:, 0:195], s_sb[hh][:, 0:195], AFT.Copy, bias=0.0,
                        scale=1.0), reads=(f"s_sb{hh}",), writes=(f"s16{hh}",))

        def epi_dve(c):
            for hh in range(2):
                psy, ky = psy_of(hh, c)
                h = heads[hh]
                tl, ktl = tloc[hh], f"tl{hh}"
                ct = C_T + 3 * hh
                p.op("dve", lambda e, ct=ct: e.reciprocal(smal[:, ct + 1:ct + 2],
                                                          smal[:, ct:ct + 1]),
                     reads=(f"den{hh}",), writes=(f"rlin{hh}",))
                p.op("dve", lambda e, ct=ct: e.reciprocal(smal[:, ct + 2:ct + 3],
                                                          smal[:, ct + 2:ct + 3]),
                     reads=(f"dloc{hh}",), writes=(f"dloc{hh}",))
                p.op("dve", lambda e, psy=psy, tl=tl, ct=ct: e.tensor_single_scalar(
                    tl, psy[:, 0:64], smal[:, ct + 2:ct + 3], ALU.mult),
                    reads=(f"dloc{hh}",), writes=(ktl, ky))
                p.op("dve", lambda e, psy=psy, tl=tl, ct=ct, c=c, h=h: e.scalar_tensor_tensor(
                    attn[:, c * DM + h * 64: c * DM + h * 64 + 64], psy[:, 65:129],
                    smal[:, ct + 1:ct + 2], tl, ALU.mult, ALU.add),
                    reads=(f"rlin{hh}", ktl), writes=("attn", ky))

        if nt > 1:
            qbuild(1)
        for c in range(nt):
            lo = 0 if c > 0 else 128
            for hh, (h, pg_, po, tp) in enumerate(hinfo):
                pss = psC[:, 0:195] if hh == 0 else psC[:, 512:707]
                ks_ = f"bC{hh}"
                vcur = vt[:, c * 1040 + h * 65: c * 1040 + h * 65 + 65]
                for k in range(3):
                    p.op("pe", lambda e, k=k, c=c, hh=hh, pss=pss, vcur=vcur: e.matmul(
                        pss[:, k * 65:(k + 1) * 65],
                        phk[hh][:, c * 280 + k * 128: c * 280 + (k + 1) * 128],
                        vcur, start=True, stop=True),
                        reads=(f"phk{hh}", "vt"), writes=(ks_,))
            if c == 0:
                for hh in range(2):
                    gmm(hh, 0, 1)

            p.act_set(AFT.Exp)
            for hh in range(2):
                esb, ke = e_sb[hh * 2 + c % 2], f"esb{hh}_{c % 2}"
                p.op("act", lambda e, hh=hh, esb=esb, lo=lo: e.activation(
                    esb[:, lo:256], psA[:, hh * 256 + lo: hh * 256 + 256], AFT.Exp,
                    bias=0.0, scale=1.0), reads=(), writes=(ke, "bA"))
                p.op("act", lambda e, hh=hh, esb=esb: e.activation(
                    esb[:, 256:384], psA[:, hh * 256 + 128: hh * 256 + 256], AFT.Square,
                    bias=csqh, scale=2.0 ** -0.5), reads=("csqh",), writes=(ke, "bA"))
            if c > 0:
                epi_act(c - 1)

            for hh in range(2):
                esb, ke = e_sb[hh * 2 + c % 2], f"esb{hh}_{c % 2}"
                p.op("dve", lambda e, esb=esb, lo=lo: e.tensor_tensor(
                    esb[:, lo:256], esb[:, lo:256], m01ea[:, lo:256], ALU.mult),
                    reads=("m01ea",), writes=(ke,))
                p.op("dve", lambda e, esb=esb: e.scalar_tensor_tensor(
                    esb[:, 256:384], esb[:, 256:384], 0.5, m01ea[:, 256:384],
                    ALU.add, ALU.mult), reads=("m01ea",), writes=(ke,))

            if c + 1 < nt:
                for hh in range(2):
                    gmm(hh, c + 1, 0)
                    gmm(hh, c + 1, 1)
                for hh in range(2):
                    kpt = f"phq_tm{hh}"
                    for k in range(3):
                        w = 128 if k < 2 else 17
                        p.op("pe", lambda e, k=k, w=w, hh=hh: e.transpose(
                            psT[0:w, hh * 512 + k * 128: hh * 512 + k * 128 + 128],
                            phq_tm[hh][:, k * 128:k * 128 + w], ident),
                            reads=(kpt, "ident"), writes=("bT",))

            for hh, (h, pg_, po, tp) in enumerate(hinfo):
                psy, ky = psy_of(hh, c)
                esb, ke = e_sb[hh * 2 + c % 2], f"esb{hh}_{c % 2}"
                vprev = vt[:, (c - 1) * 1040 + h * 65: (c - 1) * 1040 + h * 65 + 65]
                vcur = vt[:, c * 1040 + h * 65: c * 1040 + h * 65 + 65]
                if c > 0:
                    p.op("pe", lambda e, esb=esb, vprev=vprev, psy=psy: e.matmul(
                        psy[:, 0:65], esb[:, 0:128], vprev, start=True, stop=False),
                        reads=(ke, "vt"), writes=(ky,))
                    p.op("pe", lambda e, esb=esb, vcur=vcur, psy=psy: e.matmul(
                        psy[:, 0:65], esb[:, 128:256], vcur, start=False, stop=True),
                        reads=(ke, "vt"), writes=(ky,))
                else:
                    p.op("pe", lambda e, esb=esb, vcur=vcur, psy=psy: e.matmul(
                        psy[:, 0:65], esb[:, 128:256], vcur, start=True, stop=True),
                        reads=(ke, "vt"), writes=(ky,))
                p.op("pe", lambda e, esb=esb, vcur=vcur, psy=psy, c=c: e.matmul(
                    psy[:, 65:130], esb[:, 256:384], vcur, start=True, stop=(c == 0)),
                    reads=(ke, "vt"), writes=(ky,))
                if c > 0:
                    slot = c % 2
                    kr = f"phqr{hh}_{slot}"
                    for k in range(3):
                        kw = 128 if k < 2 else 17
                        p.op("pe", lambda e, k=k, kw=kw, hh=hh, slot=slot, psy=psy: e.matmul(
                            psy[:, 65:130],
                            phqr[hh][0:kw, slot * 384 + k * 128: slot * 384 + (k + 1) * 128],
                            s16[hh][0:kw, k * 65:(k + 1) * 65], start=False, stop=(k == 2)),
                            reads=(kr, f"s16{hh}"), writes=(ky,))

            for hh in range(2):
                pss = psC[:, 0:195] if hh == 0 else psC[:, 512:707]
                ks_ = f"bC{hh}"
                p.op("dve", lambda e, pss=pss, hh=hh: e.tensor_tensor(
                    s_sb[hh][:, 0:195], s_sb[hh][:, 0:195], pss[:, 0:195], ALU.add),
                    reads=(f"s_sb{hh}",), writes=(f"s_sb{hh}", ks_))
            if c > 0:
                epi_dve(c - 1)
            if c + 1 < nt:
                for hh in range(2):
                    slot = (c + 1) % 2
                    kr = f"phqr{hh}_{slot}"
                    p.op("dve", lambda e, hh=hh, slot=slot: e.tensor_copy(
                        phqr[hh][:, slot * 384: slot * 384 + 256],
                        psT[:, hh * 512: hh * 512 + 256]), reads=(), writes=(kr, "bT"))
                    p.op("dve", lambda e, hh=hh, slot=slot: e.tensor_copy(
                        phqr[hh][0:17, slot * 384 + 256: slot * 384 + 384],
                        psT[0:17, hh * 512 + 256: hh * 512 + 384]),
                        reads=(), writes=(kr, "bT"))
            if c + 2 < nt:
                qbuild(c + 2)

            if (hp * nt + c) % 2 == 0:
                nxt = next(wp_iter, None)
                if nxt is not None:
                    wpass(nxt)
        epi_act(nt - 1)
        epi_dve(nt - 1)

    for nxt in wp_iter:
        wpass(nxt)

    for i, (ws, nel, c_ma, c_sc) in enumerate([(wsum1, DM * HID, C_MA1, C_SC1),
                                               (wsum2, HID * DM, C_MA2, C_SC2)]):
        p.op("pe", lambda e, ws=ws, i=i: e.matmul(psE[0:1, 256 + i * 32:256 + (i + 1) * 32],
                                                  onecol, ws, start=True, stop=True),
             reads=("wsum", "onecol"), writes=("bE",))
        p.op("dve", lambda e, i=i: e.tensor_reduce(
            s1x1[:, 4 * i:4 * i + 1], psE[0:1, 256 + i * 32:256 + (i + 1) * 32], AX.X, ALU.add),
            reads=(), writes=(f"s1a{i}", "bE"))
        p.op("dve", lambda e, i=i, nel=nel: e.tensor_scalar(
            s1x1[:, 4 * i + 1:4 * i + 2], s1x1[:, 4 * i:4 * i + 1], 1.0 / nel, 1e-5,
            ALU.mult, ALU.max), reads=(f"s1a{i}",), writes=(f"s1b{i}",))
        p.op("pe", lambda e, i=i: e.matmul(psE[:, 320 + i:321 + i], onescf,
                                           s1x1[:, 4 * i + 1:4 * i + 2],
                                           start=True, stop=True),
             reads=(f"s1b{i}", "onescf"), writes=("bE",))
        p.op("dve", lambda e, i=i, c_ma=c_ma: e.tensor_copy(sc(c_ma), psE[:, 320 + i:321 + i]),
             reads=(), writes=(f"cma{i}", "bE"))
        p.op("dve", lambda e, c_ma=c_ma, c_sc=c_sc, i=i: e.reciprocal(sc(c_sc), sc(c_ma)),
             reads=(f"cma{i}",), writes=(f"csc{i}",))

    p.barrier()
    if dbg:
        for t in range(nt):
            p.dma("sp", "qsp", dram["dbg_attn"][t * 128:(t + 1) * 128, :],
                  attn[:, t * DM:(t + 1) * DM], reads=("attn",))

    def tern_src(i):
        if i < 32:
            r, cb = i // 4, i % 4
            return dram["w1t"][r * 128:(r + 1) * 128, cb * 1024:(cb + 1) * 1024]
        r = i - 32
        return dram["w2t"][r * 128:(r + 1) * 128, :]

    def tern_dma(i):
        p.dma("act", "qact", wbuf[i % 3], tern_src(i), writes=(f"wbuf{i % 3}",))

    def tern_chunk(i):
        if i < 32:
            dst = w1q[:, (i // 4) * HID + (i % 4) * 1024: (i // 4) * HID + (i % 4 + 1) * 1024]
            kd, scl, kscl = "w1q", sc(C_SC1), "csc0"
        else:
            dst = w2q[:, (i - 32) * DM:(i - 31) * DM]
            kd, scl, kscl = "w2q", sc(C_SC2), "csc1"
        j = i % 3
        wb, sg, db = wbuf[j], sgb[j], dbb[j]
        kw, ks, kb2 = f"wbuf{j}", f"sgb{j}", f"dbb{j}"
        p.act_set(AFT.Sign)
        p.op("act", lambda e, wb=wb, sg=sg: e.activation(sg, wb, AFT.Sign, bias=0.0,
                                                         scale=1.0),
             reads=(kw,), writes=(ks,))
        p.act_set(AFT.Abs)
        p.op("act", lambda e, scl=scl, wb=wb: e.activation(wb, wb, AFT.Abs, bias=0.0,
                                                           scale=scl),
             reads=(kw, kscl), writes=(kw,))
        p.op("dve", lambda e, wb=wb, db=db: e.tensor_single_scalar(db, wb, 0.5, ALU.is_ge),
             reads=(kw,), writes=(kb2,))
        p.op("dve", lambda e, dst=dst, db=db, sg=sg: e.tensor_tensor(dst, db, sg, ALU.mult),
             reads=(kb2, ks), writes=(kd,))
        if i + 3 < 64:
            tern_dma(i + 3)

    tern_iter = iter(range(64))
    tern_dma(0)
    tern_dma(1)
    tern_dma(2)
    for t in range(nt):
        pb = t % 2
        xb, kx = x_t4[pb], f"x4_{pb}"
        p.dma("sp", "qsp", xb, dram["x_in"][t * 128:(t + 1) * 128, :], writes=(kx,))
        asl = attn[:, t * DM:(t + 1) * DM]
        b0 = ln_stats(asl, ("attn",), pb=pb)
        p.act_set(AFT.Identity)
        p.op("act", lambda e, asl=asl, b0=b0: e.activation(ln3b, asl, AFT.Identity,
             bias=sc(b0 + 3), scale=sc(b0 + 2)),
             reads=("attn", f"c_r{pb}", f"c_nmr{pb}"), writes=("ln3b",))
        p.op("dve", lambda e, xb=xb: e.tensor_tensor(xb, ln3b, xb, ALU.add),
             reads=("ln3b", kx), writes=(kx,))
        p.dma("sp", "qsp", xmid_d[t * 128:(t + 1) * 128, :], xb, reads=(kx,),
              writes=(f"xmd{t}",))
        if dbg:
            p.dma("sp", "qsp", dram["dbg_xmid"][t * 128:(t + 1) * 128, :], xb, reads=(kx,))
        for _ in range(4):
            nxt = next(tern_iter, None)
            if nxt is not None:
                tern_chunk(nxt)
    for nxt in tern_iter:
        tern_chunk(nxt)

    p.barrier()

    def ph4b_pre_dma(t):
        sl4 = t % 4
        xm, kxm = xm_in[sl4], f"xm{sl4}"
        p.dma("sp", "qsp", xm, xmid_d[t * 128:(t + 1) * 128, :],
              reads=(f"xmd{t}",), writes=(kxm,))

    def ph4b_pre(t):
        sl4 = t % 4
        xm, kxm = xm_in[sl4], f"xm{sl4}"
        pb = t % 2
        xqtb, kqt = xqtm[pb], f"xqtm{pb}"
        st = smal[:, 60 + 12 * pb: 72 + 12 * pb]
        kst = f"st4_{pb}"
        b0 = 84 + 4 * pb
        p.op("dve", lambda e, st=st, xm=xm: e.bn_stats(st[:, 0:6], xm[:, 0:512]),
             reads=(kxm,), writes=(kst,))
        p.op("dve", lambda e, st=st, xm=xm: e.bn_stats(st[:, 6:12], xm[:, 512:1024]),
             reads=(kxm,), writes=(kst,))
        p.op("dve", lambda e, st=st, b0=b0: e.bn_aggr(smal[:, b0:b0 + 2], st[:, 0:12]),
             reads=(kst,), writes=(f"mv4{pb}",))
        p.act_set(AFT.Sqrt)
        p.op("act", lambda e, b0=b0: e.activation(sc(b0 + 2), sc(b0 + 1), AFT.Sqrt,
                                                  bias=ceps, scale=1.0),
             reads=(f"mv4{pb}", "ceps"), writes=(f"r4{pb}",))
        p.op("dve", lambda e, b0=b0: e.reciprocal(sc(b0 + 2), sc(b0 + 2)),
             reads=(f"r4{pb}",), writes=(f"r4{pb}",))
        p.op("dve", lambda e, b0=b0: e.tensor_scalar(sc(b0 + 3), sc(b0), sc(b0 + 2), -1.0,
                                                     ALU.mult, ALU.mult),
             reads=(f"mv4{pb}", f"r4{pb}"), writes=(f"n4{pb}",))
        p.act_set(AFT.Identity)
        p.op("act", lambda e, xm=xm, b0=b0: e.activation(xl2b, xm, AFT.Identity,
             bias=sc(b0 + 3), scale=sc(b0 + 2)),
             reads=(kxm, f"r4{pb}", f"n4{pb}"), writes=("xl2b",))
        cq = 92 + 3 * pb
        p.op("dve", lambda e, cq=cq: e.reduce_max(sc(cq), xl2b, AX.X,
                                                  apply_absolute_value=True),
             reads=("xl2b",), writes=(f"cmax4_{pb}",))
        p.op("dve", lambda e, cq=cq: e.tensor_single_scalar(sc(cq), sc(cq), 1e-5, ALU.max),
             reads=(f"cmax4_{pb}",), writes=(f"cmax4_{pb}",))
        p.op("dve", lambda e, cq=cq: e.tensor_scalar(sc(cq + 1), sc(cq), sc(C_MA1),
                                                     1.0 / 127.0, ALU.mult, ALU.mult),
             reads=(f"cmax4_{pb}", "cma0"), writes=(f"cisx4_{pb}",))
        p.op("dve", lambda e, cq=cq: e.reciprocal(sc(cq + 2), sc(cq)),
             reads=(f"cmax4_{pb}",), writes=(f"csx4_{pb}",))
        p.op("dve", lambda e, cq=cq: e.tensor_single_scalar(sc(cq + 2), sc(cq + 2), 127.0,
                                                            ALU.mult),
             reads=(f"csx4_{pb}",), writes=(f"csx4_{pb}",))
        p.op("dve", lambda e, cq=cq, pb=pb: e.tensor_scalar(
            xqi[pb], xl2b, sc(cq + 2), 1000.5, ALU.mult, ALU.add),
            reads=("xl2b", f"csx4_{pb}"), writes=(kqt,))
        p.op("dve", lambda e, cq=cq, pb=pb, xqtb=xqtb: e.tensor_scalar(
            xqtb, xqi[pb], 1000.0, sc(cq + 1), ALU.subtract, ALU.mult),
            reads=(kqt, f"cisx4_{pb}"), writes=(kqt,))
    def ph4b_tr_a(t):
        pb = t % 2
        xqtb, kqt = xqtm[pb], f"xqtm{pb}"
        for dc in range(4):
            p.op("pe", lambda e, dc=dc, xqtb=xqtb: e.transpose(
                psT[:, dc * 128: (dc + 1) * 128],
                xqtb[:, dc * 128:(dc + 1) * 128], ident),
                reads=(kqt, "ident"), writes=("bT",))

    def ph4b_tr(t):
        sl4 = t % 4
        pb = t % 2
        xqtb, kqt = xqtm[pb], f"xqtm{pb}"
        xs, kxs = xqs[sl4], f"xqs{sl4}"
        for dc in range(4, 8):
            p.op("pe", lambda e, dc=dc, xqtb=xqtb: e.transpose(
                psT[:, dc * 128: (dc + 1) * 128],
                xqtb[:, dc * 128:(dc + 1) * 128], ident),
                reads=(kqt, "ident"), writes=("bT",))
        d1 = fdims(xs[:, 1:2], [[132, 8], [1, 128]])
        p.op("dve", lambda e, d1=d1: e.tensor_copy(d1, psT[:, 0:1024]),
             reads=(), writes=(kxs, "bT"))
        if t == 0:
            z0 = fdims(xs[:, 0:1], [[132, 8]])
            p.op("dve", lambda e, z0=z0: e.memset(z0, 0.0), writes=(kxs,))
        else:
            slp = (t - 1) % 4
            dsth = fdims(xqs[slp][:, 129:130], [[132, 8]])
            srch = fdims(xs[:, 1:2], [[132, 8]])
            p.op("dve", lambda e, dsth=dsth, srch=srch: e.tensor_copy(dsth, srch),
                 reads=(kxs,), writes=(f"xqs{slp}",))
            dstl = fdims(xs[:, 0:1], [[132, 8]])
            srcl = fdims(xqs[slp][:, 128:129], [[132, 8]])
            p.op("dve", lambda e, dstl=dstl, srcl=srcl: e.tensor_copy(dstl, srcl),
                 reads=(f"xqs{slp}",), writes=(kxs,))
        if t == nt - 1:
            z1 = fdims(xs[:, 129:130], [[132, 8]])
            p.op("dve", lambda e, z1=z1: e.memset(z1, 0.0), writes=(kxs,))
        if dbg:
            dsrc = fdims(xs[:, 1:2], [[132, 8], [1, 128]])
            ddst = fdims(dram["dbg_xqf"][:, t * 128: t * 128 + 1], [[SEQ, 8], [1, 128]])
            p.dma("sp", "qsp", ddst, dsrc, reads=(kxs,))

    def quant_group(t, g):
        hqb, hqi, khq = hqg[g % 2], hqg_i16[g % 2], f"hqg{g % 2}"
        h2g = h2[:, g * 1024:(g + 1) * 1024]
        kh2 = f"h2g{g}"
        srep = fdims(sbc[:, 0:1], [[0, 8], [1, 128]])
        p.op("dve", lambda e, h2g=h2g, srep=srep: e.tensor_tensor(h2g, h2g, srep, ALU.mult),
             reads=("sbc", kh2), writes=(kh2,))
        p.op("dve", lambda e, h2g=h2g, hqi=hqi: e.tensor_single_scalar(hqi, h2g, 1000.5,
                                                                       ALU.add),
             reads=(kh2,), writes=(khq,))
        p.op("dve", lambda e, hqb=hqb, hqi=hqi: e.tensor_single_scalar(hqb, hqi, 1000.0,
                                                                       ALU.subtract),
             reads=(khq,), writes=(khq,))
        if dbg and t == 0:
            p.dma("sp", "qsp", dram["dbg_h2"][:, g * 1024:(g + 1) * 1024], h2g, reads=(kh2,))
            p.dma("sp", "qsp", dram["dbg_hq"][:, g * 1024:(g + 1) * 1024], hqb, reads=(khq,))

    def w2_group(g):
        hqb, khq = hqg[g % 2], f"hqg{g % 2}"
        for mj in range(8):
            m = g * 8 + mj
            for hf in range(2):
                p.op("pe", lambda e, m=m, mj=mj, hf=hf, hqb=hqb: e.matmul(
                    psC[:, hf * 512:(hf + 1) * 512], hqb[:, mj * 128:(mj + 1) * 128],
                    w2q[:, m * DM + hf * 512: m * DM + (hf + 1) * 512],
                    start=(m == 0), stop=(m == NM - 1)),
                    reads=(khq, "w2q"), writes=("bC0", "bC1"))

    def stage1(t, fill=None):
        xs, kxs = xqs[t % 4], f"xqs{t % 4}"
        w1banks = [(psA, "bA"), (psB, "bB"), (psE, "bE"), (psF, "bF")]
        for m in range(NM):
            psh, kh = w1banks[m % 4]
            hb, kb = hsb[m % 2], f"hsb{m % 2}"
            acb, kc = accb[m % 2], f"acc{m % 2}"
            for dc in range(8):
                p.op("pe", lambda e, m=m, psh=psh, xs=xs, dc=dc: e.matmul(
                    psh[:, 0:130],
                    w1q[:, dc * HID + m * 128: dc * HID + (m + 1) * 128],
                    xs[:, dc * 132: dc * 132 + 130], start=(dc == 0), stop=(dc == 7)),
                    reads=("w1q", kxs), writes=(kh,))
            if m % 2 == 0:
                p.act_set(AFT.Copy)
                p.op("act", lambda e, psh=psh, hb=hb: e.activation(
                    hb, psh[:, 0:130], AFT.Copy, bias=0.0, scale=1.0),
                    reads=(), writes=(kb, kh))
            else:
                p.op("dve", lambda e, psh=psh, hb=hb: e.tensor_copy(hb, psh[:, 0:130]),
                     reads=(), writes=(kb, kh))
            p.op("dve", lambda e, hb=hb, acb=acb, m=m: e.tensor_single_scalar(
                acb, hb[:, 0:128], cw_h[:, m:m + 1], ALU.mult),
                reads=(kb, "cw_h"), writes=(kc,))
            p.op("dve", lambda e, hb=hb, acb=acb, m=m: e.scalar_tensor_tensor(
                acb, hb[:, 1:129], cw_h[:, 32 + m:33 + m], acb, ALU.mult, ALU.add),
                reads=(kb, kc, "cw_h"), writes=(kc,))
            p.op("dve", lambda e, hb=hb, acb=acb, m=m: e.scalar_tensor_tensor(
                acb, hb[:, 2:130], cw_h[:, 64 + m:65 + m], acb, ALU.mult, ALU.add),
                reads=(kb, kc, "cw_h"), writes=(kc,))
            p.act_set(AFT.Silu)
            p.op("act", lambda e, acb=acb, m=m: e.activation(
                h2[:, m * 128:(m + 1) * 128], acb, AFT.Silu, bias=cb_h[:, m:m + 1], scale=1.0),
                reads=(kc, "cb_h"), writes=(f"h2g{m // 8}",))
            if fill is not None and m in (7, 15, 23, 31):
                fill(m // 8)

    def h2_partial(t, b):
        rin = fdims(h2[:, b * 1024: b * 1024 + 1], [[1, 128], [128, 8]])
        dst = pmax if b == 0 else sbcp
        kdst = "pmax" if b == 0 else "sbcp"
        p.op("dve", lambda e, rin=rin, dst=dst: e.tensor_reduce(
            dst, rin, AX.X, ALU.max, apply_absolute_value=True),
            reads=(f"h2g{b}",), writes=(kdst,))
        if b > 0:
            p.op("dve", lambda e: e.tensor_tensor(pmax, pmax, sbcp, ALU.max),
                 reads=("pmax", "sbcp"), writes=("pmax",))

    def qdance_p1(t):
        p.op("pe", lambda e: e.transpose(psD[:, 0:128], pmax, identf),
             reads=("pmax", "identf"), writes=("bD",))
        p.op("dve", lambda e: e.reduce_max(sc(C_MAX), psD[:, 0:128], AX.X),
             reads=(), writes=("cmax5", "bD"))
        p.op("dve", lambda e: e.tensor_single_scalar(sc(C_MAX), sc(C_MAX), 1e-5, ALU.max),
             reads=("cmax5",), writes=("cmax5",))
        cfc = C_F if t % 2 == 0 else C_ISX
        p.op("dve", lambda e, cfc=cfc: e.tensor_scalar(sc(cfc), sc(C_MAX), sc(C_MA2),
                                                       1.0 / 127.0, ALU.mult, ALU.mult),
             reads=("cmax5", "cma1"), writes=(f"cf{t % 2}",))
        p.op("dve", lambda e: e.reciprocal(sc(C_SX), sc(C_MAX)),
             reads=("cmax5",), writes=("csx5",))
        p.op("dve", lambda e: e.tensor_single_scalar(sc(C_SX), sc(C_SX), 127.0, ALU.mult),
             reads=("csx5",), writes=("csx5",))

    def qdance_p2(t):
        p.op("pe", lambda e: e.transpose(psD[0:1, 128:256], sc(C_SX), identf),
             reads=("csx5", "identf"), writes=("bD",))
        p.op("dve", lambda e: e.tensor_copy(srow_full[0:1, :], psD[0:1, 128:256]),
             reads=(), writes=("srow", "bD"))

    def qdance_p3(t):
        p.op("pe", lambda e: e.matmul(psD[:, 256:384], onescf, srow_full[0:1, :],
                                      start=True, stop=True),
             reads=("srow", "onescf"), writes=("bD",))
        p.op("dve", lambda e: e.tensor_copy(sbc, psD[:, 256:384]),
             reads=(), writes=("sbc", "bD"))
        quant_group(t, 0)
        quant_group(t, 1)

    def mk_fill(t):
        def fill(b):
            h2_partial(t, b)
            if t > 0:
                if b == 1:
                    w2_group(0)
                    quant_group(t - 1, 2)
                elif b == 2:
                    w2_group(1)
                    quant_group(t - 1, 3)
                elif b == 3:
                    w2_group(2)
                    w2_group(3)
            if b == 2 and t + 2 < nt:
                ph4b_pre(t + 2)
        return fill

    def stage2b(t):
        p.op("dve", lambda e: e.bn_stats(stat6[:, 0:6], psC[:, 0:512]),
             reads=(), writes=("stat6", "bC0"))
        p.op("dve", lambda e: e.bn_stats(stat6[:, 6:12], psC[:, 512:1024]),
             reads=(), writes=("stat6", "bC1"))
        p.op("dve", lambda e: e.bn_aggr(smal[:, C_MEAN:C_VAR + 1], stat6[:]),
             reads=("stat6",), writes=("c_mv",))
        cfc = C_F if t % 2 == 0 else C_ISX
        p.op("dve", lambda e, cfc=cfc: e.tensor_scalar(sc(C_VT2), sc(C_VAR), sc(cfc),
                                                       sc(cfc), ALU.mult, ALU.mult),
             reads=("c_mv", f"cf{t % 2}"), writes=("cvt2",))
        p.act_set(AFT.Sqrt)
        p.op("act", lambda e: e.activation(sc(C_R), sc(C_VT2), AFT.Sqrt, bias=ceps, scale=1.0),
             reads=("cvt2", "ceps"), writes=("c_r",))
        p.op("dve", lambda e: e.reciprocal(sc(C_R), sc(C_R)), reads=("c_r",), writes=("c_r",))
        p.op("dve", lambda e, cfc=cfc: e.tensor_scalar(sc(C_S4), sc(C_R), sc(cfc), None,
                                                       ALU.mult),
             reads=("c_r", f"cf{t % 2}"), writes=("cs4",))
        p.op("dve", lambda e: e.tensor_scalar(sc(C_B4), sc(C_MEAN), sc(C_S4), -1.0,
                                              ALU.mult, ALU.mult),
             reads=("c_mv", "cs4"), writes=("cb4",))
        p.act_set(AFT.Identity)
        p.op("act", lambda e: e.activation(outb, psC[:], AFT.Identity,
                                           bias=sc(C_B4), scale=sc(C_S4)),
             reads=("cs4", "cb4"), writes=("outb", "bC0", "bC1"))
        p.op("dve", lambda e, t=t: e.tensor_tensor(outb, outb, xm_in[t % 4], ALU.add),
             reads=("outb", f"xm{t % 4}"), writes=("outb",))
        p.dma("sp", "qsp", dram["out"][t * 128:(t + 1) * 128, :], outb, reads=("outb",))

    ph4b_pre_dma(0)
    ph4b_pre(0)
    if nt > 1:
        ph4b_pre_dma(1)
        ph4b_pre(1)
    ph4b_tr_a(0)
    ph4b_tr(0)
    if nt > 1:
        if nt > 2:
            ph4b_pre_dma(2)
        ph4b_tr_a(1)
        ph4b_tr(1)
    for t in range(nt + 1):
        if t < nt:
            stage1(t, fill=mk_fill(t))
        else:
            w2_group(0)
            quant_group(t - 1, 2)
            w2_group(1)
            quant_group(t - 1, 3)
            w2_group(2)
            w2_group(3)
        if t < nt:
            qdance_p1(t)
        if t + 2 < nt:
            ph4b_tr_a(t + 2)
        if t < nt:
            qdance_p2(t)
        if t + 2 < nt:
            ph4b_tr(t + 2)
        if t < nt:
            qdance_p3(t)
        if 0 < t:
            stage2b(t - 1)
        if t + 3 < nt:
            ph4b_pre_dma(t + 3)

    p.finalize()
    return nc, es


_BUILD_CACHE = {}


def _get_program(nt, dbg=False):
    key = (nt, dbg)
    if key not in _BUILD_CACHE:
        _BUILD_CACHE[key] = build_program(nt, dbg, strict_eng="dve")
    return _BUILD_CACHE[key]


def host_inputs(x_core, Wq, bq, Wk, bk, Wv, bv, w1, conv_w, conv_b, w2, nt=16):
    f32 = np.float32
    bf = ml_dtypes.bfloat16
    d = {}
    d["x_in"] = np.ascontiguousarray(x_core, dtype=f32)
    wqt = np.zeros((DM, 512), f32)
    wkt = np.zeros((DM, 512), f32)
    bqp = np.zeros((128, 4), f32)
    bkp = np.zeros((128, 4), f32)
    for h in range(H):
        pg, sl = h // 4, h % 4
        wqt[:, pg * 128 + sl * 32: pg * 128 + sl * 32 + 16] = Wq[h * 16:(h + 1) * 16, :].T
        wkt[:, pg * 128 + sl * 32: pg * 128 + sl * 32 + 16] = Wk[h * 16:(h + 1) * 16, :].T
        bqp[sl * 32: sl * 32 + 16, pg] = bq[h * 16:(h + 1) * 16]
        bkp[sl * 32: sl * 32 + 16, pg] = bk[h * 16:(h + 1) * 16]
    d["wqtp"] = wqt
    d["wktp"] = wkt
    d["bqp"] = bqp
    d["bkp"] = bkp
    d["wvt"] = np.ascontiguousarray(Wv.T, dtype=f32)
    d["bv_rep"] = np.ascontiguousarray(np.broadcast_to(bv, (128, DM)), dtype=f32)
    d["w1t"] = np.ascontiguousarray(w1.T, dtype=f32)
    d["w2t"] = np.ascontiguousarray(w2.T, dtype=f32)
    cw = np.zeros((128, 96), f32)
    for k in range(3):
        cw[:, k * 32:(k + 1) * 32] = conv_w[:, 0, k].reshape(NM, 128).T
    d["cw_h"] = cw
    d["cb_h"] = np.ascontiguousarray(conv_b.reshape(NM, 128).T, dtype=f32)
    d["ident"] = np.eye(128, dtype=bf)
    pp, qq = np.arange(128)[:, None], np.arange(128)[None, :]
    m_prev = (pp - qq >= 65).astype(bf)
    dqp = qq - pp
    m_diag = ((dqp >= 0) & (dqp <= 63)).astype(bf)
    m01a = (qq >= pp).astype(np.float32)
    d["m01ea"] = np.ascontiguousarray(
        np.concatenate([m_prev.astype(np.float32), m_diag.astype(np.float32), m01a], axis=1)).astype(bf)
    d["m01ah"] = (0.5 * m01a).astype(bf)
    d["onescf"] = np.ones((1, 128), f32)
    d["identf"] = np.eye(128, dtype=f32)
    d["one11f"] = np.ones((1, 1), f32)
    d["onecol"] = np.ones((128, 1), f32)
    return d


def kernel(L0_attn_x, L0_attn_Wq, L0_attn_bq, L0_attn_Wk, L0_attn_bk, L0_attn_Wv,
           L0_attn_bv, L0_ffn_w1, L0_ffn_conv_w, L0_ffn_conv_b, L0_ffn_w2):
    x = np.asarray(L0_attn_x)
    nt = x.shape[1] // 128
    nc, _es = _get_program(nt)
    in_maps = []
    for c in range(x.shape[0]):
        in_maps.append(host_inputs(
            x[c], np.asarray(L0_attn_Wq), np.asarray(L0_attn_bq), np.asarray(L0_attn_Wk),
            np.asarray(L0_attn_bk), np.asarray(L0_attn_Wv), np.asarray(L0_attn_bv),
            np.asarray(L0_ffn_w1), np.asarray(L0_ffn_conv_w), np.asarray(L0_ffn_conv_b),
            np.asarray(L0_ffn_w2), nt=nt))
    res = run_bass_kernel_spmd(nc, in_maps, core_ids=list(range(x.shape[0])))
    return np.stack([np.asarray(r["out"]) for r in res.results], axis=0)


# version 23 (final)
# speedup vs baseline: 1.0428x; 1.0001x over previous
import os
import numpy as np
import ml_dtypes
from contextlib import ExitStack

import concourse.bass as bass
import concourse.mybir as mybir
from concourse.alu_op_type import AluOpType as ALU
from concourse.bass_utils import run_bass_kernel_spmd

F32 = mybir.dt.float32
BF16 = mybir.dt.bfloat16
I16 = mybir.dt.int16
AFT = mybir.ActivationFunctionType
AX = mybir.AxisListType

H, DP, DH, DT = 16, 16, 64, 273
DM, HID = 1024, 4096
NM = HID // 128
SQH = 2.0 ** (-0.25)
SQHI = 2.0 ** (0.25)
EPS = 1e-5
NCOL = 52400

ENGS = ("pe", "dve", "act", "pool", "sp")
QUEUES = ("qsp", "qact", "qpool")


def fdims(ap, dims, extra_off=0):
    return bass.AP(ap.tensor, ap.offset + extra_off,
                   [list(ap.ap[0])] + [list(d) for d in dims])


class Prog:
    RSLOT = 8

    def __init__(self, nc, block, strict=False, strict_dma=False, strict_eng=""):
        self.strict = strict
        self.strict_dma = strict_dma
        self.strict_eng = strict_eng
        self.sems = {}
        self.qsems = {}
        self._ctx = ExitStack()
        for name in ENGS:
            self.sems[name] = self._ctx.enter_context(nc.semaphore("sem_" + name))
        for q in QUEUES:
            self.qsems[q] = [self._ctx.enter_context(nc.semaphore(f"sem_{q}_{i}"))
                             for i in range(self.RSLOT)]
        self.cnt = {name: 0 for name in ENGS}
        self.qcnt = {q: 0 for q in QUEUES}
        self.last_w = {}
        self.readers = {}
        self.hw = {}
        self.secs = {"pe": block.tensor, "dve": block.vector, "act": block.scalar,
                     "pool": block.gpsimd, "sp": block.sync}
        self.nops = 0
        self.maxops = int(os.environ.get("KMAXOPS", "0"))
        self.act_cur = None
        from concourse.hw_specs import get_activation_tables
        self._act_tables = list(get_activation_tables(nc.m.arch).values())

    def act_set(self, func):
        for idx, funcs in enumerate(self._act_tables):
            if func in funcs:
                break
        else:
            raise ValueError(func)
        if self.act_cur is not None and func in self._act_tables[self.act_cur]:
            return
        self.act_cur = idx
        self.op("act", lambda e: e.preload_act_table(func))

    def _qwait(self, q, idx):
        slot = (idx - 1) % self.RSLOT
        uses = (idx - 1) // self.RSLOT + 1
        return self.qsems[q][slot], 16 * uses, (q, slot)

    def _edge_wait(self, eng, dom, val, waits):
        if dom in QUEUES:
            sem, tgt, hwkey = self._qwait(dom, val)
        else:
            sem, tgt, hwkey = self.sems[dom], val, dom
        k = (eng, hwkey)
        if self.hw.get(k, 0) < tgt:
            self.hw[k] = tgt
            waits.append((sem, tgt))

    def op(self, eng, fn, reads=(), writes=(), dom=None):
        if self.maxops and self.nops >= self.maxops:
            return 0
        dom = dom or eng
        is_dma = dom in QUEUES

        def keep(d):
            if d != dom:
                return True
            if self.strict:
                return True
            if d in QUEUES:
                return self.strict_dma
            return d in self.strict_eng

        edges = []
        for k in reads:
            lw = self.last_w.get(k)
            if lw is not None and keep(lw[0]):
                edges.append(lw)
        for k in writes:
            lw = self.last_w.get(k)
            if lw is not None and keep(lw[0]):
                edges.append(lw)
            for rdom, rval in self.readers.get(k, {}).items():
                if keep(rdom):
                    edges.append((rdom, rval))
        waits = []
        for d, v in edges:
            self._edge_wait(eng, d, v, waits)
        if is_dma:
            self.qcnt[dom] += 1
            val = self.qcnt[dom]
            sem, tgt, hwkey = self._qwait(dom, val)
            k = (eng, hwkey)
            if tgt - 16 > 0 and self.hw.get(k, 0) < tgt - 16:
                self.hw[k] = tgt - 16
                waits.append((sem, tgt - 16))
            inc_sem, inc_amt = sem, 16
        else:
            self.cnt[dom] += 1
            val = self.cnt[dom]
            inc_sem, inc_amt = self.sems[dom], 1
            if self.strict and val > 1:
                waits.append((inc_sem, val - 1))

        def body(e):
            for sm, v in waits:
                e.wait_ge(sm, v)
            fn(e).then_inc(inc_sem, inc_amt)

        self.secs[eng](body)
        self.nops += 1
        for k in reads:
            self.readers.setdefault(k, {})[dom] = val
        for k in writes:
            self.last_w[k] = (dom, val)
            self.readers[k] = {}
        return val

    def dma(self, eng, queue, out, in_, reads=(), writes=()):
        return self.op(eng, lambda e: e.dma_start(out=out, in_=in_),
                       reads=reads, writes=writes, dom=queue)

    def _all_final(self):
        fin = []
        for name in ENGS:
            if self.cnt[name] > 0:
                fin.append((name, self.sems[name], self.cnt[name], name))
        for q in QUEUES:
            n = self.qcnt[q]
            for slot in range(self.RSLOT):
                uses = (n - slot + self.RSLOT - 1) // self.RSLOT if n > slot else 0
                uses = len(range(slot, n, self.RSLOT))
                if uses > 0:
                    fin.append(((q, slot), self.qsems[q][slot], 16 * uses, (q, slot)))
        return fin

    def barrier(self):
        fin = self._all_final()
        for eng in ENGS:
            waits = []
            for hwkey, sem, v, _ in fin:
                if hwkey == eng:
                    continue
                if self.hw.get((eng, hwkey), 0) < v:
                    self.hw[(eng, hwkey)] = v
                    waits.append((sem, v))
            def body(e, waits=waits):
                for sm, v in waits:
                    e.wait_ge(sm, v)

            self.secs[eng](body)

    def finalize(self):
        fin = self._all_final()

        def body(e):
            for hwkey, sem, v, _ in fin:
                if hwkey == "sp":
                    continue
                e.wait_ge(sem, v)

        self.secs["sp"](body)


class Arena:
    def __init__(self, nc):
        self.h = nc.alloc_sbuf_tensor("arena", [128, NCOL], F32)
        self.A32 = self.h[:, :]
        self.A16 = self.h[:, :].bitcast(BF16)
        self.AI16 = self.h[:, :].bitcast(I16)
        self.cur = 0
        self.limit = NCOL

    def region(self, base, limit):
        self.cur = base
        self.limit = limit

    def f32(self, n):
        ap = self.A32[:, self.cur:self.cur + n]
        self.cur += n
        assert self.cur <= self.limit, f"arena overflow {self.cur} > {self.limit}"
        return ap

    def bf16(self, n16):
        assert n16 % 2 == 0
        ap = self.A16[:, 2 * self.cur: 2 * self.cur + n16]
        self.cur += n16 // 2
        assert self.cur <= self.limit, f"arena overflow {self.cur} > {self.limit}"
        return ap

    def i16(self, n16):
        assert n16 % 2 == 0
        ap = self.AI16[:, 2 * self.cur: 2 * self.cur + n16]
        self.cur += n16 // 2
        assert self.cur <= self.limit, f"arena overflow {self.cur} > {self.limit}"
        return ap


def build_program(nt=16, dbg=False, strict=False, detect=False, strict_dma=False, strict_eng=""):
    SEQ = nt * 128
    NW = min(512, SEQ)
    NG = SEQ // NW
    nc = bass.Bass("TRN2", target_bir_lowering=False, debug=False,
                   detect_race_conditions=detect)

    dram = {}
    for name, shape, dt in [
        ("x_in", [SEQ, DM], F32), ("wqtp", [DM, 512], F32), ("wktp", [DM, 512], F32),
        ("wvt", [DM, DM], F32), ("bqp", [128, 4], F32), ("bkp", [128, 4], F32),
        ("bv_rep", [128, DM], F32), ("w1t", [DM, HID], F32), ("w2t", [HID, DM], F32),
        ("cw_h", [128, 96], F32), ("cb_h", [128, 32], F32), ("ident", [128, 128], BF16),
        ("m01ea", [128, 384], BF16), ("m01ah", [128, 128], BF16),
        ("onescf", [1, 128], F32), ("one11f", [1, 1], F32), ("onecol", [128, 1], F32),
        ("identf", [128, 128], F32),
    ]:
        dram[name] = nc.dram_tensor(name, shape, dt, kind="ExternalInput").ap()
    dram["out"] = nc.dram_tensor("out", [SEQ, DM], F32, kind="ExternalOutput").ap()
    xmid_d = nc.dram_tensor("xmid_sc", [SEQ, DM], F32, kind="Internal").ap()
    if dbg:
        for name, shape, dt in [("dbg_attn", [SEQ, DM], F32), ("dbg_xmid", [SEQ, DM], F32),
                                ("dbg_xqf", [128, 8 * SEQ], BF16),
                                ("dbg_h2", [128, HID], F32), ("dbg_hq", [128, HID], BF16)]:
            dram[name] = nc.dram_tensor(name, shape, dt, kind="ExternalOutput").ap()

    ar = Arena(nc)
    ar.region(0, 1984)
    ident = ar.bf16(128)
    m01ea = ar.bf16(384)
    m01ah = ar.bf16(128)
    onescf = ar.f32(128)[0:1, :]
    one11f = ar.f32(2)[0:1, 0:1]
    onecol = ar.f32(1)
    bqp = ar.f32(4)
    bkp = ar.f32(4)
    cw_h = ar.f32(96)
    cb_h = ar.f32(32)
    smal = ar.f32(100)
    s1x1 = ar.f32(64)[0:1, :]
    wsum1 = ar.f32(32)
    wsum2 = ar.f32(32)
    stat6 = ar.f32(12)
    identf = ar.f32(128)
    ceps = ar.f32(1)
    czero = ar.f32(1)
    c1e8 = ar.f32(1)
    csqh = ar.f32(1)
    bv_rep = ar.f32(1024)
    sbcp = bv_rep[:, 0:128]
    pmax = bv_rep[:, 128:256]

    C_MEAN, C_VAR, C_R, C_NMR = 0, 1, 2, 3
    C_MA1, C_SC1, C_MA2, C_SC2 = 4, 5, 6, 7
    C_MAX, C_ISX, C_F, C_S4, C_B4, C_VT2, C_SX = 8, 9, 10, 11, 12, 13, 14
    C_T = 16

    def sc(c):
        return smal[:, c:c + 1]

    B1, L1 = 1984, 11050
    B2, L2 = 11050, 27434
    B3, L3 = 27434, 35754
    B4, L4 = 35754, 52138

    ar.region(B2, L2)
    qfm4 = ar.bf16(4 * SEQ)
    kfm4 = ar.bf16(4 * SEQ)
    qtm4s = ar.bf16(4 * SEQ)
    ktm4s = ar.bf16(4 * SEQ)
    ar.region(B2, L2)
    w1q = ar.bf16(8 * HID)

    ar.region(B3, L3)
    vt = ar.bf16(nt * 1040)
    ar.region(B3, L3)
    wbuf = [ar.f32(1024) for _ in range(3)]
    sgb = [ar.bf16(1024) for _ in range(3)]
    dbb = [ar.bf16(1024) for _ in range(3)]
    ar.region(B3, L3)
    h2 = ar.f32(4096)
    xqs = [ar.bf16(8 * 132) for _ in range(4)]

    ar.region(B4, L4)
    attn = ar.f32(nt * DM)
    ar.region(B4, L4)
    w2q = ar.bf16(NM * DM)
    ar.region(B4, L4)
    wqk_sb = ar.bf16(8 * 512)
    wvt_sb = ar.bf16(8 * DM)
    wstg = [ar.f32(1024), ar.f32(1024)]
    x_t = [ar.f32(1024), ar.f32(1024)]
    xln_tm = [ar.bf16(1024), ar.bf16(1024)]

    ar.region(B1, L1)
    xlnf = ar.bf16(8 * SEQ)
    ar.region(B1, L1)
    phk = [ar.bf16(nt * 280 + 104), ar.bf16(nt * 280 + 104)]
    phqr = [ar.bf16(2 * 384), ar.bf16(2 * 384)]
    phq_tm = [ar.bf16(280), ar.bf16(280)]
    e_sb = [ar.bf16(384) for _ in range(4)]
    s_sb = [ar.f32(195), ar.f32(195)]
    s16 = [ar.bf16(196), ar.bf16(196)]
    tloc = [bv_rep[:, 256:320], bv_rep[:, 320:384]]
    w1stg = [ar.f32(1024), ar.f32(1024)]
    ar.region(B1, L1)
    x_t4 = [ar.f32(1024), ar.f32(1024)]
    ln3b = ar.f32(1024)
    ar.region(B1, L1)
    xm_in = [ar.f32(1024) for _ in range(4)]
    xl2b = ar.f32(1024)
    xqtm = [ar.bf16(1024), ar.bf16(1024)]
    hsb = [ar.f32(130), ar.f32(130)]
    accb = [ar.f32(128), ar.f32(128)]
    hqg = [ar.bf16(1024), ar.bf16(1024)]
    sbc = ar.f32(128)
    srow_full = ar.f32(128)
    outb = ar.f32(1024)

    hqg_i16 = [bass.AP(ar.AI16.tensor, q.offset, [list(q.ap[0])] + [list(d) for d in q.ap[1:]])
               for q in hqg]
    xqi = [bass.AP(ar.AI16.tensor, q.offset, [list(q.ap[0])] + [list(d) for d in q.ap[1:]])
           for q in xqtm]

    es = ExitStack()
    psT = es.enter_context(nc.psum_tensor("psT", [128, 1024], BF16))
    psA = es.enter_context(nc.psum_tensor("psA", [128, 512], F32))
    psB = es.enter_context(nc.psum_tensor("psB", [128, 512], F32))
    psC = es.enter_context(nc.psum_tensor("psC", [128, 1024], F32))
    psD = es.enter_context(nc.psum_tensor("psD", [128, 512], F32))
    psE = es.enter_context(nc.psum_tensor("psE", [128, 512], F32))
    psF = es.enter_context(nc.psum_tensor("psF", [128, 512], F32))
    block = es.enter_context(nc.Block())
    p = Prog(nc, block, strict=strict, strict_dma=strict_dma, strict_eng=strict_eng)

    p.op("dve", lambda e: e.memset(ceps, EPS), writes=("ceps",))
    p.op("dve", lambda e: e.memset(czero, 0.0), writes=("czero",))
    p.op("dve", lambda e: e.memset(c1e8, 1e-8), writes=("c1e8",))
    p.op("dve", lambda e: e.memset(csqh, 2.0 ** -0.5), writes=("csqh",))
    for name, dst in [("ident", ident), ("m01ea", m01ea), ("m01ah", m01ah),
                      ("onescf", onescf), ("one11f", one11f), ("onecol", onecol),
                      ("bqp", bqp), ("bkp", bkp), ("bv_rep", bv_rep),
                      ("cw_h", cw_h), ("cb_h", cb_h), ("identf", identf)]:
        p.dma("sp", "qsp", dst, dram[name], writes=(name,))

    def ln_stats(src, rkeys, pb=0):
        b0 = 24 + 8 * pb
        st = stat6 if pb == 0 else smal[:, 48:60]
        kst = f"stat6_{pb}"
        p.op("dve", lambda e, st=st: e.bn_stats(st[:, 0:6], src[:, 0:512]),
             reads=rkeys, writes=(kst,))
        p.op("dve", lambda e, st=st: e.bn_stats(st[:, 6:12], src[:, 512:1024]),
             reads=rkeys, writes=(kst,))
        p.op("dve", lambda e, st=st, b0=b0: e.bn_aggr(smal[:, b0:b0 + 2], st[:, 0:12]),
             reads=(kst,), writes=(f"c_mv{pb}",))
        p.act_set(AFT.Sqrt)
        p.op("act", lambda e, b0=b0: e.activation(sc(b0 + 2), sc(b0 + 1), AFT.Sqrt,
                                                  bias=ceps, scale=1.0),
             reads=(f"c_mv{pb}", "ceps"), writes=(f"c_r{pb}",))
        p.op("dve", lambda e, b0=b0: e.reciprocal(sc(b0 + 2), sc(b0 + 2)),
             reads=(f"c_r{pb}",), writes=(f"c_r{pb}",))
        p.op("dve", lambda e, b0=b0: e.tensor_scalar(sc(b0 + 3), sc(b0), sc(b0 + 2), -1.0,
                                                     ALU.mult, ALU.mult),
             reads=(f"c_mv{pb}", f"c_r{pb}"), writes=(f"c_nmr{pb}",))
        return b0

    for t in range(nt):
        xb, lb = x_t[t % 2], xln_tm[t % 2]
        kx, kl = f"x{t % 2}", f"xl{t % 2}"
        p.dma("sp", "qsp", xb, dram["x_in"][t * 128:(t + 1) * 128, :], writes=(kx,))
        b0 = ln_stats(xb, (kx,), pb=t % 2)
        p.act_set(AFT.Identity)
        p.op("act", lambda e, xb=xb, lb=lb, b0=b0, t=t: e.activation(lb, xb, AFT.Identity,
             bias=sc(b0 + 3), scale=sc(b0 + 2)),
             reads=(kx, f"c_r{t % 2}", f"c_nmr{t % 2}"), writes=(kl,))
        for dc in range(8):
            p.op("pe", lambda e, lb=lb, dc=dc: e.transpose(
                psT[:, dc * 128: dc * 128 + 128], lb[:, dc * 128:(dc + 1) * 128], ident),
                reads=(kl, "ident"), writes=("bT",))
        dx = fdims(xlnf[:, t * 128: t * 128 + 1], [[SEQ, 8], [1, 128]])
        p.op("dve", lambda e, dx=dx: e.tensor_copy(dx, psT[:, 0:1024]),
             reads=(), writes=("xlnf", "bT"))

    for wname, bt, bkey, dst, dkey in [("wqtp", bqp, "bqp", qfm4, "qfm4"),
                                       ("wktp", bkp, "bkp", kfm4, "kfm4")]:
        for dc in range(8):
            stg, ks = wstg[dc % 2], f"wstg{dc % 2}"
            p.dma("sp", "qsp", stg[:, 0:512], dram[wname][dc * 128:(dc + 1) * 128, :],
                  writes=(ks,))
            p.op("dve", lambda e, stg=stg, dc=dc: e.tensor_copy(
                wqk_sb[:, dc * 512:(dc + 1) * 512], stg[:, 0:512]),
                reads=(ks,), writes=("wqk",))
        for pg in range(4):
            for ng in range(NG):
                ps, kp = (psA, "bA") if ng % 2 == 0 else (psB, "bB")
                for dc in range(8):
                    p.op("pe", lambda e, pg=pg, ng=ng, dc=dc, ps=ps: e.matmul(
                        ps[:, 0:NW],
                        wqk_sb[:, dc * 512 + pg * 128: dc * 512 + (pg + 1) * 128],
                        xlnf[:, dc * SEQ + ng * NW: dc * SEQ + (ng + 1) * NW],
                        start=(dc == 0), stop=(dc == 7)),
                        reads=("wqk", "xlnf"), writes=(kp,))
                p.act_set(AFT.Identity)
                p.op("act", lambda e, pg=pg, ng=ng, ps=ps, bt=bt, dst=dst: e.activation(
                    dst[:, pg * SEQ + ng * NW: pg * SEQ + (ng + 1) * NW], ps[:, 0:NW],
                    AFT.Identity, bias=bt[:, pg:pg + 1], scale=1.0),
                    reads=(bkey,), writes=(dkey, kp))

    for dc in range(8):
        stg, ks = wstg[dc % 2], f"wstg{dc % 2}"
        p.dma("sp", "qsp", stg[:, 0:1024], dram["wvt"][dc * 128:(dc + 1) * 128, :],
              writes=(ks,))
        p.op("dve", lambda e, stg=stg, dc=dc: e.tensor_copy(
            wvt_sb[:, dc * DM:(dc + 1) * DM], stg[:, 0:1024]), reads=(ks,), writes=("wvt",))
    for t in range(nt):
        for dc in range(8):
            for hf in range(2):
                p.op("pe", lambda e, t=t, dc=dc, hf=hf: e.matmul(
                    psC[:, hf * 512:(hf + 1) * 512],
                    xlnf[:, dc * SEQ + t * 128: dc * SEQ + (t + 1) * 128],
                    wvt_sb[:, dc * DM + hf * 512: dc * DM + (hf + 1) * 512],
                    start=(dc == 0), stop=(dc == 7)),
                    reads=("wvt", "xlnf"), writes=("bC0", "bC1"))
        dstv = fdims(vt[:, t * 1040: t * 1040 + 1], [[65, 16], [1, 64]])
        p.op("dve", lambda e, dstv=dstv: e.tensor_tensor(dstv, psC[:], bv_rep, ALU.add),
             reads=("bv_rep",), writes=("vt", "bC0", "bC1"))
        ones_ap = fdims(vt[:, t * 1040 + 64: t * 1040 + 65], [[65, 16]])
        p.op("dve", lambda e, ones_ap=ones_ap: e.memset(ones_ap, 1.0), writes=("vt",))

    for src, skey, dst, dkey in [(qfm4, "qfm4", qtm4s, "qtm4s"), (kfm4, "kfm4", ktm4s, "ktm4s")]:
        for t in range(nt):
            for pg in range(4):
                p.op("pe", lambda e, src=src, pg=pg, t=t: e.transpose(
                    psT[:, pg * 128: (pg + 1) * 128],
                    src[:, pg * SEQ + t * 128: pg * SEQ + (t + 1) * 128], ident),
                    reads=(skey, "ident"), writes=("bT",))
            p.act_set(AFT.Copy)
            p.op("act", lambda e, t=t, dst=dst: e.activation(
                dst[:, t * 512:(t + 1) * 512], psT[:, 0:512],
                AFT.Copy, bias=0.0, scale=SQH),
                reads=(), writes=(dkey, "bT"))

    p.barrier()

    for hh in range(2):
        for t in range(nt):
            p.op("dve", lambda e, hh=hh, t=t: e.memset(
                phk[hh][:, t * 280: t * 280 + 1], 1.0), writes=(f"phk{hh}",))
        p.op("dve", lambda e, hh=hh: e.memset(phk[hh][:, nt * 280: nt * 280 + 104], 0.0),
             writes=(f"phk{hh}",))
        p.op("dve", lambda e, hh=hh: e.memset(phq_tm[hh][:, 0:1], 1.0),
             writes=(f"phq_tm{hh}",))

    def wp_dma(idx):
        if idx < 32:
            r, cb = idx // 4, idx % 4
            src = dram["w1t"][r * 128:(r + 1) * 128, cb * 1024:(cb + 1) * 1024]
        else:
            i = idx - 32
            src = dram["w2t"][i * 128:(i + 1) * 128, :]
        j = idx % 2
        p.dma("act", "qact", w1stg[j], src, writes=(f"w1stg{j}",))

    def wp_comp(idx):
        wcol = wsum1[:, idx:idx + 1] if idx < 32 else wsum2[:, idx - 32:idx - 31]
        j = idx % 2
        p.act_set(AFT.Abs)
        p.op("act", lambda e, j=j: e.activation(w1stg[j], w1stg[j], AFT.Abs, bias=0.0,
                                                scale=1.0),
             reads=(f"w1stg{j}",), writes=(f"w1stg{j}",))
        p.op("dve", lambda e, wcol=wcol, j=j: e.reduce_sum(wcol, w1stg[j], AX.X),
             reads=(f"w1stg{j}",), writes=("wsum",))

    def wpass(idx):
        if idx > 0:
            wp_comp(idx - 1)
        if idx < 64:
            wp_dma(idx)

    wp_iter = iter(range(65))

    for hp in range(H // 2):
        heads = (2 * hp, 2 * hp + 1)
        hinfo = []
        for h in heads:
            pg_, sl_ = h // 4, h % 4
            po = sl_ * 32
            hinfo.append((h, pg_, po, (96, 0) if po == 96 else None))

        for hh, (h, pg_, po, tp) in enumerate(hinfo):
            kph = f"phk{hh}"
            us0 = ktm4s[:, pg_ * 128 + po: pg_ * 128 + po + 16]
            uin = fdims(us0, [[512, nt], [1, 16]])
            uout = fdims(phk[hh][:, 1:2], [[280, nt], [1, 16]])
            p.op("dve", lambda e, uin=uin, uout=uout: e.tensor_single_scalar(
                uout, uin, SQHI, ALU.mult), reads=("ktm4s",), writes=(kph,))
            o_out = fdims(phk[hh][:, 17:18], [[280, nt], [16, 16], [1, 16]])
            in_i = fdims(us0, [[512, nt], [1, 16], [0, 16]])
            in_j = fdims(us0, [[512, nt], [0, 16], [1, 16]])
            p.op("dve", lambda e, o_out=o_out, in_i=in_i, in_j=in_j: e.tensor_tensor(
                o_out, in_i, in_j, ALU.mult), reads=("ktm4s",), writes=(kph,))
            p.op("dve", lambda e, hh=hh: e.memset(s_sb[hh][:, 0:195], 0.0),
                 writes=(f"s_sb{hh}",))

        def gmm(hh, cc, half):
            h, pg_, po, tp = hinfo[hh]
            kq = qfm4[:, pg_ * SEQ + cc * 128: pg_ * SEQ + (cc + 1) * 128][po:po + 16, :]
            kk = kfm4[:, pg_ * SEQ + (cc - 1 + half) * 128: pg_ * SEQ + (cc + half) * 128][po:po + 16, :]
            p.op("pe", lambda e, kk=kk, kq=kq, hh=hh, half=half, tp=tp: e.matmul(
                psA[:, hh * 256 + half * 128: hh * 256 + (half + 1) * 128], kk, kq,
                start=True, stop=True, tile_position=tp), reads=("kfm4", "qfm4"),
                writes=("bA",))

        def psy_of(hh, c):
            if hh == 0:
                return (psD[:, 0:130], "bD") if c % 2 == 0 else (psB[:, 0:130], "bB")
            return (psE[:, 0:130], "bE") if c % 2 == 0 else (psF[:, 0:130], "bF")

        def qbuild(cc):
            for hh, (h, pg_, po, tp) in enumerate(hinfo):
                kpt = f"phq_tm{hh}"
                us = qtm4s[:, cc * 512 + pg_ * 128 + po: cc * 512 + pg_ * 128 + po + 16]
                p.op("dve", lambda e, us=us, hh=hh: e.tensor_single_scalar(
                    phq_tm[hh][:, 1:17], us, SQHI, ALU.mult),
                    reads=("qtm4s",), writes=(kpt,))
                o_out = fdims(phq_tm[hh][:, 17:18], [[16, 16], [1, 16]])
                in_i = fdims(us, [[1, 16], [0, 16]])
                in_j = fdims(us, [[0, 16], [1, 16]])
                p.op("dve", lambda e, o_out=o_out, in_i=in_i, in_j=in_j: e.tensor_tensor(
                    o_out, in_i, in_j, ALU.mult), reads=("qtm4s",), writes=(kpt,))

        def epi_act(c):
            for hh in range(2):
                psy, ky = psy_of(hh, c)
                tl, ktl = tloc[hh], f"tl{hh}"
                ct = C_T + 3 * hh
                p.op("act", lambda e, psy=psy, ct=ct: e.activation(
                    smal[:, ct:ct + 1], psy[:, 129:130], AFT.Identity, bias=c1e8,
                    scale=1.0), reads=("c1e8",), writes=(f"den{hh}", ky))
                p.op("act", lambda e, psy=psy, ct=ct: e.activation(
                    smal[:, ct + 2:ct + 3], psy[:, 64:65], AFT.Identity, bias=czero,
                    scale=1.0), reads=("czero",), writes=(f"dloc{hh}", ky))
                if c + 1 < nt:
                    p.op("act", lambda e, hh=hh: e.activation(
                        s16[hh][:, 0:195], s_sb[hh][:, 0:195], AFT.Copy, bias=0.0,
                        scale=1.0), reads=(f"s_sb{hh}",), writes=(f"s16{hh}",))

        def epi_dve(c):
            for hh in range(2):
                psy, ky = psy_of(hh, c)
                h = heads[hh]
                tl, ktl = tloc[hh], f"tl{hh}"
                ct = C_T + 3 * hh
                p.op("dve", lambda e, ct=ct: e.reciprocal(smal[:, ct + 1:ct + 2],
                                                          smal[:, ct:ct + 1]),
                     reads=(f"den{hh}",), writes=(f"rlin{hh}",))
                p.op("dve", lambda e, ct=ct: e.reciprocal(smal[:, ct + 2:ct + 3],
                                                          smal[:, ct + 2:ct + 3]),
                     reads=(f"dloc{hh}",), writes=(f"dloc{hh}",))
                p.op("dve", lambda e, psy=psy, tl=tl, ct=ct: e.tensor_single_scalar(
                    tl, psy[:, 0:64], smal[:, ct + 2:ct + 3], ALU.mult),
                    reads=(f"dloc{hh}",), writes=(ktl, ky))
                p.op("dve", lambda e, psy=psy, tl=tl, ct=ct, c=c, h=h: e.scalar_tensor_tensor(
                    attn[:, c * DM + h * 64: c * DM + h * 64 + 64], psy[:, 65:129],
                    smal[:, ct + 1:ct + 2], tl, ALU.mult, ALU.add),
                    reads=(f"rlin{hh}", ktl), writes=("attn", ky))

        if nt > 1:
            qbuild(1)
        for c in range(nt):
            lo = 0 if c > 0 else 128
            for hh, (h, pg_, po, tp) in enumerate(hinfo):
                pss = psC[:, 0:195] if hh == 0 else psC[:, 512:707]
                ks_ = f"bC{hh}"
                vcur = vt[:, c * 1040 + h * 65: c * 1040 + h * 65 + 65]
                for k in range(3):
                    p.op("pe", lambda e, k=k, c=c, hh=hh, pss=pss, vcur=vcur: e.matmul(
                        pss[:, k * 65:(k + 1) * 65],
                        phk[hh][:, c * 280 + k * 128: c * 280 + (k + 1) * 128],
                        vcur, start=True, stop=True),
                        reads=(f"phk{hh}", "vt"), writes=(ks_,))
            if c == 0:
                for hh in range(2):
                    gmm(hh, 0, 1)

            p.act_set(AFT.Exp)
            for hh in range(2):
                esb, ke = e_sb[hh * 2 + c % 2], f"esb{hh}_{c % 2}"
                p.op("act", lambda e, hh=hh, esb=esb, lo=lo: e.activation(
                    esb[:, lo:256], psA[:, hh * 256 + lo: hh * 256 + 256], AFT.Exp,
                    bias=0.0, scale=1.0), reads=(), writes=(ke, "bA"))
                p.op("act", lambda e, hh=hh, esb=esb: e.activation(
                    esb[:, 256:384], psA[:, hh * 256 + 128: hh * 256 + 256], AFT.Square,
                    bias=csqh, scale=2.0 ** -0.5), reads=("csqh",), writes=(ke, "bA"))
            if c > 0:
                epi_act(c - 1)

            for hh in range(2):
                esb, ke = e_sb[hh * 2 + c % 2], f"esb{hh}_{c % 2}"
                p.op("dve", lambda e, esb=esb, lo=lo: e.tensor_tensor(
                    esb[:, lo:256], esb[:, lo:256], m01ea[:, lo:256], ALU.mult),
                    reads=("m01ea",), writes=(ke,))
                p.op("dve", lambda e, esb=esb: e.scalar_tensor_tensor(
                    esb[:, 256:384], esb[:, 256:384], 0.5, m01ea[:, 256:384],
                    ALU.add, ALU.mult), reads=("m01ea",), writes=(ke,))

            if c + 1 < nt:
                for hh in range(2):
                    gmm(hh, c + 1, 0)
                    gmm(hh, c + 1, 1)
                for hh in range(2):
                    kpt = f"phq_tm{hh}"
                    for k in range(3):
                        w = 128 if k < 2 else 17
                        p.op("pe", lambda e, k=k, w=w, hh=hh: e.transpose(
                            psT[0:w, hh * 512 + k * 128: hh * 512 + k * 128 + 128],
                            phq_tm[hh][:, k * 128:k * 128 + w], ident),
                            reads=(kpt, "ident"), writes=("bT",))

            for hh, (h, pg_, po, tp) in enumerate(hinfo):
                psy, ky = psy_of(hh, c)
                esb, ke = e_sb[hh * 2 + c % 2], f"esb{hh}_{c % 2}"
                vprev = vt[:, (c - 1) * 1040 + h * 65: (c - 1) * 1040 + h * 65 + 65]
                vcur = vt[:, c * 1040 + h * 65: c * 1040 + h * 65 + 65]
                if c > 0:
                    p.op("pe", lambda e, esb=esb, vprev=vprev, psy=psy: e.matmul(
                        psy[:, 0:65], esb[:, 0:128], vprev, start=True, stop=False),
                        reads=(ke, "vt"), writes=(ky,))
                    p.op("pe", lambda e, esb=esb, vcur=vcur, psy=psy: e.matmul(
                        psy[:, 0:65], esb[:, 128:256], vcur, start=False, stop=True),
                        reads=(ke, "vt"), writes=(ky,))
                else:
                    p.op("pe", lambda e, esb=esb, vcur=vcur, psy=psy: e.matmul(
                        psy[:, 0:65], esb[:, 128:256], vcur, start=True, stop=True),
                        reads=(ke, "vt"), writes=(ky,))
                p.op("pe", lambda e, esb=esb, vcur=vcur, psy=psy, c=c: e.matmul(
                    psy[:, 65:130], esb[:, 256:384], vcur, start=True, stop=(c == 0)),
                    reads=(ke, "vt"), writes=(ky,))
                if c > 0:
                    slot = c % 2
                    kr = f"phqr{hh}_{slot}"
                    for k in range(3):
                        kw = 128 if k < 2 else 17
                        p.op("pe", lambda e, k=k, kw=kw, hh=hh, slot=slot, psy=psy: e.matmul(
                            psy[:, 65:130],
                            phqr[hh][0:kw, slot * 384 + k * 128: slot * 384 + (k + 1) * 128],
                            s16[hh][0:kw, k * 65:(k + 1) * 65], start=False, stop=(k == 2)),
                            reads=(kr, f"s16{hh}"), writes=(ky,))

            if c + 2 < nt:
                qbuild(c + 2)
            for hh in range(2):
                pss = psC[:, 0:195] if hh == 0 else psC[:, 512:707]
                ks_ = f"bC{hh}"
                p.op("dve", lambda e, pss=pss, hh=hh: e.tensor_tensor(
                    s_sb[hh][:, 0:195], s_sb[hh][:, 0:195], pss[:, 0:195], ALU.add),
                    reads=(f"s_sb{hh}",), writes=(f"s_sb{hh}", ks_))
            if c + 1 < nt:
                for hh in range(2):
                    slot = (c + 1) % 2
                    kr = f"phqr{hh}_{slot}"
                    p.op("dve", lambda e, hh=hh, slot=slot: e.tensor_copy(
                        phqr[hh][:, slot * 384: slot * 384 + 256],
                        psT[:, hh * 512: hh * 512 + 256]), reads=(), writes=(kr, "bT"))
                    p.op("dve", lambda e, hh=hh, slot=slot: e.tensor_copy(
                        phqr[hh][0:17, slot * 384 + 256: slot * 384 + 384],
                        psT[0:17, hh * 512 + 256: hh * 512 + 384]),
                        reads=(), writes=(kr, "bT"))
            if c > 0:
                epi_dve(c - 1)

            if (hp * nt + c) % 2 == 0:
                nxt = next(wp_iter, None)
                if nxt is not None:
                    wpass(nxt)
        epi_act(nt - 1)
        epi_dve(nt - 1)

    for nxt in wp_iter:
        wpass(nxt)

    for i, (ws, nel, c_ma, c_sc) in enumerate([(wsum1, DM * HID, C_MA1, C_SC1),
                                               (wsum2, HID * DM, C_MA2, C_SC2)]):
        p.op("pe", lambda e, ws=ws, i=i: e.matmul(psE[0:1, 256 + i * 32:256 + (i + 1) * 32],
                                                  onecol, ws, start=True, stop=True),
             reads=("wsum", "onecol"), writes=("bE",))
        p.op("dve", lambda e, i=i: e.tensor_reduce(
            s1x1[:, 4 * i:4 * i + 1], psE[0:1, 256 + i * 32:256 + (i + 1) * 32], AX.X, ALU.add),
            reads=(), writes=(f"s1a{i}", "bE"))
        p.op("dve", lambda e, i=i, nel=nel: e.tensor_scalar(
            s1x1[:, 4 * i + 1:4 * i + 2], s1x1[:, 4 * i:4 * i + 1], 1.0 / nel, 1e-5,
            ALU.mult, ALU.max), reads=(f"s1a{i}",), writes=(f"s1b{i}",))
        p.op("pe", lambda e, i=i: e.matmul(psE[:, 320 + i:321 + i], onescf,
                                           s1x1[:, 4 * i + 1:4 * i + 2],
                                           start=True, stop=True),
             reads=(f"s1b{i}", "onescf"), writes=("bE",))
        p.op("dve", lambda e, i=i, c_ma=c_ma: e.tensor_copy(sc(c_ma), psE[:, 320 + i:321 + i]),
             reads=(), writes=(f"cma{i}", "bE"))
        p.op("dve", lambda e, c_ma=c_ma, c_sc=c_sc, i=i: e.reciprocal(sc(c_sc), sc(c_ma)),
             reads=(f"cma{i}",), writes=(f"csc{i}",))

    p.barrier()
    if dbg:
        for t in range(nt):
            p.dma("sp", "qsp", dram["dbg_attn"][t * 128:(t + 1) * 128, :],
                  attn[:, t * DM:(t + 1) * DM], reads=("attn",))

    def tern_src(i):
        if i < 32:
            r, cb = i // 4, i % 4
            return dram["w1t"][r * 128:(r + 1) * 128, cb * 1024:(cb + 1) * 1024]
        r = i - 32
        return dram["w2t"][r * 128:(r + 1) * 128, :]

    def tern_dma(i):
        p.dma("act", "qact", wbuf[i % 3], tern_src(i), writes=(f"wbuf{i % 3}",))

    def tern_chunk(i):
        if i < 32:
            dst = w1q[:, (i // 4) * HID + (i % 4) * 1024: (i // 4) * HID + (i % 4 + 1) * 1024]
            kd, scl, kscl = "w1q", sc(C_SC1), "csc0"
        else:
            dst = w2q[:, (i - 32) * DM:(i - 31) * DM]
            kd, scl, kscl = "w2q", sc(C_SC2), "csc1"
        j = i % 3
        wb, sg, db = wbuf[j], sgb[j], dbb[j]
        kw, ks, kb2 = f"wbuf{j}", f"sgb{j}", f"dbb{j}"
        p.act_set(AFT.Sign)
        p.op("act", lambda e, wb=wb, sg=sg: e.activation(sg, wb, AFT.Sign, bias=0.0,
                                                         scale=1.0),
             reads=(kw,), writes=(ks,))
        p.act_set(AFT.Abs)
        p.op("act", lambda e, scl=scl, wb=wb: e.activation(wb, wb, AFT.Abs, bias=0.0,
                                                           scale=scl),
             reads=(kw, kscl), writes=(kw,))
        p.op("dve", lambda e, wb=wb, db=db: e.tensor_single_scalar(db, wb, 0.5, ALU.is_ge),
             reads=(kw,), writes=(kb2,))
        p.op("dve", lambda e, dst=dst, db=db, sg=sg: e.tensor_tensor(dst, db, sg, ALU.mult),
             reads=(kb2, ks), writes=(kd,))
        if i + 3 < 64:
            tern_dma(i + 3)

    tern_iter = iter(range(64))
    tern_dma(0)
    tern_dma(1)
    tern_dma(2)
    for t in range(nt):
        pb = t % 2
        xb, kx = x_t4[pb], f"x4_{pb}"
        p.dma("sp", "qsp", xb, dram["x_in"][t * 128:(t + 1) * 128, :], writes=(kx,))
        asl = attn[:, t * DM:(t + 1) * DM]
        b0 = ln_stats(asl, ("attn",), pb=pb)
        p.act_set(AFT.Identity)
        p.op("act", lambda e, asl=asl, b0=b0: e.activation(ln3b, asl, AFT.Identity,
             bias=sc(b0 + 3), scale=sc(b0 + 2)),
             reads=("attn", f"c_r{pb}", f"c_nmr{pb}"), writes=("ln3b",))
        p.op("dve", lambda e, xb=xb: e.tensor_tensor(xb, ln3b, xb, ALU.add),
             reads=("ln3b", kx), writes=(kx,))
        p.dma("sp", "qsp", xmid_d[t * 128:(t + 1) * 128, :], xb, reads=(kx,),
              writes=(f"xmd{t}",))
        if dbg:
            p.dma("sp", "qsp", dram["dbg_xmid"][t * 128:(t + 1) * 128, :], xb, reads=(kx,))
        for _ in range(4):
            nxt = next(tern_iter, None)
            if nxt is not None:
                tern_chunk(nxt)
    for nxt in tern_iter:
        tern_chunk(nxt)

    p.barrier()

    def ph4b_pre_dma(t):
        sl4 = t % 4
        xm, kxm = xm_in[sl4], f"xm{sl4}"
        p.dma("sp", "qsp", xm, xmid_d[t * 128:(t + 1) * 128, :],
              reads=(f"xmd{t}",), writes=(kxm,))

    def ph4b_pre(t):
        sl4 = t % 4
        xm, kxm = xm_in[sl4], f"xm{sl4}"
        pb = t % 2
        xqtb, kqt = xqtm[pb], f"xqtm{pb}"
        st = smal[:, 60 + 12 * pb: 72 + 12 * pb]
        kst = f"st4_{pb}"
        b0 = 84 + 4 * pb
        p.op("dve", lambda e, st=st, xm=xm: e.bn_stats(st[:, 0:6], xm[:, 0:512]),
             reads=(kxm,), writes=(kst,))
        p.op("dve", lambda e, st=st, xm=xm: e.bn_stats(st[:, 6:12], xm[:, 512:1024]),
             reads=(kxm,), writes=(kst,))
        p.op("dve", lambda e, st=st, b0=b0: e.bn_aggr(smal[:, b0:b0 + 2], st[:, 0:12]),
             reads=(kst,), writes=(f"mv4{pb}",))
        p.act_set(AFT.Sqrt)
        p.op("act", lambda e, b0=b0: e.activation(sc(b0 + 2), sc(b0 + 1), AFT.Sqrt,
                                                  bias=ceps, scale=1.0),
             reads=(f"mv4{pb}", "ceps"), writes=(f"r4{pb}",))
        p.op("dve", lambda e, b0=b0: e.reciprocal(sc(b0 + 2), sc(b0 + 2)),
             reads=(f"r4{pb}",), writes=(f"r4{pb}",))
        p.op("dve", lambda e, b0=b0: e.tensor_scalar(sc(b0 + 3), sc(b0), sc(b0 + 2), -1.0,
                                                     ALU.mult, ALU.mult),
             reads=(f"mv4{pb}", f"r4{pb}"), writes=(f"n4{pb}",))
        p.act_set(AFT.Identity)
        p.op("act", lambda e, xm=xm, b0=b0: e.activation(xl2b, xm, AFT.Identity,
             bias=sc(b0 + 3), scale=sc(b0 + 2)),
             reads=(kxm, f"r4{pb}", f"n4{pb}"), writes=("xl2b",))
        cq = 92 + 3 * pb
        p.op("dve", lambda e, cq=cq: e.reduce_max(sc(cq), xl2b, AX.X,
                                                  apply_absolute_value=True),
             reads=("xl2b",), writes=(f"cmax4_{pb}",))
        p.op("dve", lambda e, cq=cq: e.tensor_single_scalar(sc(cq), sc(cq), 1e-5, ALU.max),
             reads=(f"cmax4_{pb}",), writes=(f"cmax4_{pb}",))
        p.op("dve", lambda e, cq=cq: e.tensor_scalar(sc(cq + 1), sc(cq), sc(C_MA1),
                                                     1.0 / 127.0, ALU.mult, ALU.mult),
             reads=(f"cmax4_{pb}", "cma0"), writes=(f"cisx4_{pb}",))
        p.op("dve", lambda e, cq=cq: e.reciprocal(sc(cq + 2), sc(cq)),
             reads=(f"cmax4_{pb}",), writes=(f"csx4_{pb}",))
        p.op("dve", lambda e, cq=cq: e.tensor_single_scalar(sc(cq + 2), sc(cq + 2), 127.0,
                                                            ALU.mult),
             reads=(f"csx4_{pb}",), writes=(f"csx4_{pb}",))
        p.op("dve", lambda e, cq=cq, pb=pb: e.tensor_scalar(
            xqi[pb], xl2b, sc(cq + 2), 1000.5, ALU.mult, ALU.add),
            reads=("xl2b", f"csx4_{pb}"), writes=(kqt,))
        p.op("dve", lambda e, cq=cq, pb=pb, xqtb=xqtb: e.tensor_scalar(
            xqtb, xqi[pb], 1000.0, sc(cq + 1), ALU.subtract, ALU.mult),
            reads=(kqt, f"cisx4_{pb}"), writes=(kqt,))
    def ph4b_tr_a(t):
        pb = t % 2
        xqtb, kqt = xqtm[pb], f"xqtm{pb}"
        for dc in range(4):
            p.op("pe", lambda e, dc=dc, xqtb=xqtb: e.transpose(
                psT[:, dc * 128: (dc + 1) * 128],
                xqtb[:, dc * 128:(dc + 1) * 128], ident),
                reads=(kqt, "ident"), writes=("bT",))

    def ph4b_tr(t):
        sl4 = t % 4
        pb = t % 2
        xqtb, kqt = xqtm[pb], f"xqtm{pb}"
        xs, kxs = xqs[sl4], f"xqs{sl4}"
        for dc in range(4, 8):
            p.op("pe", lambda e, dc=dc, xqtb=xqtb: e.transpose(
                psT[:, dc * 128: (dc + 1) * 128],
                xqtb[:, dc * 128:(dc + 1) * 128], ident),
                reads=(kqt, "ident"), writes=("bT",))
        d1 = fdims(xs[:, 1:2], [[132, 8], [1, 128]])
        p.op("dve", lambda e, d1=d1: e.tensor_copy(d1, psT[:, 0:1024]),
             reads=(), writes=(kxs, "bT"))
        if t == 0:
            z0 = fdims(xs[:, 0:1], [[132, 8]])
            p.op("dve", lambda e, z0=z0: e.memset(z0, 0.0), writes=(kxs,))
        else:
            slp = (t - 1) % 4
            dsth = fdims(xqs[slp][:, 129:130], [[132, 8]])
            srch = fdims(xs[:, 1:2], [[132, 8]])
            p.op("dve", lambda e, dsth=dsth, srch=srch: e.tensor_copy(dsth, srch),
                 reads=(kxs,), writes=(f"xqs{slp}",))
            dstl = fdims(xs[:, 0:1], [[132, 8]])
            srcl = fdims(xqs[slp][:, 128:129], [[132, 8]])
            p.op("dve", lambda e, dstl=dstl, srcl=srcl: e.tensor_copy(dstl, srcl),
                 reads=(f"xqs{slp}",), writes=(kxs,))
        if t == nt - 1:
            z1 = fdims(xs[:, 129:130], [[132, 8]])
            p.op("dve", lambda e, z1=z1: e.memset(z1, 0.0), writes=(kxs,))
        if dbg:
            dsrc = fdims(xs[:, 1:2], [[132, 8], [1, 128]])
            ddst = fdims(dram["dbg_xqf"][:, t * 128: t * 128 + 1], [[SEQ, 8], [1, 128]])
            p.dma("sp", "qsp", ddst, dsrc, reads=(kxs,))

    def quant_group(t, g):
        hqb, hqi, khq = hqg[g % 2], hqg_i16[g % 2], f"hqg{g % 2}"
        h2g = h2[:, g * 1024:(g + 1) * 1024]
        kh2 = f"h2g{g}"
        srep = fdims(sbc[:, 0:1], [[0, 8], [1, 128]])
        p.op("dve", lambda e, h2g=h2g, srep=srep: e.tensor_tensor(h2g, h2g, srep, ALU.mult),
             reads=("sbc", kh2), writes=(kh2,))
        p.op("dve", lambda e, h2g=h2g, hqi=hqi: e.tensor_single_scalar(hqi, h2g, 1000.5,
                                                                       ALU.add),
             reads=(kh2,), writes=(khq,))
        p.op("dve", lambda e, hqb=hqb, hqi=hqi: e.tensor_single_scalar(hqb, hqi, 1000.0,
                                                                       ALU.subtract),
             reads=(khq,), writes=(khq,))
        if dbg and t == 0:
            p.dma("sp", "qsp", dram["dbg_h2"][:, g * 1024:(g + 1) * 1024], h2g, reads=(kh2,))
            p.dma("sp", "qsp", dram["dbg_hq"][:, g * 1024:(g + 1) * 1024], hqb, reads=(khq,))

    def w2_group(g):
        hqb, khq = hqg[g % 2], f"hqg{g % 2}"
        for mj in range(8):
            m = g * 8 + mj
            for hf in range(2):
                p.op("pe", lambda e, m=m, mj=mj, hf=hf, hqb=hqb: e.matmul(
                    psC[:, hf * 512:(hf + 1) * 512], hqb[:, mj * 128:(mj + 1) * 128],
                    w2q[:, m * DM + hf * 512: m * DM + (hf + 1) * 512],
                    start=(m == 0), stop=(m == NM - 1)),
                    reads=(khq, "w2q"), writes=("bC0", "bC1"))

    def stage1(t, fill=None):
        xs, kxs = xqs[t % 4], f"xqs{t % 4}"
        w1banks = [(psA, "bA"), (psB, "bB"), (psE, "bE"), (psF, "bF")]
        for m in range(NM):
            psh, kh = w1banks[m % 4]
            hb, kb = hsb[m % 2], f"hsb{m % 2}"
            acb, kc = accb[m % 2], f"acc{m % 2}"
            for dc in range(8):
                p.op("pe", lambda e, m=m, psh=psh, xs=xs, dc=dc: e.matmul(
                    psh[:, 0:130],
                    w1q[:, dc * HID + m * 128: dc * HID + (m + 1) * 128],
                    xs[:, dc * 132: dc * 132 + 130], start=(dc == 0), stop=(dc == 7)),
                    reads=("w1q", kxs), writes=(kh,))
            if m % 2 == 0:
                p.act_set(AFT.Copy)
                p.op("act", lambda e, psh=psh, hb=hb: e.activation(
                    hb, psh[:, 0:130], AFT.Copy, bias=0.0, scale=1.0),
                    reads=(), writes=(kb, kh))
            else:
                p.op("dve", lambda e, psh=psh, hb=hb: e.tensor_copy(hb, psh[:, 0:130]),
                     reads=(), writes=(kb, kh))
            p.op("dve", lambda e, hb=hb, acb=acb, m=m: e.tensor_single_scalar(
                acb, hb[:, 0:128], cw_h[:, m:m + 1], ALU.mult),
                reads=(kb, "cw_h"), writes=(kc,))
            p.op("dve", lambda e, hb=hb, acb=acb, m=m: e.scalar_tensor_tensor(
                acb, hb[:, 1:129], cw_h[:, 32 + m:33 + m], acb, ALU.mult, ALU.add),
                reads=(kb, kc, "cw_h"), writes=(kc,))
            p.op("dve", lambda e, hb=hb, acb=acb, m=m: e.scalar_tensor_tensor(
                acb, hb[:, 2:130], cw_h[:, 64 + m:65 + m], acb, ALU.mult, ALU.add),
                reads=(kb, kc, "cw_h"), writes=(kc,))
            p.act_set(AFT.Silu)
            p.op("act", lambda e, acb=acb, m=m: e.activation(
                h2[:, m * 128:(m + 1) * 128], acb, AFT.Silu, bias=cb_h[:, m:m + 1], scale=1.0),
                reads=(kc, "cb_h"), writes=(f"h2g{m // 8}",))
            if fill is not None and m in (7, 15, 23, 31):
                fill(m // 8)

    def h2_partial(t, b):
        rin = fdims(h2[:, b * 1024: b * 1024 + 1], [[1, 128], [128, 8]])
        dst = pmax if b == 0 else sbcp
        kdst = "pmax" if b == 0 else "sbcp"
        p.op("dve", lambda e, rin=rin, dst=dst: e.tensor_reduce(
            dst, rin, AX.X, ALU.max, apply_absolute_value=True),
            reads=(f"h2g{b}",), writes=(kdst,))
        if b > 0:
            p.op("dve", lambda e: e.tensor_tensor(pmax, pmax, sbcp, ALU.max),
                 reads=("pmax", "sbcp"), writes=("pmax",))

    def qdance_p1(t):
        p.op("pe", lambda e: e.transpose(psD[:, 0:128], pmax, identf),
             reads=("pmax", "identf"), writes=("bD",))
        p.op("dve", lambda e: e.reduce_max(sc(C_MAX), psD[:, 0:128], AX.X),
             reads=(), writes=("cmax5", "bD"))
        p.op("dve", lambda e: e.tensor_single_scalar(sc(C_MAX), sc(C_MAX), 1e-5, ALU.max),
             reads=("cmax5",), writes=("cmax5",))
        cfc = C_F if t % 2 == 0 else C_ISX
        p.op("dve", lambda e, cfc=cfc: e.tensor_scalar(sc(cfc), sc(C_MAX), sc(C_MA2),
                                                       1.0 / 127.0, ALU.mult, ALU.mult),
             reads=("cmax5", "cma1"), writes=(f"cf{t % 2}",))
        p.op("dve", lambda e: e.reciprocal(sc(C_SX), sc(C_MAX)),
             reads=("cmax5",), writes=("csx5",))
        p.op("dve", lambda e: e.tensor_single_scalar(sc(C_SX), sc(C_SX), 127.0, ALU.mult),
             reads=("csx5",), writes=("csx5",))

    def qdance_p2(t):
        p.op("pe", lambda e: e.transpose(psD[0:1, 128:256], sc(C_SX), identf),
             reads=("csx5", "identf"), writes=("bD",))
        p.op("dve", lambda e: e.tensor_copy(srow_full[0:1, :], psD[0:1, 128:256]),
             reads=(), writes=("srow", "bD"))

    def qdance_p3(t):
        p.op("pe", lambda e: e.matmul(psD[:, 256:384], onescf, srow_full[0:1, :],
                                      start=True, stop=True),
             reads=("srow", "onescf"), writes=("bD",))
        p.op("dve", lambda e: e.tensor_copy(sbc, psD[:, 256:384]),
             reads=(), writes=("sbc", "bD"))
        quant_group(t, 0)
        quant_group(t, 1)

    def mk_fill(t):
        def fill(b):
            h2_partial(t, b)
            if t > 0:
                if b == 1:
                    w2_group(0)
                    quant_group(t - 1, 2)
                elif b == 2:
                    w2_group(1)
                    quant_group(t - 1, 3)
                elif b == 3:
                    w2_group(2)
                    w2_group(3)
            if b == 2 and t + 2 < nt:
                ph4b_pre(t + 2)
        return fill

    def stage2b(t):
        p.op("dve", lambda e: e.bn_stats(stat6[:, 0:6], psC[:, 0:512]),
             reads=(), writes=("stat6", "bC0"))
        p.op("dve", lambda e: e.bn_stats(stat6[:, 6:12], psC[:, 512:1024]),
             reads=(), writes=("stat6", "bC1"))
        p.op("dve", lambda e: e.bn_aggr(smal[:, C_MEAN:C_VAR + 1], stat6[:]),
             reads=("stat6",), writes=("c_mv",))
        cfc = C_F if t % 2 == 0 else C_ISX
        p.op("dve", lambda e, cfc=cfc: e.tensor_scalar(sc(C_VT2), sc(C_VAR), sc(cfc),
                                                       sc(cfc), ALU.mult, ALU.mult),
             reads=("c_mv", f"cf{t % 2}"), writes=("cvt2",))
        p.act_set(AFT.Sqrt)
        p.op("act", lambda e: e.activation(sc(C_R), sc(C_VT2), AFT.Sqrt, bias=ceps, scale=1.0),
             reads=("cvt2", "ceps"), writes=("c_r",))
        p.op("dve", lambda e: e.reciprocal(sc(C_R), sc(C_R)), reads=("c_r",), writes=("c_r",))
        p.op("dve", lambda e, cfc=cfc: e.tensor_scalar(sc(C_S4), sc(C_R), sc(cfc), None,
                                                       ALU.mult),
             reads=("c_r", f"cf{t % 2}"), writes=("cs4",))
        p.op("dve", lambda e: e.tensor_scalar(sc(C_B4), sc(C_MEAN), sc(C_S4), -1.0,
                                              ALU.mult, ALU.mult),
             reads=("c_mv", "cs4"), writes=("cb4",))
        p.act_set(AFT.Identity)
        p.op("act", lambda e: e.activation(outb, psC[:], AFT.Identity,
                                           bias=sc(C_B4), scale=sc(C_S4)),
             reads=("cs4", "cb4"), writes=("outb", "bC0", "bC1"))
        p.op("dve", lambda e, t=t: e.tensor_tensor(outb, outb, xm_in[t % 4], ALU.add),
             reads=("outb", f"xm{t % 4}"), writes=("outb",))
        p.dma("sp", "qsp", dram["out"][t * 128:(t + 1) * 128, :], outb, reads=("outb",))

    ph4b_pre_dma(0)
    ph4b_pre(0)
    if nt > 1:
        ph4b_pre_dma(1)
        ph4b_pre(1)
    ph4b_tr_a(0)
    ph4b_tr(0)
    if nt > 1:
        if nt > 2:
            ph4b_pre_dma(2)
        ph4b_tr_a(1)
        ph4b_tr(1)
    for t in range(nt + 1):
        if t < nt:
            stage1(t, fill=mk_fill(t))
        else:
            w2_group(0)
            quant_group(t - 1, 2)
            w2_group(1)
            quant_group(t - 1, 3)
            w2_group(2)
            w2_group(3)
        if t < nt:
            qdance_p1(t)
        if t + 2 < nt:
            ph4b_tr_a(t + 2)
        if t < nt:
            qdance_p2(t)
        if t + 2 < nt:
            ph4b_tr(t + 2)
        if t < nt:
            qdance_p3(t)
        if 0 < t:
            stage2b(t - 1)
        if t + 3 < nt:
            ph4b_pre_dma(t + 3)

    p.finalize()
    return nc, es


_BUILD_CACHE = {}


def _get_program(nt, dbg=False):
    key = (nt, dbg)
    if key not in _BUILD_CACHE:
        _BUILD_CACHE[key] = build_program(nt, dbg, strict_eng="dve")
    return _BUILD_CACHE[key]


def host_inputs(x_core, Wq, bq, Wk, bk, Wv, bv, w1, conv_w, conv_b, w2, nt=16):
    f32 = np.float32
    bf = ml_dtypes.bfloat16
    d = {}
    d["x_in"] = np.ascontiguousarray(x_core, dtype=f32)
    wqt = np.zeros((DM, 512), f32)
    wkt = np.zeros((DM, 512), f32)
    bqp = np.zeros((128, 4), f32)
    bkp = np.zeros((128, 4), f32)
    for h in range(H):
        pg, sl = h // 4, h % 4
        wqt[:, pg * 128 + sl * 32: pg * 128 + sl * 32 + 16] = Wq[h * 16:(h + 1) * 16, :].T
        wkt[:, pg * 128 + sl * 32: pg * 128 + sl * 32 + 16] = Wk[h * 16:(h + 1) * 16, :].T
        bqp[sl * 32: sl * 32 + 16, pg] = bq[h * 16:(h + 1) * 16]
        bkp[sl * 32: sl * 32 + 16, pg] = bk[h * 16:(h + 1) * 16]
    d["wqtp"] = wqt
    d["wktp"] = wkt
    d["bqp"] = bqp
    d["bkp"] = bkp
    d["wvt"] = np.ascontiguousarray(Wv.T, dtype=f32)
    d["bv_rep"] = np.ascontiguousarray(np.broadcast_to(bv, (128, DM)), dtype=f32)
    d["w1t"] = np.ascontiguousarray(w1.T, dtype=f32)
    d["w2t"] = np.ascontiguousarray(w2.T, dtype=f32)
    cw = np.zeros((128, 96), f32)
    for k in range(3):
        cw[:, k * 32:(k + 1) * 32] = conv_w[:, 0, k].reshape(NM, 128).T
    d["cw_h"] = cw
    d["cb_h"] = np.ascontiguousarray(conv_b.reshape(NM, 128).T, dtype=f32)
    d["ident"] = np.eye(128, dtype=bf)
    pp, qq = np.arange(128)[:, None], np.arange(128)[None, :]
    m_prev = (pp - qq >= 65).astype(bf)
    dqp = qq - pp
    m_diag = ((dqp >= 0) & (dqp <= 63)).astype(bf)
    m01a = (qq >= pp).astype(np.float32)
    d["m01ea"] = np.ascontiguousarray(
        np.concatenate([m_prev.astype(np.float32), m_diag.astype(np.float32), m01a], axis=1)).astype(bf)
    d["m01ah"] = (0.5 * m01a).astype(bf)
    d["onescf"] = np.ones((1, 128), f32)
    d["identf"] = np.eye(128, dtype=f32)
    d["one11f"] = np.ones((1, 1), f32)
    d["onecol"] = np.ones((128, 1), f32)
    return d


def kernel(L0_attn_x, L0_attn_Wq, L0_attn_bq, L0_attn_Wk, L0_attn_bk, L0_attn_Wv,
           L0_attn_bv, L0_ffn_w1, L0_ffn_conv_w, L0_ffn_conv_b, L0_ffn_w2):
    x = np.asarray(L0_attn_x)
    nt = x.shape[1] // 128
    nc, _es = _get_program(nt)
    in_maps = []
    for c in range(x.shape[0]):
        in_maps.append(host_inputs(
            x[c], np.asarray(L0_attn_Wq), np.asarray(L0_attn_bq), np.asarray(L0_attn_Wk),
            np.asarray(L0_attn_bk), np.asarray(L0_attn_Wv), np.asarray(L0_attn_bv),
            np.asarray(L0_ffn_w1), np.asarray(L0_ffn_conv_w), np.asarray(L0_ffn_conv_b),
            np.asarray(L0_ffn_w2), nt=nt))
    res = run_bass_kernel_spmd(nc, in_maps, core_ids=list(range(x.shape[0])))
    return np.stack([np.asarray(r["out"]) for r in res.results], axis=0)
